# version 03
# baseline (speedup 1.0000x reference)
import numpy as np
import ml_dtypes
from contextlib import ExitStack

import concourse.bass as bass
import concourse.mybir as mybir
from concourse.bass_utils import run_bass_kernel_spmd
from concourse.bass_core_rust import IndirectOffsetOnAxis

F32 = mybir.dt.float32
BF16 = mybir.dt.bfloat16
I32 = mybir.dt.int32
AF = mybir.ActivationFunctionType
ALU = mybir.AluOpType
AX = mybir.AxisListType
BF = ml_dtypes.bfloat16

D, S, HTOK, FF = 1024, 4096, 2048, 4096
E, CAP, CAPP = 8, 4916, 4992
NC = 8
EPS = 1e-5
NST = CAPP // 128
RB_ST = [list(range(rb * 4, min(rb * 4 + 4, NST))) for rb in range(10)]


class Plan:
    ENG = ("sync", "scalar", "vector", "tensor", "gpsimd")

    def __init__(self):
        self.ops = {e: [] for e in self.ENG}
        self.cnt = {}

    def w(self, eng, sem, val):
        if val > 0:
            self.ops[eng].append(("w", sem, val))

    def o(self, eng, fn, inc=None, k=1):
        incs = [(inc, k)] if inc is not None else []
        self.ops[eng].append(["o", fn, incs])
        if inc is not None:
            self.cnt[inc] = self.cnt.get(inc, 0) + k
        return self.cnt.get(inc, 0)

    def m(self, eng, inc, k=1):
        for entry in reversed(self.ops[eng]):
            if entry[0] == "o":
                entry[2].append((inc, k))
                break
        else:
            raise RuntimeError("no op to mark")
        self.cnt[inc] = self.cnt.get(inc, 0) + k
        return self.cnt[inc]

    def v(self, sem):
        return self.cnt.get(sem, 0)


def _host_constants():
    inv_freq = 1.0 / 10000.0 ** (2.0 * np.arange(0, 32, dtype=np.float64) / 64.0)
    ang = np.arange(S, dtype=np.float64)[None, :] * inv_freq[:, None]
    c32, s32 = np.cos(ang), np.sin(ang)
    costab = np.empty((128, S), np.float32)
    sintab = np.empty((128, S), np.float32)
    for d in range(128):
        j = d % 64
        costab[d] = c32[j % 32]
        sintab[d] = (-1.0 if j < 32 else 1.0) * s32[j % 32]
    cm = np.zeros((128, 4, 512), np.float32)
    ki = np.arange(128)[:, None]
    qi = np.arange(512)[None, :]
    for ob in range(4):
        cm[:, ob, :] = (ki + 128 * ob <= qi)
    p32 = np.zeros((128, 128), np.float32)
    for m in range(128):
        p32[(m + 32) if m % 64 < 32 else (m - 32), m] = 1.0
    return {
        "costab": costab.astype(BF), "sintab": sintab.astype(BF),
        "cmask": cm.reshape(128, 2048).astype(BF),
        "p32": p32.astype(BF),
        "ltri": (np.arange(128)[:, None] < np.arange(128)[None, :]).astype(BF),
        "ident": np.eye(128, dtype=BF),
        "onesbf1": np.ones((1, 128), BF),
        "onesf128": np.ones((128, 128), np.float32),
        "identf": np.eye(128, dtype=np.float32),
        "ones128bf": np.ones((128, 1), BF),
        "onesb128": np.ones((128, 128), BF),
        "fcon": np.broadcast_to(np.array([1e-5, 0.5, -0.5, -1.0, 0.125, 1.0, 0.0, 0.0], np.float32), (128, 8)).copy(),
    }


def _perm_cols():
    idx = np.empty(512, np.int64)
    for h in range(8):
        b = h * 64
        idx[b:b + 32] = b + 2 * np.arange(32)
        idx[b + 32:b + 64] = b + 2 * np.arange(32) + 1
    return idx


def _host_inputs_for_core(c, inp, consts, perm):
    b, hg = c // 2, c % 2
    x = np.asarray(inp["L0_attn_x"])
    xb = np.ascontiguousarray(x[b])
    sl = slice(hg * 512, (hg + 1) * 512)
    d = {
        "xT": np.ascontiguousarray(xb.T),
        "xrows": np.ascontiguousarray(xb[hg * HTOK:(hg + 1) * HTOK]),
        "wq": np.ascontiguousarray(np.asarray(inp["L0_attn_Wq"])[:, sl][:, perm]),
        "wk": np.ascontiguousarray(np.asarray(inp["L0_attn_Wk"])[:, sl][:, perm]),
        "wv": np.ascontiguousarray(np.asarray(inp["L0_attn_Wv"])[:, sl]),
        "wo": np.ascontiguousarray(np.asarray(inp["L0_attn_Wo"])),
        "bq": np.ascontiguousarray(np.asarray(inp["L0_attn_bq"])[sl][perm].reshape(4, 128).T),
        "bk": np.ascontiguousarray(np.asarray(inp["L0_attn_bk"])[sl][perm].reshape(4, 128).T),
        "bvbc": np.ascontiguousarray(np.broadcast_to(np.asarray(inp["L0_attn_bv"])[sl], (128, 512))),
        "bobc": np.ascontiguousarray(np.broadcast_to(np.asarray(inp["L0_attn_bo"]), (128, 1024))),
        "wg": np.ascontiguousarray(np.asarray(inp["L0_ffn_Wg"])),
        "bgbc": np.ascontiguousarray(np.broadcast_to(np.asarray(inp["L0_ffn_bg"]), (128, 8))),
        "w1": np.ascontiguousarray(np.asarray(inp["L0_ffn_W1"])[c]),
        "w2": np.ascontiguousarray(np.asarray(inp["L0_ffn_W2"])[c]),
        "b1": np.ascontiguousarray(np.asarray(inp["L0_ffn_b1"])[c].reshape(32, 128).T),
        "b2bc": np.ascontiguousarray(np.broadcast_to(np.asarray(inp["L0_ffn_b2"])[c], (128, 1024))),
        "tok0": np.full((128, 1), float(c * HTOK), np.float32),
        "hcol": np.full((128, 1), float(hg * HTOK), np.float32),
        "myslot0": np.full((128, 1), float(c * CAPP), np.float32),
    }
    sel = np.zeros((16, 2), np.float32)
    for r in range(8):
        for kp in range(2):
            if kp == 0 and r < c:
                sel[2 * r + kp, 0] = 1.0
            if (kp == 0 and r != c) or (kp == 1 and r < c):
                sel[2 * r + kp, 1] = 1.0
    d["selmat"] = sel
    d.update(consts)
    return d


def build(debug=(), upto="all", race=True):
    nc = bass.Bass("TRN2", target_bir_lowering=False, detect_race_conditions=race)
    ctx = ExitStack()

    def din(name, shape, dt=F32):
        return nc.dram_tensor(name, shape, dt, kind="ExternalInput").ap()

    xT = din("xT", [D, S]); xrows = din("xrows", [HTOK, D])
    wq_d = din("wq", [D, 512]); wk_d = din("wk", [D, 512])
    wv_d = din("wv", [D, 512]); wo_d = din("wo", [D, D])
    bq_d = din("bq", [128, 4]); bk_d = din("bk", [128, 4])
    bvbc_d = din("bvbc", [128, 512]); bobc_d = din("bobc", [128, 1024])
    wg_d = din("wg", [D, 8]); bgbc_d = din("bgbc", [128, 8])
    w1_d = din("w1", [D, FF]); w2_d = din("w2", [FF, D])
    b1_d = din("b1", [128, 32]); b2bc_d = din("b2bc", [128, 1024])
    selmat_d = din("selmat", [16, 2]); tok0_d = din("tok0", [128, 1])
    hcol_d = din("hcol", [128, 1])
    myslot0_d = din("myslot0", [128, 1])
    costab_d = din("costab", [128, S], BF16); sintab_d = din("sintab", [128, S], BF16)
    cmask_d = din("cmask", [128, 2048], BF16)
    p32_d = din("p32", [128, 128], BF16); ltri_d = din("ltri", [128, 128], BF16)
    ident_d = din("ident", [128, 128], BF16)
    onesbf1_d = din("onesbf1", [1, 128], BF16)
    onesf128_d = din("onesf128", [128, 128])
    identf_d = din("identf", [128, 128])
    ones128bf_d = din("ones128bf", [128, 1], BF16)
    onesb128_d = din("onesb128", [128, 128], BF16)
    fcon_d = din("fcon", [128, 8])

    y_out = nc.dram_tensor("y_out", [HTOK, D], F32, kind="ExternalOutput").ap()
    dbgout = {}

    def dout(name, shape, dt=F32):
        t = nc.dram_tensor("dbg_" + name, shape, dt, kind="ExternalOutput").ap()
        dbgout[name] = t
        return t

    xt_bf = nc.dram_tensor("xt_bf", [D, S], BF16)
    ag_in = nc.dram_tensor("ag_in", [512, S], BF16)
    ag_out = nc.dram_tensor("ag_out", [1024, S], BF16)
    x1_tm = nc.dram_tensor("x1_tm", [HTOK, D], F32)
    ln2_loc = nc.dram_tensor("ln2_loc", [HTOK, D], BF16)
    ln2_all = nc.dram_tensor("ln2_all", [NC * HTOK, D], BF16)
    cnt_loc = nc.dram_tensor("cnt_loc", [2, 8], F32)
    cnt_all = nc.dram_tensor("cnt_all", [16, 8], F32)
    bar_o = nc.dram_tensor("bar_o", [16, 8], F32)
    slot_loc = nc.dram_tensor("slot_loc", [E * CAPP, 1], F32)
    slot_ag = nc.dram_tensor("slot_ag", [NC * E * CAPP, 1], F32)
    ye_loc = nc.dram_tensor("ye_loc", [5120, D], BF16)
    ye_all = nc.dram_tensor("ye_all", [NC * 5120, D], BF16)

    sb = lambda n, s, d: ctx.enter_context(nc.sbuf_tensor(n, s, d))
    cmask = sb("cmaskt", [128, 4, 512], BF16)
    p32 = sb("p32t", [128, 128], BF16)
    ltri = sb("ltrit", [128, 128], BF16)
    ident = sb("identt", [128, 128], BF16)
    onesbf1 = sb("onesbf1t", [1, 128], BF16)
    onesf128 = sb("onesf128t", [128, 128], F32)
    onesb128 = sb("onesb128t", [128, 128], BF16)
    identf = sb("identft", [128, 128], F32)
    ones128bf = sb("ones128bft", [128, 1], BF16)
    fcon = sb("fcont", [128, 8], F32)
    for _i, _v in enumerate([1e-5, 0.5, -0.5, -1.0, 0.125, 1.0]):
        try:
            nc.const_aps.register(F32, _v, fcon[:, _i:_i + 1])
        except Exception:
            pass
    wgs = sb("wgs", [128, 8, 8], BF16)
    bqc = sb("bqc", [128, 4], F32)
    bkc = sb("bkc", [128, 4], F32)
    bgbc = sb("bgbct", [128, 8], F32)
    bgeps = sb("bgepst", [128, 8], F32)
    l1a = sb("l1at", [128, 16], F32)
    l1b = sb("l1bt", [128, 16], F32)
    l2a = sb("l2at", [128, 16], F32)
    l2b = sb("l2bt", [128, 16], F32)
    dal = sb("dalt", [128, 16], F32)
    offsl1 = sb("offsl1t", [128, 16], F32)
    offsl2 = sb("offsl2t", [128, 16], F32)
    cntrow = sb("cntrowt", [1, 16], F32)
    b1c = sb("b1ct", [128, 32], F32)
    b2bc = sb("b2bct", [128, 1024], F32)
    selmat = sb("selmatt", [16, 2], F32)
    tok0 = sb("tok0t", [128, 1], F32)
    hcol = sb("hcolt", [128, 1], F32)
    msl0 = sb("msl0t", [128, 1], F32)
    qn = sb("qnt", [1, 8], F32)
    negm = sb("negmt", [128, 1], F32)
    rtmp1 = sb("rtmp1", [128, 512], BF16)
    rtmp2 = sb("rtmp2", [128, 512], BF16)
    wqoc = sb("wqoc", [128, 2, 8, 128], BF16)
    wkoc = sb("wkoc", [128, 2, 8, 128], BF16)
    iot8 = sb("iot8", [128, 8], F32)
    e1s = sb("e1st", [128, 16], F32); e2s = sb("e2st", [128, 16], F32)
    poss1 = sb("poss1", [128, 16], F32); poss2 = sb("poss2", [128, 16], F32)
    g1s = sb("g1st", [128, 16], F32); g2s = sb("g2st", [128, 16], F32)
    keep1 = sb("keep1t", [128, 16], F32); keep2 = sb("keep2t", [128, 16], F32)
    idx1s = sb("idx1st", [128, 16], I32); idx2s = sb("idx2st", [128, 16], I32)
    idx1g = sb("idx1gt", [128, 16], I32); idx2g = sb("idx2gt", [128, 16], I32)
    y1i = sb("y1it", [128, 16], I32); y2i = sb("y2it", [128, 16], I32)
    m1i = sb("m1it", [128, 16], I32); m2i = sb("m2it", [128, 16], I32)
    t1i = sb("t1it", [128, 16], I32); t2i = sb("t2it", [128, 16], I32)
    idx1y = sb("idx1yt", [128, 16], I32); idx2y = sb("idx2yt", [128, 16], I32)
    slotY = sb("slotYt", [128, 39], I32)
    i39a = sb("i39at", [128, 39], I32); i39b = sb("i39bt", [128, 39], I32)
    i39c = sb("i39ct", [128, 39], I32)
    toks = sb("tokst", [128, 16], F32)
    offrow = sb("offrowt", [1, 16], F32)
    cntag = sb("cntagt", [16, 8], F32)
    offb = sb("offbt", [128, 2, 8], F32)
    zero1 = sb("zero1t", [128, 312], F32)
    st39 = sb("st39t", [39, 2, 128], F32)
    sum39 = sb("sum39t", [39, 128], F32)
    slotm = sb("slotmt", [128, 39], F32)
    slotmi = sb("slotmit", [128, 39], I32)
    agix = sb("agixt", [128, 1], I32)
    agixf = sb("agixft", [128, 1], F32)
    agix8 = sb("agix8t", [128, 8], I32)
    agix8f = sb("agix8ft", [128, 8], F32)
    stt = sb("sttt", [128, 8], F32)

    auto_used = nc.SBUF_PARTITION_SIZE_BYTES - nc.sbuf_bytes_remaining
    B0 = 48640
    assert auto_used <= B0, f"auto SBUF region overflow: {auto_used} > {B0}"
    TOP = nc.SBUF_PARTITION_SIZE_BYTES
    at = lambda n, s, d, off: nc.alloc_sbuf_tensor_at(n, s, d, offset=B0 + off)

    qrot = at("qrot", [128, 4, S], BF16, 0)
    krot = at("krot", [128, 4, S], BF16, 32768)
    vext = at("vext", [128, 32, 520], BF16, 65536)
    Z0 = 98816
    xf0 = at("xf0", [128, S], F32, 0)
    xf1 = at("xf1", [128, S], F32, 16384)
    xbb = at("xbb", [128, S], BF16, 32768)
    sqb = at("sqb", [128, S], BF16, 40960)
    R1 = at("R1", [1, S], F32, 49152)
    R2 = at("R2", [1, S], F32, 65536)
    xtil = at("xtil", [128, S], BF16, 81920)
    R1b = at("R1b", [1, S], BF16, Z0 + 0)
    R2b = at("R2b", [1, S], BF16, Z0 + 8192)
    rbcb = at("rbcb", [128, S], BF16, Z0 + 16384)
    murbcb = at("murbcb", [128, S], BF16, Z0 + 24576)
    tmpr = at("tmprt", [1, 3, 512], F32, Z0 + 32768)
    xstr = at("xstr", [128, 2, 8, 512], BF16, Z0 + 0)
    aq = at("aqt", [128, S], BF16, Z0 + 16384)
    ak = at("akt", [128, S], BF16, Z0 + 24576)
    costab = at("costabt", [128, S], BF16, Z0 + 32768)
    sintab = at("sintabt", [128, S], BF16, Z0 + 40960)
    wvs = at("wvst", [128, 8, 512], BF16, Z0 + 16384)
    bvbc = at("bvbct", [128, 512], F32, Z0 + 24576)
    psb = [at("psb0", [128, 1024], BF16, Z0 + 32768),
           at("psb1", [128, 1024], BF16, Z0 + 34816)]
    rlA = at("rlA", [128, 512], F32, Z0 + 36864)
    rlB = at("rlB", [128, 512], F32, Z0 + 38912)
    lrow = at("lrowt", [65, 1024], F32, Z0 + 40960)
    lrow2 = at("lrow2t", [65, 1024], F32, Z0 + 45056)
    stg = at("stgt", [64, 2, 2, 512], BF16, Z0 + 0)
    agT = at("agTt", [128, 8, HTOK], BF16, 0)
    ln2T = at("ln2Tt", [128, 8, HTOK], BF16, 32768)
    xrw = at("xrwt", [128, 2, 1024], F32, 65536)
    x1t = at("x1tt", [128, 2, 1024], F32, 73728)
    ln2t = at("ln2tt", [128, 2, 1024], BF16, 81920)
    sqf = at("sqft", [128, 1024], F32, 86016)
    rt0 = at("rt0t", [128, 16, 8], F32, 65536)
    eq1a = at("eq1at", [128, 16, 8], F32, 66048)
    msk2 = at("msk2t", [128, 16, 8], F32, 66560)
    eq2a = at("eq2at", [128, 16, 8], F32, 67072)
    ohb = at("ohbt", [128, 2, 16, 8], BF16, 67584)
    posall = at("posallt", [128, 2, 16, 8], F32, 68096)
    wos = at("wost", [128, 8, 1024], BF16, 90112)
    bobc = at("bobct", [128, 1024], F32, 106496)
    w1s = at("w1st", [128, 8, FF], BF16, 0)
    w2s = at("w2st", [128, 32, 1024], BF16, 65536)
    hsb = at("hsbt", [128, 32, 512], BF16, 131072)
    bufT = at("bufTt", [128, 8, 512], BF16, 163840)
    stage = at("staget", [128, 2, 1024], BF16, 172032)
    yet = at("yett", [128, 2, 1024], BF16, 176128)
    gb1 = at("gb1t", [128, 2, 1024], BF16, 0)
    gb2 = at("gb2t", [128, 2, 1024], BF16, 4096)
    x1r = at("x1rt", [128, 2, 1024], F32, 8192)
    outt = at("outtt", [128, 2, 1024], F32, 16384)
    zsum = at("zsumt", [128, 1024], F32, 24576)
    assert B0 + 180224 <= TOP - 8, (B0 + 180224, TOP)

    ps_ = lambda n, s, d: ctx.enter_context(nc.psum_tensor(n, s, d))
    pbA = ps_("pbA", [128, 1024], F32)
    pbB = ps_("pbB", [128, 1024], F32)
    pbC = ps_("pbC", [128, 512], F32)
    pbD = ps_("pbD", [128, 512], F32)
    pbE = ps_("pbE", [128, 512], F32)
    pbT = ps_("pbT", [128, 256], BF16)

    semn = ["ld", "gw", "pe", "dv", "ac", "sp", "g2", "sc", "gp",
            "smm", "exp", "msk", "omm", "nrm", "lcp", "rld", "rbc", "cc"]
    sems = {n: ctx.enter_context(nc.semaphore("s_" + n)) for n in semn}

    P = Plan()

    def xbounce():
        P.o("vector", lambda v: v.engine_nop(), "dv")
        P.w("scalar", "dv", P.v("dv"))
        P.o("scalar", lambda s: s.copy(cntrow[0:1, 0:1], cntrow[0:1, 0:1]), "ac")
        P.w("gpsimd", "ac", P.v("ac"))
        P.w("vector", "ac", P.v("ac"))

    def _finish():
        for _sn in semn:
            P.w("sync", _sn, P.v(_sn))
        block = ctx.enter_context(nc.Block())

        def emit(name):
            def body(eng):
                for entry in P.ops[name]:
                    if entry[0] == "w":
                        eng.wait_ge(sems[entry[1]], entry[2])
                    else:
                        bi = entry[1](eng)
                        for (inc, k) in entry[2]:
                            bi = bi.then_inc(sems[inc], k)
            return body

        block.sync(emit("sync"))
        block.scalar(emit("scalar"))
        block.vector(emit("vector"))
        block.tensor(emit("tensor"))
        block.gpsimd(emit("gpsimd"))
        ctx.close()
        return nc, dbgout


    const_loads = [(cmask, cmask_d.rearrange("p (a b) -> p a b", b=512)),
                   (p32, p32_d), (ltri, ltri_d), (ident, ident_d),
                   (onesbf1, onesbf1_d), (onesf128, onesf128_d), (identf, identf_d),
                   (onesb128, onesb128_d),
                   (ones128bf, ones128bf_d), (fcon, fcon_d), (bqc, bq_d), (bkc, bk_d),
                   (bgbc, bgbc_d),
                   (b1c, b1_d), (b2bc, b2bc_d), (selmat, selmat_d),
                   (tok0, tok0_d), (hcol, hcol_d), (msl0, myslot0_d)]
    for dst, src in const_loads:
        P.o("sync", (lambda d_, s_: (lambda sy: sy.dma_start(out=d_[:], in_=s_[:])))(dst, src), "ld", 16)
    LD_CONST = P.v("ld")
    P.o("gpsimd", lambda g: g.dma_start(out=wgs[:], in_=wg_d.rearrange("(c p) m -> p c m", p=128)), "gw", 16)
    GW_WG = P.v("gw")
    REGS = {}
    def _mkregs(g):
        REGS["slot"] = g.to_reg(E * CAPP - 1)
        REGS["ln2"] = g.to_reg(NC * HTOK - 1)
        REGS["ag"] = g.to_reg(1024 * S - 1)
        REGS["sag"] = g.to_reg(NC * E * CAPP - 1)
        REGS["ye"] = g.to_reg(NC * 5120 - 1)
        return g.nop()
    P.o("gpsimd", _mkregs)
    P.o("gpsimd", lambda g: g.iota(iot8[:], pattern=[[1, 8]], base=0, channel_multiplier=0,
                                   allow_small_or_imprecise_dtypes=True), "gp")
    P.o("gpsimd", lambda g: g.iota(toks[:], pattern=[[128, 16]], base=0, channel_multiplier=1,
                                   allow_small_or_imprecise_dtypes=True), "gp")
    G2_IOTA = P.v("gp")
    P.o("vector", lambda v: v.memset(zero1[:], 0.0), "dv")
    P.o("vector", lambda v: v.memset(qn[:], 0.0), "dv")
    P.o("vector", lambda v: v.memset(R1[:], 0.0), "dv")
    P.o("vector", lambda v: v.memset(R2[:], 0.0), "dv")
    P.o("vector", lambda v: v.memset(sum39[:], 0.0), "dv")
    P.o("vector", lambda v: v.memset(cntrow[:], 0.0), "dv")
    P.w("vector", "gp", G2_IOTA)
    P.w("vector", "ld", LD_CONST)
    P.o("vector", lambda v: v.tensor_scalar_add(toks[:], toks[:], tok0[:, 0:1]), "dv")
    P.o("vector", lambda v: v.tensor_scalar_mul(l1a[:, 0:8], iot8[:], 1e-5), "dv")
    P.w("vector", "dv", P.v("dv"))
    P.o("vector", lambda v: v.tensor_sub(bgeps[:], bgbc[:], l1a[:, 0:8]), "dv")
    DV_INIT = P.v("dv")

    xTr = xT.rearrange("(c p) t -> p c t", p=128)
    xf = [xf0, xf1]

    dv_cast = {}
    for c8 in range(8):
        if c8 >= 2:
            P.w("sync", "dv", dv_cast[c8 - 2])
        P.o("sync", (lambda cc: (lambda sy: sy.dma_start(out=xf[cc % 2][:], in_=xTr[:, cc, :])))(c8), "ld", 16)
        ldv = P.v("ld")
        P.w("vector", "ld", ldv)
        if c8 >= 1:
            P.w("vector", "pe", P.v("pe"))
        P.o("vector", (lambda cc: (lambda v: v.tensor_copy(xbb[:], xf[cc % 2][:])))(c8), "dv")
        dv_cast[c8] = P.v("dv")
        P.w("scalar", "dv", dv_cast[c8])
        P.w("scalar", "pe", P.v("pe"))
        P.o("scalar", lambda s: s.activation(sqb[:], xbb[:], AF.Square), "ac")
        acs = P.v("ac")
        for tb in range(8):
            P.w("tensor", "dv", dv_cast[c8])
            P.w("tensor", "ac", acs)
            P.w("tensor", "dv", P.v("dv"))
            P.o("tensor", (lambda t0: (lambda t: t.matmul(pbC[0:1, :], ones128bf[:],
                 xbb[:, t0 * 512:(t0 + 1) * 512], start=True, stop=True)))(tb))
            P.o("tensor", (lambda t0: (lambda t: t.matmul(pbD[0:1, :], ones128bf[:],
                 sqb[:, t0 * 512:(t0 + 1) * 512], start=True, stop=True)))(tb), "pe")
            P.w("vector", "pe", P.v("pe"))
            P.o("vector", (lambda t0: (lambda v: v.tensor_add(
                 R1[0:1, t0 * 512:(t0 + 1) * 512], R1[0:1, t0 * 512:(t0 + 1) * 512], pbC[0:1, :])))(tb))
            P.o("vector", (lambda t0: (lambda v: v.tensor_add(
                 R2[0:1, t0 * 512:(t0 + 1) * 512], R2[0:1, t0 * 512:(t0 + 1) * 512], pbD[0:1, :])))(tb), "dv")
    for tb in range(8):
        s0, s1 = tb * 512, (tb + 1) * 512
        P.w("vector", "dv", P.v("dv"))
        P.o("vector", (lambda a, b: (lambda v: v.tensor_scalar_mul(tmpr[0:1, 0, :], R1[0:1, a:b], 1.0 / D)))(s0, s1))
        P.o("vector", (lambda a, b: (lambda v: v.tensor_scalar_mul(tmpr[0:1, 1, :], R2[0:1, a:b], 1.0 / D)))(s0, s1), "dv")
        P.w("scalar", "dv", P.v("dv"))
        P.o("scalar", lambda s: s.activation(tmpr[0:1, 2, :], tmpr[0:1, 0, :], AF.Square), "ac")
        P.w("vector", "ac", P.v("ac"))
        P.o("vector", lambda v: v.tensor_sub(tmpr[0:1, 1, :], tmpr[0:1, 1, :], tmpr[0:1, 2, :]), "dv")
        P.w("scalar", "dv", P.v("dv"))
        P.o("scalar", lambda s: s.activation(tmpr[0:1, 1, :], tmpr[0:1, 1, :], AF.Ln, bias=EPS))
        P.o("scalar", (lambda a, b: (lambda s: s.activation(R1[0:1, a:b], tmpr[0:1, 1, :], AF.Exp, scale=-0.5)))(s0, s1), "ac")
        P.w("vector", "ac", P.v("ac"))
        P.o("vector", (lambda a, b: (lambda v: v.tensor_mul(R2[0:1, a:b], tmpr[0:1, 0, :], R1[0:1, a:b])))(s0, s1))
        P.o("vector", (lambda a, b: (lambda v: v.tensor_copy(R1b[0:1, a:b], R1[0:1, a:b])))(s0, s1))
        P.o("vector", (lambda a, b: (lambda v: v.tensor_copy(R2b[0:1, a:b], R2[0:1, a:b])))(s0, s1), "dv")
    DV_STATS = P.v("dv")
    for tb in range(8):
        s0, s1 = tb * 512, (tb + 1) * 512
        P.w("tensor", "dv", P.v("dv"))
        P.o("tensor", (lambda a, b: (lambda t: t.matmul(pbE[:], onesbf1[:], R1b[0:1, a:b],
             start=True, stop=True)))(s0, s1), "pe")
        P.w("vector", "pe", P.v("pe"))
        P.o("vector", (lambda a, b: (lambda v: v.tensor_copy(rbcb[:, a:b], pbE[:])))(s0, s1), "dv")
        P.w("tensor", "dv", P.v("dv"))
        P.o("tensor", (lambda a, b: (lambda t: t.matmul(pbE[:], onesbf1[:], R2b[0:1, a:b],
             start=True, stop=True)))(s0, s1), "pe")
        P.w("vector", "pe", P.v("pe"))
        P.o("vector", (lambda a, b: (lambda v: v.tensor_copy(murbcb[:, a:b], pbE[:])))(s0, s1), "dv")
    DV_BC = P.v("dv")

    sp_xtil = {}
    for c8 in range(8):
        P.w("sync", "dv", DV_BC)
        if c8 >= 2:
            P.w("sync", "dv", P.v("dv"))
        P.o("sync", (lambda cc: (lambda sy: sy.dma_start(out=xf[cc % 2][:], in_=xTr[:, cc, :])))(c8), "ld", 16)
        ldv = P.v("ld")
        P.w("vector", "ld", ldv)
        if c8 >= 1:
            P.w("vector", "sp", sp_xtil[c8 - 1])
        P.o("vector", (lambda cc: (lambda v: v.tensor_copy(xbb[:], xf[cc % 2][:])))(c8))
        P.o("vector", lambda v: v.tensor_mul(xtil[:], xbb[:], rbcb[:]))
        P.o("vector", lambda v: v.tensor_sub(xtil[:], xtil[:], murbcb[:]), "dv")
        P.w("sync", "dv", P.v("dv"))
        P.o("sync", (lambda cc: (lambda sy: sy.dma_start(
            out=xt_bf.ap()[cc * 128:(cc + 1) * 128, :], in_=xtil[:])))(c8), "sp", 16)
        sp_xtil[c8] = P.v("sp")
    SP_XT = P.v("sp")
    if "xt" in debug:
        dout("xt", [D, S], BF16)
        P.w("sync", "sp", SP_XT)
        P.o("sync", lambda sy: sy.dma_start(out=dbgout["xt"][:], in_=xt_bf.ap()[:]), "sp", 16)
    if upto == "p0":
        return _finish()

    P.w("sync", "sp", SP_XT)
    P.o("sync", lambda sy: sy.dma_start(out=costab[:], in_=costab_d[:]), "ld", 16)
    P.o("sync", lambda sy: sy.dma_start(out=sintab[:], in_=sintab_d[:]), "ld", 16)
    LD_TAB = P.v("ld")

    xtr = xt_bf.ap().rearrange("(c p) t -> p c t", p=128)
    pe_xstr = {}
    gw_oc = {}
    for oc in range(4):
        P.w("gpsimd", "pe", pe_xstr.get((oc - 1, 7), 0))
        P.o("gpsimd", (lambda o_: (lambda g: g.dma_start(out=wqoc[:, o_ % 2, :, :],
             in_=wq_d.rearrange("(c p) m -> p c m", p=128)[:, :, o_ * 128:(o_ + 1) * 128])))(oc), "gw", 16)
        P.o("gpsimd", (lambda o_: (lambda g: g.dma_start(out=wkoc[:, o_ % 2, :, :],
             in_=wk_d.rearrange("(c p) m -> p c m", p=128)[:, :, o_ * 128:(o_ + 1) * 128])))(oc), "gw", 16)
        gw_oc[oc] = P.v("gw")
        for tb in range(8):
            it = oc * 8 + tb
            P.w("sync", "sp", SP_XT)
            if it >= 2:
                P.w("sync", "pe", pe_xstr[(oc, tb - 2)] if tb >= 2 else pe_xstr[(oc - 1, 6 + tb)])
            P.o("sync", (lambda t0, pr: (lambda sy: sy.dma_start(
                out=xstr[:, pr, :, :], in_=xtr[:, :, t0 * 512:(t0 + 1) * 512])))(tb, it % 2), "ld", 16)
            ldv = P.v("ld")
            P.w("tensor", "ld", ldv)
            P.w("tensor", "gw", gw_oc[oc])
            P.w("tensor", "dv", P.v("dv"))
            for c8 in range(8):
                P.o("tensor", (lambda cc, o_, pr: (lambda t: t.matmul(pbC[:],
                     wqoc[:, o_ % 2, cc, :], xstr[:, pr, cc, :],
                     start=(cc == 0), stop=(cc == 7))))(c8, oc, it % 2))
            for c8 in range(8):
                P.o("tensor", (lambda cc, o_, pr: (lambda t: t.matmul(pbD[:],
                     wkoc[:, o_ % 2, cc, :], xstr[:, pr, cc, :],
                     start=(cc == 0), stop=(cc == 7))))(c8, oc, it % 2))
            pe_xstr[(oc, tb)] = P.m("tensor", "pe")
            s0, s1 = tb * 512, (tb + 1) * 512
            P.w("vector", "pe", P.v("pe"))
            P.o("vector", (lambda a, b, o_: (lambda v: v.tensor_scalar_add(aq[:, a:b], pbC[:], bqc[:, o_:o_ + 1])))(s0, s1, oc))
            P.o("vector", (lambda a, b, o_: (lambda v: v.tensor_scalar_add(ak[:, a:b], pbD[:], bkc[:, o_:o_ + 1])))(s0, s1, oc), "dv")
            dvqk = P.v("dv")
            if upto in ("p1q",) and oc == 0 and tb == 7:
                return _finish()
            if upto == "p1q":
                continue
            for (src, dst, half) in ((aq, qrot, slice(0, 512)), (ak, krot, slice(512, 1024))):
                P.w("tensor", "dv", dvqk)
                P.w("tensor", "dv", P.v("dv"))
                P.o("tensor", (lambda s_, a, b, h: (lambda t: t.matmul(pbA[:, h],
                     p32[:], s_[:, a:b], start=True, stop=True)))(src, s0, s1, half), "pe")
                P.w("vector", "pe", P.v("pe"))
                P.w("vector", "ld", LD_TAB)
                P.o("vector", (lambda a, b, h: (lambda v: v.tensor_mul(rtmp1[:], pbA[:, h], sintab[:, a:b])))(s0, s1, half))
                P.o("vector", (lambda s_, a, b: (lambda v: v.tensor_mul(rtmp2[:], s_[:, a:b], costab[:, a:b])))(src, s0, s1))
                P.o("vector", (lambda d_, o_, a, b: (lambda v: v.tensor_add(d_[:, o_, a:b], rtmp1[:], rtmp2[:])))(dst, oc, s0, s1), "dv")
        if upto == "p1r" and oc == 0:
            return _finish()
        for (dst, col) in ((qrot, 0), (krot, 1)):
            P.w("scalar", "dv", P.v("dv"))
            P.o("scalar", (lambda d_, o_: (lambda s: s.activation(aq[:], d_[:, o_, :], AF.Square)))(dst, oc), "ac")
            acs = P.v("ac")
            for tb in range(8):
                P.w("tensor", "ac", acs)
                P.w("tensor", "dv", P.v("dv"))
                P.o("tensor", (lambda t0: (lambda t: t.matmul(pbE[0:1, :], ones128bf[:],
                     aq[:, t0 * 512:(t0 + 1) * 512], start=True, stop=True)))(tb), "pe")
                P.w("vector", "pe", P.v("pe"))
                P.o("vector", lambda v: v.reduce_max(qn[0:1, 7:8], pbE[0:1, :], axis=AX.X))
                P.o("vector", (lambda cl: (lambda v: v.tensor_max(qn[0:1, cl:cl + 1],
                     qn[0:1, cl:cl + 1], qn[0:1, 7:8])))(col), "dv")
        if upto == "p1a" and oc == 0:
            return _finish()
    P.w("scalar", "dv", P.v("dv"))
    P.o("scalar", lambda s: s.activation(qn[0:1, 2:3], qn[0:1, 0:1], AF.Ln))
    P.o("scalar", lambda s: s.activation(qn[0:1, 3:4], qn[0:1, 1:2], AF.Ln), "ac")
    P.w("vector", "ac", P.v("ac"))
    P.o("vector", lambda v: v.tensor_add(qn[0:1, 4:5], qn[0:1, 2:3], qn[0:1, 3:4]), "dv")
    P.w("scalar", "dv", P.v("dv"))
    P.o("scalar", lambda s: s.activation(qn[0:1, 5:6], qn[0:1, 4:5], AF.Exp, scale=0.5), "ac")
    P.w("vector", "ac", P.v("ac"))
    P.o("vector", lambda v: v.tensor_scalar_mul(qn[0:1, 6:7], qn[0:1, 5:6], -0.125), "dv")
    P.w("tensor", "dv", P.v("dv"))
    P.o("tensor", lambda t: t.matmul(pbE[:, 0:1], onesf128[0:1, :], qn[0:1, 6:7], start=True, stop=True), "pe")
    P.w("vector", "pe", P.v("pe"))
    P.o("vector", lambda v: v.tensor_copy(negm[:], pbE[:, 0:1]), "dv")
    DV_NEGM = P.v("dv")
    if upto == "p1b":
        return _finish()
    P.w("gpsimd", "dv", DV_NEGM)
    P.o("gpsimd", lambda g: g.dma_start(out=wvs[:], in_=wv_d.rearrange("(c p) m -> p c m", p=128)), "gw", 16)
    GW_WV = P.v("gw")
    P.w("sync", "dv", DV_NEGM)
    P.o("sync", lambda sy: sy.dma_start(out=bvbc[:], in_=bvbc_d[:]), "ld", 16)
    for tb in range(8):
        it = 32 + tb
        prev = pe_xstr.get((3, tb - 2), 0) if tb < 2 else 0
        P.w("sync", "pe", pe_xstr[("v", tb - 2)] if tb >= 2 else pe_xstr[(3, 6 + tb)])
        P.o("sync", (lambda t0, pr: (lambda sy: sy.dma_start(
            out=xstr[:, pr, :, :], in_=xtr[:, :, t0 * 512:(t0 + 1) * 512])))(tb, it % 2), "ld", 16)
        ldv = P.v("ld")
        for ttl in range(4):
            tts = tb * 4 + ttl
            pv = pbC if tts % 2 == 0 else pbD
            P.w("tensor", "ld", ldv)
            P.w("tensor", "gw", GW_WV)
            P.w("tensor", "dv", P.v("dv"))
            for c8 in range(8):
                P.o("tensor", (lambda cc, tl, pr, pv_: (lambda t: t.matmul(pv_[:],
                     xstr[:, pr, cc, tl * 128:(tl + 1) * 128], wvs[:, cc, :],
                     start=(cc == 0), stop=(cc == 7))))(c8, ttl, it % 2, pv))
            mark = P.m("tensor", "pe")
            if ttl == 3:
                pe_xstr[("v", tb)] = mark
            P.w("vector", "pe", P.v("pe"))
            P.o("vector", (lambda tt_: (lambda v: v.memset(
                vext[:, tt_, :].rearrange("p (h x) -> p h x", x=65)[:, :, 64:65], 1.0)))(tts))
            P.o("vector", (lambda tt_, pv_: (lambda v: v.tensor_add(
                vext[:, tt_, :].rearrange("p (h x) -> p h x", x=65)[:, :, 0:64],
                pv_[:].rearrange("p (h x) -> p h x", x=64),
                bvbc[:].rearrange("p (h x) -> p h x", x=64))))(tts, pv), "dv")
    DV_P1 = P.v("dv")
    if "qkv" in debug:
        dout("qrot", [128, 4 * S], BF16); dout("krot", [128, 4 * S], BF16)
        dout("vext", [128, 32 * 520], BF16); dout("negm", [128, 1])
        P.w("sync", "dv", DV_P1)
        P.o("sync", lambda sy: sy.dma_start(out=dbgout["qrot"][:], in_=qrot[:].rearrange("p a b -> p (a b)")), "sp", 16)
        P.o("sync", lambda sy: sy.dma_start(out=dbgout["krot"][:], in_=krot[:].rearrange("p a b -> p (a b)")), "sp", 16)
        P.o("sync", lambda sy: sy.dma_start(out=dbgout["vext"][:], in_=vext[:].rearrange("p a b -> p (a b)")), "sp", 16)
        P.o("sync", lambda sy: sy.dma_start(out=dbgout["negm"][:], in_=negm[:]), "sp", 16)
    if upto == "p1":
        return _finish()

    SB2 = [pbA, pbB]
    exp_after, msk_after, omm_after = {}, {}, {}
    rbe_after, sc_stage = {}, {}
    gi, nq = 0, 0
    for ph in range(4):
        hA, hB = 2 * ph, 2 * ph + 1
        for qb in range(8):
            K = 4 * (qb + 1)
            qs = slice(qb * 512, (qb + 1) * 512)
            for kt in range(K):
                par = gi % 2
                ks = slice(kt * 128, (kt + 1) * 128)
                if gi == 0:
                    P.w("tensor", "dv", DV_P1)
                if kt == 0 and nq > 0:
                    P.w("tensor", "nrm", nq)
                if kt < 2 and nq > 0:
                    P.w("tensor", "dv", rbe_after[nq - 1])
                if gi >= 2:
                    P.w("tensor", "exp", exp_after[gi - 2])
                P.o("tensor", (lambda p_, k_, q_, pr: (lambda t: t.matmul(SB2[pr][:, 0:512],
                     krot[0:64, p_, k_], qrot[0:64, p_, q_], start=True, stop=True)))(ph, ks, qs, par))
                P.o("tensor", (lambda p_, k_, q_, pr: (lambda t: t.matmul(SB2[pr][:, 512:1024],
                     krot[64:128, p_, k_], qrot[64:128, p_, q_], start=True, stop=True)))(ph, ks, qs, par), "smm")
                smv = P.v("smm")
                P.w("scalar", "smm", smv)
                P.w("scalar", "dv", DV_NEGM)
                if gi >= 2:
                    P.w("scalar", "omm", omm_after[gi - 2])
                P.o("scalar", (lambda pr: (lambda s: s.activation(psb[pr][:], SB2[pr][:, 0:1024],
                     AF.Exp, bias=negm[:, 0:1], scale=0.125)))(par), "exp")
                exp_after[gi] = P.v("exp")
                diag = kt >= K - 4
                if diag:
                    ob = kt - 4 * qb
                    P.w("vector", "exp", exp_after[gi])
                    P.o("vector", (lambda pr, o_: (lambda v: v.tensor_mul(psb[pr][:, 0:512],
                         psb[pr][:, 0:512], cmask[:, o_, :])))(par, ob))
                    P.o("vector", (lambda pr, o_: (lambda v: v.tensor_mul(psb[pr][:, 512:1024],
                         psb[pr][:, 512:1024], cmask[:, o_, :])))(par, ob), "msk")
                    msk_after[gi] = P.v("msk")
                    P.w("tensor", "msk", msk_after[gi])
                else:
                    P.w("tensor", "exp", exp_after[gi])
                P.o("tensor", (lambda kt_, pr, st_, sp2, h_: (lambda t: t.matmul(pbC[0:65, :],
                     vext[:, kt_, h_ * 65:h_ * 65 + 65], psb[pr][:, 0:512],
                     start=st_, stop=sp2)))(kt, par, kt == 0, kt == K - 1, hA))
                P.o("tensor", (lambda kt_, pr, st_, sp2, h_: (lambda t: t.matmul(pbD[0:65, :],
                     vext[:, kt_, h_ * 65:h_ * 65 + 65], psb[pr][:, 512:1024],
                     start=st_, stop=sp2)))(kt, par, kt == 0, kt == K - 1, hB), "omm")
                omm_after[gi] = P.v("omm")
                gi += 1
            P.w("vector", "omm", omm_after[gi - 1])
            P.o("vector", lambda v: v.tensor_copy(lrow[64:65, 0:512], pbC[64:65, :]))
            P.o("vector", lambda v: v.tensor_copy(lrow[64:65, 512:1024], pbD[64:65, :]), "lcp")
            P.w("scalar", "lcp", P.v("lcp"))
            P.o("scalar", lambda s: s.activation(lrow2[64:65, :], lrow[64:65, :], AF.Ln))
            P.o("scalar", lambda s: s.activation(lrow[64:65, :], lrow2[64:65, :], AF.Exp, scale=-1.0), "rld")
            P.w("tensor", "rld", P.v("rld"))
            P.w("tensor", "exp", exp_after[gi - 1])
            P.o("tensor", lambda t: t.matmul(pbA[:, 0:512], onesf128[64:65, :],
                 lrow[64:65, 0:512], start=True, stop=True))
            P.o("tensor", lambda t: t.matmul(pbA[:, 512:1024], onesf128[64:65, :],
                 lrow[64:65, 512:1024], start=True, stop=True), "rbc")
            P.w("vector", "rbc", P.v("rbc"))
            P.o("vector", lambda v: v.tensor_copy(rlA[:], pbA[:, 0:512]))
            P.o("vector", lambda v: v.tensor_copy(rlB[:], pbA[:, 512:1024]), "dv")
            rbe_after[nq] = P.v("dv")
            spar = nq % 2
            if nq >= 2:
                P.w("vector", "sc", sc_stage[nq - 2])
            P.o("vector", (lambda sp2: (lambda v: v.tensor_mul(stg[0:64, sp2, 0, :],
                 pbC[0:64, :], rlA[0:64, :])))(spar))
            P.o("vector", (lambda sp2: (lambda v: v.tensor_mul(stg[0:64, sp2, 1, :],
                 pbD[0:64, :], rlB[0:64, :])))(spar), "nrm")
            P.w("scalar", "nrm", P.v("nrm"))
            P.o("scalar", (lambda sp2, h_, q_: (lambda s: s.dma_start(
                out=ag_in.ap()[h_ * 64:(h_ + 1) * 64, q_], in_=stg[0:64, sp2, 0, :])))(spar, hA, qs), "sc", 16)
            P.o("scalar", (lambda sp2, h_, q_: (lambda s: s.dma_start(
                out=ag_in.ap()[h_ * 64:(h_ + 1) * 64, q_], in_=stg[0:64, sp2, 1, :])))(spar, hB, qs), "sc", 16)
            sc_stage[nq] = P.v("sc")
            nq += 1
    SC_ATT = P.v("sc")
    if "attn" in debug:
        dout("attn", [512, S], BF16)
        P.w("sync", "sc", SC_ATT)
        P.o("sync", lambda sy: sy.dma_start(out=dbgout["attn"][:], in_=ag_in.ap()[:]), "sp", 16)
    if upto == "p2":
        return _finish()

    P.w("gpsimd", "omm", omm_after[gi - 1])
    P.w("gpsimd", "sc", SC_ATT)
    P.o("gpsimd", lambda g: g.dma_start(out=wos[:], in_=wo_d.rearrange("(c p) m -> p c m", p=128)), "gw", 16)
    GW_WO = P.v("gw")
    P.w("sync", "omm", omm_after[gi - 1])
    P.o("sync", lambda sy: sy.dma_start(out=bobc[:], in_=bobc_d[:]), "ld", 16)

    P.w("gpsimd", "sc", SC_ATT)
    P.o("gpsimd", lambda g: g.collective_compute("AllGather", ALU.bypass,
         replica_groups=[[0, 1], [2, 3], [4, 5], [6, 7]],
         ins=[ag_in.ap().opt()], outs=[ag_out.ap().opt()]), "cc")
    CC_ATT = P.v("cc")
    g2_agT = {}
    P.w("gpsimd", "ld", LD_CONST)
    for c8 in range(8):
        P.o("gpsimd", (lambda cc: (lambda g: g.iota(agix8f[:, cc:cc + 1],
             pattern=[[0, 1]], base=cc * 128 * S, channel_multiplier=S,
             allow_small_or_imprecise_dtypes=True)))(c8))
    P.o("gpsimd", lambda g: g.tensor_scalar_add(agix8f[:], agix8f[:], hcol[:, 0:1]))
    P.o("gpsimd", lambda g: g.tensor_copy(agix8[:], agix8f[:]), "gp")
    P.w("gpsimd", "cc", CC_ATT)
    for c8 in range(8):
        P.o("gpsimd", (lambda cc: (lambda g: g.indirect_dma_start(
             out=agT[:, cc, :], out_offset=None,
             in_=ag_out.ap(),
             in_offset=IndirectOffsetOnAxis(agix8[:, cc:cc + 1], 1),
             bounds_check=REGS["ag"], oob_is_err=False)))(c8), "g2", 16)
        g2_agT[c8] = P.v("g2")
    G2_AGT = P.v("g2")
    dv_ln2T = {}
    sc_ln2h = {}
    sp_x1h = {}
    for tt in range(16):
        tpar = tt % 2
        P.w("tensor", "g2", G2_AGT)
        P.w("tensor", "gw", GW_WO)
        P.w("tensor", "dv", P.v("dv"))
        for nb2 in range(2):
            for c8 in range(8):
                P.o("tensor", (lambda cc, n_, t_: (lambda t: t.matmul(
                     pbA[:, n_ * 512:(n_ + 1) * 512], agT[:, cc, t_ * 128:(t_ + 1) * 128],
                     wos[:, cc, n_ * 512:(n_ + 1) * 512], start=(cc == 0), stop=(cc == 7))))(c8, nb2, tt))
        P.m("tensor", "pe")
        pey = P.v("pe")
        if tt >= 2:
            P.w("sync", "dv", P.v("dv"))
        P.o("sync", (lambda t_, pr: (lambda sy: sy.dma_start(out=xrw[:, pr, :],
             in_=xrows[t_ * 128:(t_ + 1) * 128, :])))(tt, tpar), "ld", 16)
        P.w("vector", "pe", pey)
        P.w("vector", "ld", P.v("ld"))
        if tt >= 2:
            P.w("vector", "sp", sp_x1h[tt - 2])
        P.o("vector", (lambda pr: (lambda v: v.tensor_add(x1t[:, pr, :], pbA[:, 0:1024], bobc[:])))(tpar))
        P.o("vector", (lambda pr: (lambda v: v.tensor_add(x1t[:, pr, :], x1t[:, pr, :], xrw[:, pr, :])))(tpar), "dv")
        dvx1 = P.v("dv")
        P.w("sync", "dv", dvx1)
        P.o("sync", (lambda t_, pr: (lambda sy: sy.dma_start(
             out=x1_tm.ap()[t_ * 128:(t_ + 1) * 128, :], in_=x1t[:, pr, :])))(tt, tpar), "sp", 16)
        sp_x1h[tt] = P.v("sp")
        P.w("vector", "dv", dvx1)
        P.o("vector", (lambda pr: (lambda v: v.reduce_sum(stt[:, 0:1], x1t[:, pr, :], axis=AX.X)))(tpar), "dv")
        P.w("scalar", "dv", P.v("dv"))
        P.o("scalar", (lambda pr: (lambda s: s.activation(sqf[:], x1t[:, pr, :], AF.Square)))(tpar), "ac")
        P.w("vector", "ac", P.v("ac"))
        P.o("vector", lambda v: v.reduce_sum(stt[:, 1:2], sqf[:], axis=AX.X))
        P.o("vector", lambda v: v.tensor_scalar_mul(stt[:, 2:3], stt[:, 0:1], 1.0 / D))
        P.o("vector", lambda v: v.tensor_scalar_mul(stt[:, 3:4], stt[:, 1:2], 1.0 / D), "dv")
        P.w("scalar", "dv", P.v("dv"))
        P.o("scalar", lambda s: s.activation(stt[:, 4:5], stt[:, 2:3], AF.Square), "ac")
        P.w("vector", "ac", P.v("ac"))
        P.o("vector", lambda v: v.tensor_sub(stt[:, 5:6], stt[:, 3:4], stt[:, 4:5]), "dv")
        P.w("scalar", "dv", P.v("dv"))
        P.o("scalar", lambda s: s.activation(stt[:, 5:6], stt[:, 5:6], AF.Ln, bias=EPS))
        P.o("scalar", lambda s: s.activation(stt[:, 6:7], stt[:, 5:6], AF.Exp, scale=-0.5), "ac")
        P.w("vector", "ac", P.v("ac"))
        if tt >= 2:
            P.w("vector", "sc", sc_ln2h[tt - 2])
        P.o("vector", (lambda pr: (lambda v: v.tensor_scalar_sub(sqf[:], x1t[:, pr, :],
             stt[:, 2:3])))(tpar))
        P.o("vector", (lambda pr: (lambda v: v.tensor_scalar_mul(ln2t[:, pr, :], sqf[:],
             stt[:, 6:7])))(tpar), "dv")
        acl = P.v("dv")
        P.w("scalar", "dv", acl)
        P.o("scalar", (lambda t_, pr: (lambda s: s.dma_start(
             out=ln2_loc.ap()[t_ * 128:(t_ + 1) * 128, :], in_=ln2t[:, pr, :])))(tt, tpar), "sc", 16)
        sc_ln2h[tt] = P.v("sc")
        P.w("gpsimd", "sc", sc_ln2h[tt])
        P.o("gpsimd", (lambda t_: (lambda g: g.collective_compute("AllGather",
             ALU.bypass, replica_groups=[list(range(8))],
             ins=[ln2_loc.ap()[t_ * 128:(t_ + 1) * 128, :].opt()],
             outs=[ln2_all.ap()[t_ * 1024:(t_ + 1) * 1024, :].opt()])))(tt), "cc")
        for c8 in range(8):
            s8 = slice((c8 % 2) * 128, (c8 % 2) * 128 + 128)
            P.w("tensor", "dv", acl)
            P.w("tensor", "dv", P.v("dv"))
            P.o("tensor", (lambda cc, s8_, pr: (lambda t: t.transpose(pbT[:, s8_],
                 ln2t[:, pr, cc * 128:(cc + 1) * 128], ident[:])))(c8, s8, tpar), "pe")
            P.w("vector", "pe", P.v("pe"))
            P.o("vector", (lambda cc, t_, s8_: (lambda v: v.tensor_copy(
                 ln2T[:, cc, t_ * 128:(t_ + 1) * 128], pbT[:, s8_])))(c8, tt, s8), "dv")
        dv_ln2T[tt] = P.v("dv")
    SC_LN2 = P.v("sc")
    SP_X1 = P.v("sp")
    if "x1" in debug:
        dout("x1", [HTOK, D])
        P.w("sync", "sp", SP_X1)
        P.o("sync", lambda sy: sy.dma_start(out=dbgout["x1"][:], in_=x1_tm.ap()[:]), "sp", 16)
    if "ln2" in debug:
        dout("ln2", [HTOK, D], BF16)
        P.w("sync", "sc", SC_LN2)
        P.o("sync", lambda sy: sy.dma_start(out=dbgout["ln2"][:], in_=ln2_loc.ap()[:]), "sp", 16)
    if upto == "p3":
        return _finish()

    CC_LN2 = P.v("cc")

    for tt in range(16):
        P.w("tensor", "dv", dv_ln2T[tt])
        P.w("tensor", "gw", GW_WG)
        P.w("tensor", "dv", P.v("dv"))
        for c8 in range(8):
            P.o("tensor", (lambda cc, t_: (lambda t: t.matmul(pbC[:, 0:8],
                 ln2T[:, cc, t_ * 128:(t_ + 1) * 128], wgs[:, cc, 0:8],
                 start=(cc == 0), stop=(cc == 7))))(c8, tt), "pe" if c8 == 7 else None)
        P.w("vector", "pe", P.v("pe"))
        P.w("vector", "dv", DV_INIT)
        P.o("vector", (lambda t_: (lambda v: v.tensor_add(rt0[:, t_, :], pbC[:, 0:8], bgeps[:])))(tt), "dv")
    for tt in range(16):
        P.o("vector", (lambda t_: (lambda v: v.reduce_max(l1a[:, t_:t_ + 1], rt0[:, t_, :], axis=AX.X)))(tt))
    P.o("vector", lambda v: v.engine_nop(), "dv")
    P.w("scalar", "dv", P.v("dv"))
    P.o("scalar", lambda s: s.copy(l1b[:], l1a[:]), "ac")
    P.w("vector", "ac", P.v("ac"))
    for tt in range(16):
        P.o("vector", (lambda t_: (lambda v: v.tensor_single_scalar(eq1a[:, t_, :], rt0[:, t_, :],
             l1b[:, t_:t_ + 1], ALU.is_equal)))(tt))
    for tt in range(16):
        P.o("vector", (lambda t_: (lambda v: v.scalar_tensor_tensor(msk2[:, t_, :], eq1a[:, t_, :],
             -1e30, rt0[:, t_, :], ALU.mult, ALU.add)))(tt))
    for tt in range(16):
        P.o("vector", (lambda t_: (lambda v: v.reduce_max(l2a[:, t_:t_ + 1], msk2[:, t_, :], axis=AX.X)))(tt))
    P.o("vector", lambda v: v.engine_nop(), "dv")
    P.w("scalar", "dv", P.v("dv"))
    P.o("scalar", lambda s: s.copy(l2b[:], l2a[:]), "ac")
    P.w("vector", "ac", P.v("ac"))
    for tt in range(16):
        P.o("vector", (lambda t_: (lambda v: v.tensor_single_scalar(eq2a[:, t_, :], msk2[:, t_, :],
             l2b[:, t_:t_ + 1], ALU.is_equal)))(tt))
    P.o("vector", lambda v: v.tensor_sub(dal[:], l2a[:], l1a[:]), "dv")
    P.w("scalar", "dv", P.v("dv"))
    P.o("scalar", lambda s: s.activation(g1s[:], dal[:], AF.Sigmoid, scale=-1.0))
    P.o("scalar", lambda s: s.activation(g2s[:], dal[:], AF.Sigmoid), "ac")
    for tt in range(16):
        P.o("vector", (lambda t_: (lambda v: v.tensor_mul(rt0[:, t_, :], eq1a[:, t_, :], iot8[:])))(tt))
    for tt in range(16):
        P.o("vector", (lambda t_: (lambda v: v.reduce_sum(e1s[:, t_:t_ + 1], rt0[:, t_, :], axis=AX.X)))(tt))
    for tt in range(16):
        P.o("vector", (lambda t_: (lambda v: v.tensor_mul(rt0[:, t_, :], eq2a[:, t_, :], iot8[:])))(tt))
    for tt in range(16):
        P.o("vector", (lambda t_: (lambda v: v.reduce_sum(e2s[:, t_:t_ + 1], rt0[:, t_, :], axis=AX.X)))(tt))
    P.o("vector", lambda v: v.tensor_copy(ohb[:, 0, :, :], eq1a[:]))
    P.o("vector", lambda v: v.tensor_copy(ohb[:, 1, :, :], eq2a[:]), "dv")
    DV_OH = P.v("dv")
    pv_evac = {}
    slot_i = 0
    for k2 in range(2):
        for tt in range(16):
            pbank = pbD if slot_i % 2 == 0 else pbC
            sl8 = slice(((slot_i // 2) % 8) * 8, ((slot_i // 2) % 8) * 8 + 8)
            P.w("tensor", "dv", DV_OH)
            if slot_i >= 2:
                P.w("tensor", "dv", pv_evac[slot_i - 2])
            first = True
            for (kk, tt2) in [(0, t2) for t2 in range(16)] + [(1, t2) for t2 in range(16)]:
                if (kk, tt2) >= (k2, tt):
                    break
                P.o("tensor", (lambda kk_, t2_, s8_, st_, pb_: (lambda t: t.matmul(pb_[:, s8_],
                     onesb128[:], ohb[:, kk_, t2_, :], start=st_, stop=False)))(kk, tt2, sl8, first, pbank))
                first = False
            P.o("tensor", (lambda k_, t_, s8_, st_, pb_: (lambda t: t.matmul(pb_[:, s8_],
                 ltri[:], ohb[:, k_, t_, :], start=st_, stop=True)))(k2, tt, sl8, first, pbank), "pe")
            P.w("vector", "pe", P.v("pe"))
            P.o("vector", (lambda k_, t_, s8_, pb_: (lambda v: v.tensor_copy(posall[:, k_, t_, :], pb_[:, s8_])))(k2, tt, sl8, pbank), "dv")
            pv_evac[slot_i] = P.v("dv")
            slot_i += 1
    for k2 in range(2):
        for tt in range(16):
            P.w("tensor", "dv", DV_OH)
            P.o("tensor", (lambda k_, t_, st_: (lambda t: t.matmul(pbE[:, k_ * 8:k_ * 8 + 8],
                 onesb128[:], ohb[:, k_, t_, :], start=st_, stop=(t_ == 15))))(k2, tt, tt == 0),
                "pe" if tt == 15 else None)
    P.w("vector", "pe", P.v("pe"))
    P.o("vector", lambda v: v.tensor_copy(cntrow[0:1, :], pbE[0:1, 0:16]), "dv")
    DV_CNT = P.v("dv")
    P.o("vector", lambda v: v.tensor_mul(msk2[:], posall[:, 0, :, :], eq1a[:]))
    P.o("vector", lambda v: v.tensor_mul(rt0[:], posall[:, 1, :, :], eq2a[:]), "dv")
    for tt in range(16):
        P.o("vector", (lambda t_: (lambda v: v.reduce_sum(poss1[:, t_:t_ + 1], msk2[:, t_, :], axis=AX.X)))(tt))
    for tt in range(16):
        P.o("vector", (lambda t_: (lambda v: v.reduce_sum(poss2[:, t_:t_ + 1], rt0[:, t_, :], axis=AX.X)))(tt))
    P.o("vector", lambda v: v.engine_nop(), "dv")
    DV_ROUTE = P.v("dv")
    P.w("scalar", "dv", DV_CNT)
    P.o("scalar", lambda s: s.dma_start(out=cnt_loc.ap()[0:1, :], in_=cntrow[0:1, 0:8]), "sc", 16)
    P.o("scalar", lambda s: s.dma_start(out=cnt_loc.ap()[1:2, :], in_=cntrow[0:1, 8:16]), "sc", 16)
    P.w("gpsimd", "sc", P.v("sc"))
    P.o("gpsimd", lambda g: g.collective_compute("AllGather", ALU.bypass,
         replica_groups=[list(range(8))],
         ins=[cnt_loc.ap().opt()], outs=[cnt_all.ap().opt()]), "cc")
    P.w("sync", "cc", P.v("cc"))
    P.o("sync", lambda sy: sy.dma_start(out=cntag[:], in_=cnt_all.ap()[:]), "ld", 16)
    P.w("tensor", "ld", P.v("ld"))
    P.o("tensor", lambda t: t.matmul(pbE[0:1, 0:8], selmat[:, 0:1], cntag[:], start=True, stop=True))
    P.o("tensor", lambda t: t.matmul(pbE[0:1, 8:16], selmat[:, 1:2], cntag[:], start=True, stop=True), "pe")
    P.w("vector", "pe", P.v("pe"))
    P.o("vector", lambda v: v.tensor_copy(offrow[0:1, :], pbE[0:1, 0:16]), "dv")
    P.w("tensor", "dv", P.v("dv"))
    P.o("tensor", lambda t: t.matmul(pbE[:, 0:8], onesf128[0:1, :], offrow[0:1, 0:8], start=True, stop=True))
    P.o("tensor", lambda t: t.matmul(pbE[:, 8:16], onesf128[0:1, :], offrow[0:1, 8:16], start=True, stop=True), "pe")
    P.w("vector", "pe", P.v("pe"))
    P.o("vector", lambda v: v.tensor_copy(offb[:, 0, :], pbE[:, 0:8]))
    P.o("vector", lambda v: v.tensor_copy(offb[:, 1, :], pbE[:, 8:16]), "dv")
    DV_OFFB = P.v("dv")
    P.w("vector", "dv", DV_OFFB)
    for tt in range(16):
        P.o("vector", (lambda t_: (lambda v: v.tensor_mul(msk2[:, t_, :], eq1a[:, t_, :], offb[:, 0, :])))(tt))
    for tt in range(16):
        P.o("vector", (lambda t_: (lambda v: v.tensor_mul(rt0[:, t_, :], eq2a[:, t_, :], offb[:, 1, :])))(tt))
    for tt in range(16):
        P.o("vector", (lambda t_: (lambda v: v.reduce_sum(offsl1[:, t_:t_ + 1], msk2[:, t_, :], axis=AX.X)))(tt))
    for tt in range(16):
        P.o("vector", (lambda t_: (lambda v: v.reduce_sum(offsl2[:, t_:t_ + 1], rt0[:, t_, :], axis=AX.X)))(tt))
    def bounce():
        P.o("vector", lambda v: v.engine_nop(), "dv")
        P.w("scalar", "dv", P.v("dv"))
        P.o("scalar", lambda s: s.copy(cntrow[0:1, 0:1], cntrow[0:1, 0:1]), "ac")
        P.w("vector", "ac", P.v("ac"))
    P.o("vector", lambda v: v.tensor_add(poss1[:], poss1[:], offsl1[:]))
    P.o("vector", lambda v: v.tensor_add(poss2[:], poss2[:], offsl2[:]))
    bounce()
    P.o("vector", lambda v: v.tensor_single_scalar(keep1[:], poss1[:], float(CAP), ALU.is_lt))
    P.o("vector", lambda v: v.tensor_single_scalar(keep2[:], poss2[:], float(CAP), ALU.is_lt))
    P.o("vector", lambda v: v.tensor_scalar_mul(offsl1[:], e1s[:], float(CAPP)))
    P.o("vector", lambda v: v.tensor_scalar_mul(offsl2[:], e2s[:], float(CAPP)))
    bounce()
    P.o("vector", lambda v: v.tensor_scalar(l1a[:], keep1[:], -1e6, 1e6, ALU.mult, ALU.add))
    P.o("vector", lambda v: v.tensor_scalar(l2a[:], keep2[:], -1e6, 1e6, ALU.mult, ALU.add))
    P.o("vector", lambda v: v.tensor_add(offsl1[:], offsl1[:], poss1[:]))
    P.o("vector", lambda v: v.tensor_add(offsl2[:], offsl2[:], poss2[:]))
    bounce()
    P.o("vector", lambda v: v.tensor_add(offsl1[:], offsl1[:], l1a[:]))
    P.o("vector", lambda v: v.tensor_add(offsl2[:], offsl2[:], l2a[:]))
    P.o("vector", lambda v: v.tensor_mul(g1s[:], g1s[:], keep1[:]))
    P.o("vector", lambda v: v.tensor_mul(g2s[:], g2s[:], keep2[:]))
    bounce()
    P.o("vector", lambda v: v.tensor_copy(idx1s[:], offsl1[:]))
    P.o("vector", lambda v: v.tensor_copy(idx2s[:], offsl2[:]))
    P.o("vector", lambda v: v.tensor_copy(y1i[:], poss1[:]))
    P.o("vector", lambda v: v.tensor_copy(y2i[:], poss2[:]))
    P.o("vector", lambda v: v.tensor_copy(t1i[:], e1s[:]))
    P.o("vector", lambda v: v.tensor_copy(t2i[:], e2s[:]))
    bounce()
    P.o("vector", lambda v: v.tensor_single_scalar(m1i[:], y1i[:], 511, ALU.bitwise_and))
    P.o("vector", lambda v: v.tensor_single_scalar(m2i[:], y2i[:], 511, ALU.bitwise_and))
    P.o("vector", lambda v: v.tensor_single_scalar(idx1y[:], y1i[:], 8, ALU.mult))
    P.o("vector", lambda v: v.tensor_single_scalar(idx2y[:], y2i[:], 8, ALU.mult))
    P.o("vector", lambda v: v.tensor_single_scalar(t1i[:], t1i[:], 512, ALU.mult))
    P.o("vector", lambda v: v.tensor_single_scalar(t2i[:], t2i[:], 512, ALU.mult))
    bounce()
    P.o("vector", lambda v: v.tensor_single_scalar(m1i[:], m1i[:], 7, ALU.mult))
    P.o("vector", lambda v: v.tensor_single_scalar(m2i[:], m2i[:], 7, ALU.mult))
    P.o("vector", lambda v: v.tensor_add(idx1y[:], idx1y[:], t1i[:]))
    P.o("vector", lambda v: v.tensor_add(idx2y[:], idx2y[:], t2i[:]))
    P.o("vector", lambda v: v.tensor_copy(t1i[:], l1a[:]))
    P.o("vector", lambda v: v.tensor_copy(t2i[:], l2a[:]))
    bounce()
    P.o("vector", lambda v: v.tensor_sub(idx1y[:], idx1y[:], m1i[:]))
    P.o("vector", lambda v: v.tensor_sub(idx2y[:], idx2y[:], m2i[:]))
    bounce()
    P.o("vector", lambda v: v.tensor_add(idx1y[:], idx1y[:], t1i[:]))
    P.o("vector", lambda v: v.tensor_add(idx2y[:], idx2y[:], t2i[:]), "dv")
    xbounce()
    DV_IDX = P.v("dv")
    if upto == "route":
        return _finish()
    w1r = w1_d.rearrange("(c p) f -> p c f", p=128)
    w2r = w2_d.rearrange("(c p) m -> p c m", p=128)
    gw_w1 = []
    P.w("gpsimd", "dv", dv_ln2T[15])
    P.w("gpsimd", "pe", P.v("pe"))
    P.w("gpsimd", "sp", P.v("sp"))
    P.w("gpsimd", "sc", P.v("sc"))
    for q4 in range(4):
        P.o("gpsimd", (lambda q_: (lambda g: g.dma_start(out=w1s[:, :, q_ * 1024:(q_ + 1) * 1024],
             in_=w1r[:, :, q_ * 1024:(q_ + 1) * 1024])))(q4), "gw", 16)
        gw_w1.append(P.v("gw"))
    gw_w2 = []
    for q4 in range(4):
        P.o("gpsimd", (lambda q_: (lambda g: g.dma_start(out=w2s[:, q_ * 8:(q_ + 1) * 8, :],
             in_=w2r[:, q_ * 8:(q_ + 1) * 8, :])))(q4), "gw", 16)
        gw_w2.append(P.v("gw"))
    slot_loc_v = slot_loc.ap().rearrange("(p c) u -> p (c u)", p=128)
    P.w("gpsimd", "dv", DV_INIT)
    P.o("gpsimd", lambda g: g.dma_start(out=slot_loc_v, in_=zero1[:]), "g2", 16)
    G2_Z = P.v("g2")
    P.w("gpsimd", "dv", DV_IDX)
    P.o("gpsimd", lambda g: g.tensor_copy(idx1g[:], idx1s[:]))
    P.o("gpsimd", lambda g: g.tensor_copy(idx2g[:], idx2s[:]), "gp")
    for tt in range(16):
        for k2 in range(2):
            ix = idx1g if k2 == 0 else idx2g
            P.w("gpsimd", "g2", G2_Z)
            P.o("gpsimd", (lambda t_, i_: (lambda g: g.indirect_dma_start(
                 out=slot_loc.ap(),
                 out_offset=IndirectOffsetOnAxis(i_[:, t_:t_ + 1], 0),
                 in_=toks[:, t_:t_ + 1], in_offset=None,
                 bounds_check=REGS["slot"], oob_is_err=False)))(tt, ix), "g2", 16)
    if "sloc" in debug:
        dout("sloc", [E * CAPP, 1])
        P.w("sync", "g2", P.v("g2"))
        P.o("sync", lambda sy: sy.dma_start(out=dbgout["sloc"][:], in_=slot_loc.ap()[:]), "sp", 16)
    P.w("gpsimd", "g2", P.v("g2"))
    P.o("gpsimd", lambda g: g.collective_compute("AllGather", ALU.bypass,
         replica_groups=[list(range(8))],
         ins=[slot_loc.ap().opt()], outs=[slot_ag.ap().opt()]), "cc")
    CC_SLOT = P.v("cc")
    P.o("gpsimd", lambda g: g.tensor_copy(idx1g[:], idx1y[:]))
    P.o("gpsimd", lambda g: g.tensor_copy(idx2g[:], idx2y[:]), "gp")
    P.w("gpsimd", "cc", CC_SLOT)
    P.o("gpsimd", lambda g: g.collective_compute("AllGather", ALU.bypass,
         replica_groups=[list(range(8))],
         ins=[cnt_loc.ap().opt()], outs=[bar_o.ap().opt()]), "cc")
    CC_B2 = P.v("cc")
    if "sag" in debug:
        dout("sag1", [NC * E * CAPP, 1])
        P.w("sync", "cc", CC_B2)
        P.w("sync", "gw", gw_w2[3])
        P.o("sync", lambda sy: sy.dma_start(out=dbgout["sag1"][:], in_=slot_ag.ap()[:]), "sp", 16)
    P.w("gpsimd", "ld", LD_CONST)
    for j in range(8):
        P.o("gpsimd", (lambda j_: (lambda g: g.iota(agix8f[:, j_:j_ + 1],
             pattern=[[0, 1]], base=j_ * E * CAPP, channel_multiplier=128,
             allow_small_or_imprecise_dtypes=True)))(j))
    P.o("gpsimd", lambda g: g.tensor_scalar_add(agix8f[:], agix8f[:], msl0[:, 0:1]))
    P.o("gpsimd", lambda g: g.tensor_copy(agix8[:], agix8f[:]), "gp")
    for j in range(8):
        P.w("gpsimd", "dv", P.v("dv"))
        P.w("gpsimd", "cc", CC_B2)
        P.w("gpsimd", "gw", gw_w2[3])
        P.o("gpsimd", (lambda j_, pr: (lambda g: g.indirect_dma_start(
             out=st39[:, pr, :], out_offset=None, in_=slot_ag.ap(),
             in_offset=IndirectOffsetOnAxis(agix8[0:39, j_:j_ + 1], 0),
             bounds_check=REGS["sag"], oob_is_err=False)))(j, j % 2), "g2", 16)
        P.w("vector", "g2", P.v("g2"))
        P.w("vector", "dv", DV_INIT)
        P.o("vector", (lambda pr: (lambda v: v.tensor_add(sum39[:], sum39[:], st39[:, pr, :])))(j % 2), "dv")
    P.w("tensor", "dv", P.v("dv"))
    P.o("tensor", lambda t: t.transpose(pbE[:, 0:39], sum39[:], identf[0:39, 0:39]), "pe")
    P.w("vector", "pe", P.v("pe"))
    P.o("vector", lambda v: v.tensor_copy(slotm[:], pbE[:, 0:39]), "dv")
    xbounce()
    P.o("vector", lambda v: v.tensor_copy(i39a[:], slotm[:]), "dv")
    xbounce()
    P.o("vector", lambda v: v.tensor_single_scalar(i39b[:], i39a[:], 11, ALU.arith_shift_right))
    P.o("vector", lambda v: v.tensor_single_scalar(i39c[:], i39a[:], 127, ALU.bitwise_and))
    P.o("vector", lambda v: v.tensor_single_scalar(slotY[:], i39a[:], 8, ALU.mult))
    bounce()
    P.o("vector", lambda v: v.tensor_single_scalar(i39b[:], i39b[:], 16256, ALU.mult))
    P.o("vector", lambda v: v.tensor_single_scalar(i39c[:], i39c[:], 7, ALU.mult))
    bounce()
    P.o("vector", lambda v: v.tensor_sub(slotY[:], slotY[:], i39b[:]))
    bounce()
    P.o("vector", lambda v: v.tensor_sub(slotY[:], slotY[:], i39c[:]), "dv")
    xbounce()
    DV_SLOT = P.v("dv")
    P.w("gpsimd", "dv", DV_SLOT)
    P.o("gpsimd", lambda g: g.tensor_copy(slotmi[:], slotY[:]), "gp")
    if "route2" in debug:
        dout("oh1d", [128, 128]); dout("e1d", [128, 16])
        dout("posd", [128, 256]); dout("cntd", [1, 16])
        dout("offbd", [128, 16]); dout("cntagd", [16, 8])
        P.w("sync", "dv", DV_SLOT)
        P.o("sync", lambda sy: sy.dma_start(out=dbgout["oh1d"][:], in_=eq1a[:].rearrange("p a b -> p (a b)")), "sp", 16)
        P.o("sync", lambda sy: sy.dma_start(out=dbgout["e1d"][:], in_=e1s[:]), "sp", 16)
        P.o("sync", lambda sy: sy.dma_start(out=dbgout["posd"][:], in_=posall[:].rearrange("p a b c -> p (a b c)")), "sp", 16)
        P.o("sync", lambda sy: sy.dma_start(out=dbgout["cntd"][:], in_=cntrow[:]), "sp", 16)
        P.o("sync", lambda sy: sy.dma_start(out=dbgout["offbd"][:], in_=offb[:].rearrange("p a b -> p (a b)")), "sp", 16)
        P.o("sync", lambda sy: sy.dma_start(out=dbgout["cntagd"][:], in_=cntag[:]), "sp", 16)
    if "route" in debug:
        dout("idx1", [128, 16], I32); dout("idx2", [128, 16], I32); dout("g1", [128, 16])
        dout("g2v", [128, 16]); dout("slotm", [128, 39]); dout("poss1", [128, 16])
        P.w("sync", "dv", DV_SLOT)
        P.o("sync", lambda sy: sy.dma_start(out=dbgout["idx1"][:], in_=idx1s[:]), "sp", 16)
        P.o("sync", lambda sy: sy.dma_start(out=dbgout["idx2"][:], in_=idx2s[:]), "sp", 16)
        P.o("sync", lambda sy: sy.dma_start(out=dbgout["g1"][:], in_=g1s[:]), "sp", 16)
        P.o("sync", lambda sy: sy.dma_start(out=dbgout["g2v"][:], in_=g2s[:]), "sp", 16)
        P.o("sync", lambda sy: sy.dma_start(out=dbgout["slotm"][:], in_=slotm[:]), "sp", 16)
        P.o("sync", lambda sy: sy.dma_start(out=dbgout["poss1"][:], in_=poss1[:]), "sp", 16)
        dout("slotmi", [128, 39], I32)
        P.o("sync", lambda sy: sy.dma_start(out=dbgout["slotmi"][:], in_=slotmi[:]), "sp", 16)
    if upto == "slot":
        return _finish()

    g2_gath, pe_stage, ach, dv_ye, sc_ye = {}, {}, {}, {}, {}
    HSL = [pbA[:, 0:512], pbA[:, 512:1024], pbB[:, 0:512], pbB[:, 512:1024]]
    if "ln2a" in debug:
        dout("ln2a", [NC * HTOK, D], BF16)
        P.w("sync", "cc", CC_B2)
        P.o("sync", lambda sy: sy.dma_start(out=dbgout["ln2a"][:], in_=ln2_all.ap()[:]), "sp", 16)
    def ye_chunk_trigger(k):
        P.w("gpsimd", "sc", sc_ye[min(4 * k + 3, 38)])
        P.o("gpsimd", (lambda k_: (lambda g: g.collective_compute("AllGather",
             ALU.bypass, replica_groups=[list(range(8))],
             ins=[ye_loc.ap()[k_ * 512:(k_ + 1) * 512, :].opt()],
             outs=[ye_all.ap()[k_ * 4096:(k_ + 1) * 4096, :].opt()])))(k), "cc")
    for rb in range(10):
        sts = RB_ST[rb]
        if rb >= 2:
            ye_chunk_trigger(rb - 2)
        for st in sts:
            j = st - rb * 4
            P.w("gpsimd", "dv", DV_SLOT)
            P.w("gpsimd", "cc", CC_B2)
            if st >= 2:
                P.w("gpsimd", "pe", pe_stage[st - 2])
            P.o("gpsimd", (lambda s_, pr: (lambda g: g.indirect_dma_start(
                 out=stage[:, pr, :], out_offset=None,
                 in_=ln2_all.ap(),
                 in_offset=IndirectOffsetOnAxis(slotmi[:, s_:s_ + 1], 0),
                 bounds_check=REGS["ln2"], oob_is_err=False)))(st, st % 2), "g2", 16)
            g2_gath[st] = P.v("g2")
            for c8 in range(8):
                s8 = slice(((st * 8 + c8) % 2) * 128, ((st * 8 + c8) % 2) * 128 + 128)
                P.w("tensor", "g2", g2_gath[st])
                P.w("tensor", "dv", P.v("dv"))
                P.o("tensor", (lambda pr, cc, s8_: (lambda t: t.transpose(pbT[:, s8_],
                     stage[:, pr, cc * 128:(cc + 1) * 128], ident[:])))(st % 2, c8, s8), "pe")
                if c8 == 7:
                    pe_stage[st] = P.v("pe")
                P.w("vector", "pe", P.v("pe"))
                P.o("vector", (lambda cc, j_, s8_: (lambda v: v.tensor_copy(
                     bufT[:, cc, j_ * 128:(j_ + 1) * 128], pbT[:, s8_])))(c8, j, s8), "dv")
        dv_bufT = P.v("dv")
        xbounce()
        AC_BUF = P.v("ac")
        for fc in range(32):
            P.w("tensor", "dv", dv_bufT)
            P.w("tensor", "ac", AC_BUF)
            P.w("tensor", "gw", gw_w1[fc // 8])
            if fc >= 4:
                P.w("tensor", "ac", ach[(rb, fc - 4)])
            for c8 in range(8):
                P.o("tensor", (lambda cc, f_, sl_: (lambda t: t.matmul(sl_,
                     w1s[:, cc, f_ * 128:(f_ + 1) * 128], bufT[:, cc, :],
                     start=(cc == 0), stop=(cc == 7))))(c8, fc, HSL[fc % 4]))
            P.m("tensor", "pe")
            P.w("scalar", "pe", P.v("pe"))
            P.o("scalar", (lambda f_, sl_: (lambda s: s.activation(hsb[:, f_, :], sl_,
                 AF.Gelu_apprx_tanh, bias=b1c[:, f_:f_ + 1])))(fc, HSL[fc % 4]), "ac")
            ach[(rb, fc)] = P.v("ac")
        P.w("scalar", "ac", ach[(rb, 31)])
        P.o("scalar", lambda s: s.copy(cntrow[0:1, 0:1], cntrow[0:1, 0:1]), "ac")
        AC_H = P.v("ac")
        for j in range(len(sts)):
            rt = rb * 4 + j
            ypar = rt % 2
            P.w("tensor", "ac", AC_H)
            P.w("tensor", "gw", gw_w2[3])
            if rt >= 1:
                P.w("tensor", "dv", dv_ye[rt - 1])
            for nb2 in range(2):
                pv = pbC if nb2 == 0 else pbD
                for fc in range(32):
                    P.o("tensor", (lambda f_, j_, n_, pv_: (lambda t: t.matmul(pv_[:],
                         hsb[:, f_, j_ * 128:(j_ + 1) * 128], w2s[:, f_, n_ * 512:(n_ + 1) * 512],
                         start=(f_ == 0), stop=(f_ == 31))))(fc, j, nb2, pv))
            P.m("tensor", "pe")
            P.w("vector", "pe", P.v("pe"))
            if rt >= 2:
                P.w("vector", "sc", sc_ye[rt - 2])
            P.o("vector", (lambda pr: (lambda v: v.tensor_add(yet[:, pr, 0:512], pbC[:], b2bc[:, 0:512])))(ypar))
            P.o("vector", (lambda pr: (lambda v: v.tensor_add(yet[:, pr, 512:1024], pbD[:], b2bc[:, 512:1024])))(ypar), "dv")
            dv_ye[rt] = P.v("dv")
            P.w("scalar", "dv", dv_ye[rt])
            P.o("scalar", (lambda r_, pr: (lambda s: s.dma_start(
                 out=ye_loc.ap()[r_ * 128:(r_ + 1) * 128, :], in_=yet[:, pr, :])))(rt, ypar), "sc", 16)
            sc_ye[rt] = P.v("sc")
    SC_YE = P.v("sc")
    if "ye" in debug:
        dout("ye", [CAPP, D], BF16)
        P.w("sync", "sc", SC_YE)
        P.o("sync", lambda sy: sy.dma_start(out=dbgout["ye"][:], in_=ye_loc.ap()[0:CAPP, :]), "sp", 16)
    if upto == "p4":
        return _finish()
    ye_chunk_trigger(8)
    ye_chunk_trigger(9)
    CC_YE = P.v("cc")
    P.w("gpsimd", "cc", CC_YE)
    P.o("gpsimd", lambda g: g.collective_compute("AllGather", ALU.bypass,
         replica_groups=[list(range(8))],
         ins=[cnt_loc.ap().opt()], outs=[bar_o.ap().opt()]), "cc")
    CC_B3 = P.v("cc")

    sp_outh = {}
    P.w("vector", "cc", CC_B3)
    P.o("vector", lambda v: v.memset(gb1[:], 0.0))
    P.o("vector", lambda v: v.memset(gb2[:], 0.0), "dv")
    xbounce()
    GB_Z = P.v("dv")
    dv_comb = {}
    for tt in range(16):
        tpar = tt % 2
        if tt >= 2:
            P.w("gpsimd", "dv", dv_comb[tt - 2])
        P.w("gpsimd", "dv", GB_Z)
        P.w("gpsimd", "cc", CC_B3)
        P.o("gpsimd", (lambda t_, pr: (lambda g: g.indirect_dma_start(
             out=gb1[:, pr, :], out_offset=None, in_=ye_all.ap(),
             in_offset=IndirectOffsetOnAxis(idx1g[:, t_:t_ + 1], 0),
             bounds_check=REGS["ye"], oob_is_err=False)))(tt, tpar), "g2", 16)
        P.o("gpsimd", (lambda t_, pr: (lambda g: g.indirect_dma_start(
             out=gb2[:, pr, :], out_offset=None, in_=ye_all.ap(),
             in_offset=IndirectOffsetOnAxis(idx2g[:, t_:t_ + 1], 0),
             bounds_check=REGS["ye"], oob_is_err=False)))(tt, tpar), "g2", 16)
        g2v = P.v("g2")
        if tt < 2:
            P.w("sync", "cc", CC_B3)
        P.o("sync", (lambda t_, pr: (lambda sy: sy.dma_start(out=x1r[:, pr, :],
             in_=x1_tm.ap()[t_ * 128:(t_ + 1) * 128, :])))(tt, tpar), "ld", 16)
        P.w("vector", "g2", g2v)
        P.w("vector", "ld", P.v("ld"))
        if tt >= 2:
            P.w("vector", "sp", sp_outh[tt - 2])
        P.o("vector", (lambda t_, pr: (lambda v: v.tensor_scalar_mul(zsum[:], gb1[:, pr, :],
             g1s[:, t_:t_ + 1])))(tt, tpar))
        P.o("vector", (lambda t_, pr: (lambda v: v.scalar_tensor_tensor(outt[:, pr, :],
             gb2[:, pr, :], g2s[:, t_:t_ + 1], zsum[:], ALU.mult, ALU.add)))(tt, tpar))
        P.o("vector", (lambda pr: (lambda v: v.tensor_add(outt[:, pr, :], outt[:, pr, :],
             x1r[:, pr, :])))(tpar), "dv")
        dv_comb[tt] = P.v("dv")
        P.w("sync", "dv", P.v("dv"))
        P.o("sync", (lambda t_, pr: (lambda sy: sy.dma_start(
             out=y_out[t_ * 128:(t_ + 1) * 128, :], in_=outt[:, pr, :])))(tt, tpar), "sp", 16)
        sp_outh[tt] = P.v("sp")
    if "sag" in debug:
        dout("sag2", [NC * E * CAPP, 1])
        P.w("sync", "sp", P.v("sp"))
        P.w("sync", "dv", P.v("dv"))
        P.o("sync", lambda sy: sy.dma_start(out=dbgout["sag2"][:], in_=slot_ag.ap()[:]), "sp", 16)
    P.w("sync", "sp", P.v("sp"))
    P.w("sync", "sc", P.v("sc"))
    P.w("sync", "g2", P.v("g2"))
    P.w("sync", "cc", P.v("cc"))
    P.w("sync", "gw", P.v("gw"))

    return _finish()


_NC_CACHE = {}


def kernel(**inputs):
    if "main" not in _NC_CACHE:
        _NC_CACHE["main"] = build()[0]
    nc = _NC_CACHE["main"]
    consts = _host_constants()
    perm = _perm_cols()
    in_maps = [_host_inputs_for_core(c, inputs, consts, perm) for c in range(NC)]
    res = run_bass_kernel_spmd(nc, in_maps, core_ids=list(range(NC)))
    out = np.empty((4, S, D), np.float32)
    for c in range(NC):
        b, hg = c // 2, c % 2
        out[b, hg * HTOK:(hg + 1) * HTOK, :] = res.results[c]["y_out"]
    return out


# version 04
# speedup vs baseline: 1.2077x; 1.2077x over previous
import numpy as np
import ml_dtypes
from contextlib import ExitStack

import concourse.bass as bass
import concourse.mybir as mybir
from concourse.bass_utils import run_bass_kernel_spmd
from concourse.bass_core_rust import IndirectOffsetOnAxis

F32 = mybir.dt.float32
BF16 = mybir.dt.bfloat16
I32 = mybir.dt.int32
AF = mybir.ActivationFunctionType
ALU = mybir.AluOpType
AX = mybir.AxisListType
BF = ml_dtypes.bfloat16

D, S, HTOK, FF = 1024, 4096, 2048, 4096
E, CAP, CAPP = 8, 4916, 4992
NC = 8
EPS = 1e-5
NST = CAPP // 128
RB_ST = [list(range(rb * 4, min(rb * 4 + 4, NST))) for rb in range(10)]


class Plan:
    ENG = ("sync", "scalar", "vector", "tensor", "gpsimd")

    def __init__(self):
        self.ops = {e: [] for e in self.ENG}
        self.cnt = {}

    def w(self, eng, sem, val):
        if val > 0:
            self.ops[eng].append(("w", sem, val))

    def o(self, eng, fn, inc=None, k=1):
        incs = [(inc, k)] if inc is not None else []
        self.ops[eng].append(["o", fn, incs])
        if inc is not None:
            self.cnt[inc] = self.cnt.get(inc, 0) + k
        return self.cnt.get(inc, 0)

    def m(self, eng, inc, k=1):
        for entry in reversed(self.ops[eng]):
            if entry[0] == "o":
                entry[2].append((inc, k))
                break
        else:
            raise RuntimeError("no op to mark")
        self.cnt[inc] = self.cnt.get(inc, 0) + k
        return self.cnt[inc]

    def v(self, sem):
        return self.cnt.get(sem, 0)


def _host_constants():
    inv_freq = 1.0 / 10000.0 ** (2.0 * np.arange(0, 32, dtype=np.float64) / 64.0)
    ang = np.arange(S, dtype=np.float64)[None, :] * inv_freq[:, None]
    c32, s32 = np.cos(ang), np.sin(ang)
    costab = np.empty((128, S), np.float32)
    sintab = np.empty((128, S), np.float32)
    for d in range(128):
        j = d % 64
        costab[d] = c32[j % 32]
        sintab[d] = (-1.0 if j < 32 else 1.0) * s32[j % 32]
    cm = np.zeros((128, 4, 512), np.float32)
    ki = np.arange(128)[:, None]
    qi = np.arange(512)[None, :]
    for ob in range(4):
        cm[:, ob, :] = (ki + 128 * ob <= qi)
    p32 = np.zeros((128, 128), np.float32)
    for m in range(128):
        p32[(m + 32) if m % 64 < 32 else (m - 32), m] = 1.0
    return {
        "costab": costab.astype(BF), "sintab": sintab.astype(BF),
        "cmask": cm.reshape(128, 2048).astype(BF),
        "p32": p32.astype(BF),
        "ltri": (np.arange(128)[:, None] < np.arange(128)[None, :]).astype(BF),
        "ident": np.eye(128, dtype=BF),
        "onesbf1": np.ones((1, 128), BF),
        "onesf128": np.ones((128, 128), np.float32),
        "identf": np.eye(128, dtype=np.float32),
        "ones128bf": np.ones((128, 1), BF),
        "onesb128": np.ones((128, 128), BF),
        "fcon": np.broadcast_to(np.array([1e-5, 0.5, -0.5, -1.0, 0.125, 1.0, 0.0, 0.0], np.float32), (128, 8)).copy(),
    }


def _perm_cols():
    idx = np.empty(512, np.int64)
    for h in range(8):
        b = h * 64
        idx[b:b + 32] = b + 2 * np.arange(32)
        idx[b + 32:b + 64] = b + 2 * np.arange(32) + 1
    return idx


def _host_inputs_for_core(c, inp, consts, perm):
    b, hg = c // 2, c % 2
    x = np.asarray(inp["L0_attn_x"])
    xb = np.ascontiguousarray(x[b])
    sl = slice(hg * 512, (hg + 1) * 512)
    d = {
        "xT": np.ascontiguousarray(xb.T),
        "xrows": np.ascontiguousarray(xb[hg * HTOK:(hg + 1) * HTOK]),
        "wq": np.ascontiguousarray(np.asarray(inp["L0_attn_Wq"])[:, sl][:, perm]),
        "wk": np.ascontiguousarray(np.asarray(inp["L0_attn_Wk"])[:, sl][:, perm]),
        "wv": np.ascontiguousarray(np.asarray(inp["L0_attn_Wv"])[:, sl]),
        "wo": np.ascontiguousarray(np.asarray(inp["L0_attn_Wo"])),
        "bq": np.ascontiguousarray(np.asarray(inp["L0_attn_bq"])[sl][perm].reshape(4, 128).T),
        "bk": np.ascontiguousarray(np.asarray(inp["L0_attn_bk"])[sl][perm].reshape(4, 128).T),
        "bvbc": np.ascontiguousarray(np.broadcast_to(np.asarray(inp["L0_attn_bv"])[sl], (128, 512))),
        "bobc": np.ascontiguousarray(np.broadcast_to(np.asarray(inp["L0_attn_bo"]), (128, 1024))),
        "wg": np.ascontiguousarray(np.asarray(inp["L0_ffn_Wg"])),
        "bgbc": np.ascontiguousarray(np.broadcast_to(np.asarray(inp["L0_ffn_bg"]), (128, 8))),
        "w1": np.ascontiguousarray(np.asarray(inp["L0_ffn_W1"])[c]),
        "w2": np.ascontiguousarray(np.asarray(inp["L0_ffn_W2"])[c]),
        "b1": np.ascontiguousarray(np.asarray(inp["L0_ffn_b1"])[c].reshape(32, 128).T),
        "b2bc": np.ascontiguousarray(np.broadcast_to(np.asarray(inp["L0_ffn_b2"])[c], (128, 1024))),
        "tok0": np.full((128, 1), float(c * HTOK), np.float32),
        "hcol": np.full((128, 1), float(hg * HTOK), np.float32),
        "myslot0": np.full((128, 1), float(c * CAPP), np.float32),
    }
    sel = np.zeros((16, 2), np.float32)
    for r in range(8):
        for kp in range(2):
            if kp == 0 and r < c:
                sel[2 * r + kp, 0] = 1.0
            if (kp == 0 and r != c) or (kp == 1 and r < c):
                sel[2 * r + kp, 1] = 1.0
    d["selmat"] = sel
    d.update(consts)
    return d


def build(debug=(), upto="all", race=True):
    nc = bass.Bass("TRN2", target_bir_lowering=False, detect_race_conditions=race)
    ctx = ExitStack()

    def din(name, shape, dt=F32):
        return nc.dram_tensor(name, shape, dt, kind="ExternalInput").ap()

    xT = din("xT", [D, S]); xrows = din("xrows", [HTOK, D])
    wq_d = din("wq", [D, 512]); wk_d = din("wk", [D, 512])
    wv_d = din("wv", [D, 512]); wo_d = din("wo", [D, D])
    bq_d = din("bq", [128, 4]); bk_d = din("bk", [128, 4])
    bvbc_d = din("bvbc", [128, 512]); bobc_d = din("bobc", [128, 1024])
    wg_d = din("wg", [D, 8]); bgbc_d = din("bgbc", [128, 8])
    w1_d = din("w1", [D, FF]); w2_d = din("w2", [FF, D])
    b1_d = din("b1", [128, 32]); b2bc_d = din("b2bc", [128, 1024])
    selmat_d = din("selmat", [16, 2]); tok0_d = din("tok0", [128, 1])
    hcol_d = din("hcol", [128, 1])
    myslot0_d = din("myslot0", [128, 1])
    costab_d = din("costab", [128, S], BF16); sintab_d = din("sintab", [128, S], BF16)
    cmask_d = din("cmask", [128, 2048], BF16)
    p32_d = din("p32", [128, 128], BF16); ltri_d = din("ltri", [128, 128], BF16)
    ident_d = din("ident", [128, 128], BF16)
    onesbf1_d = din("onesbf1", [1, 128], BF16)
    onesf128_d = din("onesf128", [128, 128])
    identf_d = din("identf", [128, 128])
    ones128bf_d = din("ones128bf", [128, 1], BF16)
    onesb128_d = din("onesb128", [128, 128], BF16)
    fcon_d = din("fcon", [128, 8])

    y_out = nc.dram_tensor("y_out", [HTOK, D], F32, kind="ExternalOutput").ap()
    dbgout = {}

    def dout(name, shape, dt=F32):
        t = nc.dram_tensor("dbg_" + name, shape, dt, kind="ExternalOutput").ap()
        dbgout[name] = t
        return t

    xt_bf = nc.dram_tensor("xt_bf", [D, S], BF16)
    ag_in = nc.dram_tensor("ag_in", [512, S], BF16)
    ag_out = nc.dram_tensor("ag_out", [1024, S], BF16)
    x1_tm = nc.dram_tensor("x1_tm", [HTOK, D], F32)
    ln2_loc = nc.dram_tensor("ln2_loc", [HTOK, D], BF16)
    ln2_all = nc.dram_tensor("ln2_all", [NC * HTOK, D], BF16)
    cnt_loc = nc.dram_tensor("cnt_loc", [2, 8], F32)
    cnt_all = nc.dram_tensor("cnt_all", [16, 8], F32)
    bar_o = nc.dram_tensor("bar_o", [16, 8], F32)
    slot_loc = nc.dram_tensor("slot_loc", [E * CAPP, 1], F32)
    slot_ag = nc.dram_tensor("slot_ag", [NC * E * CAPP, 1], F32)
    ye_loc = nc.dram_tensor("ye_loc", [5120, D], BF16)
    ye_all = nc.dram_tensor("ye_all", [NC * 5120, D], BF16)

    sb = lambda n, s, d: ctx.enter_context(nc.sbuf_tensor(n, s, d))
    cmask = sb("cmaskt", [128, 4, 512], BF16)
    p32 = sb("p32t", [128, 128], BF16)
    ltri = sb("ltrit", [128, 128], BF16)
    ident = sb("identt", [128, 128], BF16)
    onesbf1 = sb("onesbf1t", [1, 128], BF16)
    onesf128 = sb("onesf128t", [128, 128], F32)
    onesb128 = sb("onesb128t", [128, 128], BF16)
    identf = sb("identft", [128, 128], F32)
    ones128bf = sb("ones128bft", [128, 1], BF16)
    fcon = sb("fcont", [128, 8], F32)
    for _i, _v in enumerate([1e-5, 0.5, -0.5, -1.0, 0.125, 1.0]):
        try:
            nc.const_aps.register(F32, _v, fcon[:, _i:_i + 1])
        except Exception:
            pass
    wgs = sb("wgs", [128, 8, 8], BF16)
    bqc = sb("bqc", [128, 4], F32)
    bkc = sb("bkc", [128, 4], F32)
    bgbc = sb("bgbct", [128, 8], F32)
    bgeps = sb("bgepst", [128, 8], F32)
    l1a = sb("l1at", [128, 16], F32)
    l1b = sb("l1bt", [128, 16], F32)
    l2a = sb("l2at", [128, 16], F32)
    l2b = sb("l2bt", [128, 16], F32)
    dal = sb("dalt", [128, 16], F32)
    offsl1 = sb("offsl1t", [128, 16], F32)
    offsl2 = sb("offsl2t", [128, 16], F32)
    cntrow = sb("cntrowt", [1, 16], F32)
    b1c = sb("b1ct", [128, 32], F32)
    b2bc = sb("b2bct", [128, 1024], F32)
    selmat = sb("selmatt", [16, 2], F32)
    tok0 = sb("tok0t", [128, 1], F32)
    hcol = sb("hcolt", [128, 1], F32)
    msl0 = sb("msl0t", [128, 1], F32)
    qn = sb("qnt", [1, 8], F32)
    negm = sb("negmt", [128, 1], F32)
    rtmp1 = sb("rtmp1", [128, 512], BF16)
    rtmp2 = sb("rtmp2", [128, 512], BF16)
    wqoc = sb("wqoc", [128, 2, 8, 128], BF16)
    wkoc = sb("wkoc", [128, 2, 8, 128], BF16)
    iot8 = sb("iot8", [128, 8], F32)
    e1s = sb("e1st", [128, 16], F32); e2s = sb("e2st", [128, 16], F32)
    poss1 = sb("poss1", [128, 16], F32); poss2 = sb("poss2", [128, 16], F32)
    g1s = sb("g1st", [128, 16], F32); g2s = sb("g2st", [128, 16], F32)
    keep1 = sb("keep1t", [128, 16], F32); keep2 = sb("keep2t", [128, 16], F32)
    idx1s = sb("idx1st", [128, 16], I32); idx2s = sb("idx2st", [128, 16], I32)
    idx1g = sb("idx1gt", [128, 16], I32); idx2g = sb("idx2gt", [128, 16], I32)
    y1i = sb("y1it", [128, 16], I32); y2i = sb("y2it", [128, 16], I32)
    m1i = sb("m1it", [128, 16], I32); m2i = sb("m2it", [128, 16], I32)
    t1i = sb("t1it", [128, 16], I32); t2i = sb("t2it", [128, 16], I32)
    idx1y = sb("idx1yt", [128, 16], I32); idx2y = sb("idx2yt", [128, 16], I32)
    slotY = sb("slotYt", [128, 39], I32)
    i39a = sb("i39at", [128, 39], I32); i39b = sb("i39bt", [128, 39], I32)
    i39c = sb("i39ct", [128, 39], I32)
    toks = sb("tokst", [128, 16], F32)
    offrow = sb("offrowt", [1, 16], F32)
    cntag = sb("cntagt", [16, 8], F32)
    offb = sb("offbt", [128, 2, 8], F32)
    zero1 = sb("zero1t", [128, 312], F32)
    st39 = sb("st39t", [39, 2, 128], F32)
    sum39 = sb("sum39t", [39, 128], F32)
    slotm = sb("slotmt", [128, 39], F32)
    slotmi = sb("slotmit", [128, 39], I32)
    agix = sb("agixt", [128, 1], I32)
    agixf = sb("agixft", [128, 1], F32)
    agix8 = sb("agix8t", [128, 8], I32)
    agix8f = sb("agix8ft", [128, 8], F32)
    stt = sb("sttt", [128, 8], F32)

    auto_used = nc.SBUF_PARTITION_SIZE_BYTES - nc.sbuf_bytes_remaining
    B0 = 48640
    assert auto_used <= B0, f"auto SBUF region overflow: {auto_used} > {B0}"
    TOP = nc.SBUF_PARTITION_SIZE_BYTES
    at = lambda n, s, d, off: nc.alloc_sbuf_tensor_at(n, s, d, offset=B0 + off)

    qrot = at("qrot", [128, 4, S], BF16, 0)
    krot = at("krot", [128, 4, S], BF16, 32768)
    vext = at("vext", [128, 32, 520], BF16, 65536)
    Z0 = 98816
    xf0 = at("xf0", [128, S], F32, 0)
    xf1 = at("xf1", [128, S], F32, 16384)
    xbb = at("xbb", [128, S], BF16, 32768)
    sqb = at("sqb", [128, S], BF16, 40960)
    R1 = at("R1", [1, S], F32, 49152)
    R2 = at("R2", [1, S], F32, 65536)
    xtil = at("xtil", [128, S], BF16, 81920)
    R1b = at("R1b", [1, S], BF16, Z0 + 0)
    R2b = at("R2b", [1, S], BF16, Z0 + 8192)
    rbcb = at("rbcb", [128, S], BF16, Z0 + 16384)
    murbcb = at("murbcb", [128, S], BF16, Z0 + 24576)
    tmpr = at("tmprt", [1, 3, 512], F32, Z0 + 32768)
    xstr = at("xstr", [128, 2, 8, 512], BF16, Z0 + 0)
    aq = at("aqt", [128, S], BF16, Z0 + 16384)
    ak = at("akt", [128, S], BF16, Z0 + 24576)
    costab = at("costabt", [128, S], BF16, Z0 + 32768)
    sintab = at("sintabt", [128, S], BF16, Z0 + 40960)
    wvs = at("wvst", [128, 8, 512], BF16, Z0 + 16384)
    bvbc = at("bvbct", [128, 512], F32, Z0 + 24576)
    psb = [at("psb0", [128, 1024], BF16, Z0 + 32768),
           at("psb1", [128, 1024], BF16, Z0 + 34816)]
    rlA = at("rlA", [128, 512], F32, Z0 + 36864)
    rlB = at("rlB", [128, 512], F32, Z0 + 38912)
    lrow = at("lrowt", [65, 1024], F32, Z0 + 40960)
    lrow2 = at("lrow2t", [65, 1024], F32, Z0 + 45056)
    stg = at("stgt", [64, 2, 2, 512], BF16, Z0 + 0)
    agT = at("agTt", [128, 8, HTOK], BF16, 0)
    ln2T = at("ln2Tt", [128, 8, HTOK], BF16, 32768)
    xrw = at("xrwt", [128, 2, 1024], F32, 65536)
    x1t = at("x1tt", [128, 2, 1024], F32, 73728)
    ln2t = at("ln2tt", [128, 2, 1024], BF16, 81920)
    sqf = at("sqft", [128, 1024], F32, 86016)
    rt0 = at("rt0t", [128, 16, 8], F32, 65536)
    eq1a = at("eq1at", [128, 16, 8], F32, 66048)
    msk2 = at("msk2t", [128, 16, 8], F32, 66560)
    eq2a = at("eq2at", [128, 16, 8], F32, 67072)
    ohb = at("ohbt", [128, 2, 16, 8], BF16, 67584)
    posall = at("posallt", [128, 2, 16, 8], F32, 68096)
    wos = at("wost", [128, 8, 1024], BF16, 90112)
    bobc = at("bobct", [128, 1024], F32, 106496)
    w1s = at("w1st", [128, 8, FF], BF16, 0)
    w2s = at("w2st", [128, 32, 1024], BF16, 65536)
    hsb = at("hsbt", [128, 32, 512], BF16, 131072)
    bufT = at("bufTt", [128, 8, 512], BF16, 163840)
    stage = at("staget", [128, 2, 1024], BF16, 172032)
    yet = at("yett", [128, 2, 1024], BF16, 176128)
    gb1 = at("gb1t", [128, 2, 1024], BF16, 0)
    gb2 = at("gb2t", [128, 2, 1024], BF16, 4096)
    x1r = at("x1rt", [128, 2, 1024], F32, 8192)
    outt = at("outtt", [128, 2, 1024], F32, 16384)
    zsum = at("zsumt", [128, 1024], F32, 24576)
    assert B0 + 180224 <= TOP - 8, (B0 + 180224, TOP)

    ps_ = lambda n, s, d: ctx.enter_context(nc.psum_tensor(n, s, d))
    pbA = ps_("pbA", [128, 1024], F32)
    pbB = ps_("pbB", [128, 1024], F32)
    pbC = ps_("pbC", [128, 512], F32)
    pbD = ps_("pbD", [128, 512], F32)
    pbE = ps_("pbE", [128, 512], F32)
    pbT = ps_("pbT", [128, 256], BF16)

    semn = ["ld", "gw", "pe", "dv", "ac", "sp", "g2", "sc", "gp",
            "smm", "exp", "msk", "omm", "nrm", "lcp", "rld", "rbc", "cc"]
    sems = {n: ctx.enter_context(nc.semaphore("s_" + n)) for n in semn}

    P = Plan()

    def xbounce():
        P.o("vector", lambda v: v.engine_nop(), "dv")
        P.w("scalar", "dv", P.v("dv"))
        P.o("scalar", lambda s: s.copy(cntrow[0:1, 0:1], cntrow[0:1, 0:1]), "ac")
        P.w("gpsimd", "ac", P.v("ac"))
        P.w("vector", "ac", P.v("ac"))

    def _finish():
        for _sn in semn:
            P.w("sync", _sn, P.v(_sn))
        block = ctx.enter_context(nc.Block())

        def emit(name):
            def body(eng):
                for entry in P.ops[name]:
                    if entry[0] == "w":
                        eng.wait_ge(sems[entry[1]], entry[2])
                    else:
                        bi = entry[1](eng)
                        for (inc, k) in entry[2]:
                            bi = bi.then_inc(sems[inc], k)
            return body

        block.sync(emit("sync"))
        block.scalar(emit("scalar"))
        block.vector(emit("vector"))
        block.tensor(emit("tensor"))
        block.gpsimd(emit("gpsimd"))
        ctx.close()
        return nc, dbgout


    const_loads = [(cmask, cmask_d.rearrange("p (a b) -> p a b", b=512)),
                   (p32, p32_d), (ltri, ltri_d), (ident, ident_d),
                   (onesbf1, onesbf1_d), (onesf128, onesf128_d), (identf, identf_d),
                   (onesb128, onesb128_d),
                   (ones128bf, ones128bf_d), (fcon, fcon_d), (bqc, bq_d), (bkc, bk_d),
                   (bgbc, bgbc_d),
                   (b1c, b1_d), (b2bc, b2bc_d), (selmat, selmat_d),
                   (tok0, tok0_d), (hcol, hcol_d), (msl0, myslot0_d)]
    for dst, src in const_loads:
        P.o("sync", (lambda d_, s_: (lambda sy: sy.dma_start(out=d_[:], in_=s_[:])))(dst, src), "ld", 16)
    LD_CONST = P.v("ld")
    P.o("gpsimd", lambda g: g.dma_start(out=wgs[:], in_=wg_d.rearrange("(c p) m -> p c m", p=128)), "gw", 16)
    GW_WG = P.v("gw")
    REGS = {}
    def _mkregs(g):
        REGS["slot"] = g.to_reg(E * CAPP - 1)
        REGS["ln2"] = g.to_reg(NC * HTOK - 1)
        REGS["ag"] = g.to_reg(1024 * S - 1)
        REGS["sag"] = g.to_reg(NC * E * CAPP - 1)
        REGS["ye"] = g.to_reg(NC * 5120 - 1)
        return g.nop()
    P.o("gpsimd", _mkregs)
    P.o("gpsimd", lambda g: g.iota(iot8[:], pattern=[[1, 8]], base=0, channel_multiplier=0,
                                   allow_small_or_imprecise_dtypes=True), "gp")
    P.o("gpsimd", lambda g: g.iota(toks[:], pattern=[[128, 16]], base=0, channel_multiplier=1,
                                   allow_small_or_imprecise_dtypes=True), "gp")
    G2_IOTA = P.v("gp")
    P.o("vector", lambda v: v.memset(zero1[:], 0.0), "dv")
    P.o("vector", lambda v: v.memset(qn[:], 0.0), "dv")
    P.o("vector", lambda v: v.memset(R1[:], 0.0), "dv")
    P.o("vector", lambda v: v.memset(R2[:], 0.0), "dv")
    P.o("vector", lambda v: v.memset(sum39[:], 0.0), "dv")
    P.o("vector", lambda v: v.memset(cntrow[:], 0.0), "dv")
    P.w("vector", "gp", G2_IOTA)
    P.w("vector", "ld", LD_CONST)
    P.o("vector", lambda v: v.tensor_scalar_add(toks[:], toks[:], tok0[:, 0:1]), "dv")
    P.o("vector", lambda v: v.tensor_scalar_mul(l1a[:, 0:8], iot8[:], 1e-5), "dv")
    P.w("vector", "dv", P.v("dv"))
    P.o("vector", lambda v: v.tensor_sub(bgeps[:], bgbc[:], l1a[:, 0:8]), "dv")
    DV_INIT = P.v("dv")

    xTr = xT.rearrange("(c p) t -> p c t", p=128)
    xf = [xf0, xf1]

    dv_cast = {}
    for c8 in range(8):
        if c8 >= 2:
            P.w("sync", "dv", dv_cast[c8 - 2])
        P.o("sync", (lambda cc: (lambda sy: sy.dma_start(out=xf[cc % 2][:], in_=xTr[:, cc, :])))(c8), "ld", 16)
        ldv = P.v("ld")
        P.w("vector", "ld", ldv)
        if c8 >= 1:
            P.w("vector", "pe", P.v("pe"))
        P.o("vector", (lambda cc: (lambda v: v.tensor_copy(xbb[:], xf[cc % 2][:])))(c8), "dv")
        dv_cast[c8] = P.v("dv")
        P.w("scalar", "dv", dv_cast[c8])
        P.w("scalar", "pe", P.v("pe"))
        P.o("scalar", lambda s: s.activation(sqb[:], xbb[:], AF.Square), "ac")
        acs = P.v("ac")
        for tb in range(8):
            P.w("tensor", "dv", dv_cast[c8])
            P.w("tensor", "ac", acs)
            P.w("tensor", "dv", P.v("dv"))
            P.o("tensor", (lambda t0: (lambda t: t.matmul(pbC[0:1, :], ones128bf[:],
                 xbb[:, t0 * 512:(t0 + 1) * 512], start=True, stop=True)))(tb))
            P.o("tensor", (lambda t0: (lambda t: t.matmul(pbD[0:1, :], ones128bf[:],
                 sqb[:, t0 * 512:(t0 + 1) * 512], start=True, stop=True)))(tb), "pe")
            P.w("vector", "pe", P.v("pe"))
            P.o("vector", (lambda t0: (lambda v: v.tensor_add(
                 R1[0:1, t0 * 512:(t0 + 1) * 512], R1[0:1, t0 * 512:(t0 + 1) * 512], pbC[0:1, :])))(tb))
            P.o("vector", (lambda t0: (lambda v: v.tensor_add(
                 R2[0:1, t0 * 512:(t0 + 1) * 512], R2[0:1, t0 * 512:(t0 + 1) * 512], pbD[0:1, :])))(tb), "dv")
    for tb in range(8):
        s0, s1 = tb * 512, (tb + 1) * 512
        P.w("vector", "dv", P.v("dv"))
        P.o("vector", (lambda a, b: (lambda v: v.tensor_scalar_mul(tmpr[0:1, 0, :], R1[0:1, a:b], 1.0 / D)))(s0, s1))
        P.o("vector", (lambda a, b: (lambda v: v.tensor_scalar_mul(tmpr[0:1, 1, :], R2[0:1, a:b], 1.0 / D)))(s0, s1), "dv")
        P.w("scalar", "dv", P.v("dv"))
        P.o("scalar", lambda s: s.activation(tmpr[0:1, 2, :], tmpr[0:1, 0, :], AF.Square), "ac")
        P.w("vector", "ac", P.v("ac"))
        P.o("vector", lambda v: v.tensor_sub(tmpr[0:1, 1, :], tmpr[0:1, 1, :], tmpr[0:1, 2, :]), "dv")
        P.w("scalar", "dv", P.v("dv"))
        P.o("scalar", lambda s: s.activation(tmpr[0:1, 1, :], tmpr[0:1, 1, :], AF.Ln, bias=EPS))
        P.o("scalar", (lambda a, b: (lambda s: s.activation(R1[0:1, a:b], tmpr[0:1, 1, :], AF.Exp, scale=-0.5)))(s0, s1), "ac")
        P.w("vector", "ac", P.v("ac"))
        P.o("vector", (lambda a, b: (lambda v: v.tensor_mul(R2[0:1, a:b], tmpr[0:1, 0, :], R1[0:1, a:b])))(s0, s1))
        P.o("vector", (lambda a, b: (lambda v: v.tensor_copy(R1b[0:1, a:b], R1[0:1, a:b])))(s0, s1))
        P.o("vector", (lambda a, b: (lambda v: v.tensor_copy(R2b[0:1, a:b], R2[0:1, a:b])))(s0, s1), "dv")
    DV_STATS = P.v("dv")
    for tb in range(8):
        s0, s1 = tb * 512, (tb + 1) * 512
        P.w("tensor", "dv", P.v("dv"))
        P.o("tensor", (lambda a, b: (lambda t: t.matmul(pbE[:], onesbf1[:], R1b[0:1, a:b],
             start=True, stop=True)))(s0, s1), "pe")
        P.w("vector", "pe", P.v("pe"))
        P.o("vector", (lambda a, b: (lambda v: v.tensor_copy(rbcb[:, a:b], pbE[:])))(s0, s1), "dv")
        P.w("tensor", "dv", P.v("dv"))
        P.o("tensor", (lambda a, b: (lambda t: t.matmul(pbE[:], onesbf1[:], R2b[0:1, a:b],
             start=True, stop=True)))(s0, s1), "pe")
        P.w("vector", "pe", P.v("pe"))
        P.o("vector", (lambda a, b: (lambda v: v.tensor_copy(murbcb[:, a:b], pbE[:])))(s0, s1), "dv")
    DV_BC = P.v("dv")

    sp_xtil = {}
    for c8 in range(8):
        P.w("sync", "dv", DV_BC)
        if c8 >= 2:
            P.w("sync", "dv", P.v("dv"))
        P.o("sync", (lambda cc: (lambda sy: sy.dma_start(out=xf[cc % 2][:], in_=xTr[:, cc, :])))(c8), "ld", 16)
        ldv = P.v("ld")
        P.w("vector", "ld", ldv)
        if c8 >= 1:
            P.w("vector", "sp", sp_xtil[c8 - 1])
        P.o("vector", (lambda cc: (lambda v: v.tensor_copy(xbb[:], xf[cc % 2][:])))(c8))
        P.o("vector", lambda v: v.tensor_mul(xtil[:], xbb[:], rbcb[:]))
        P.o("vector", lambda v: v.tensor_sub(xtil[:], xtil[:], murbcb[:]), "dv")
        P.w("sync", "dv", P.v("dv"))
        P.o("sync", (lambda cc: (lambda sy: sy.dma_start(
            out=xt_bf.ap()[cc * 128:(cc + 1) * 128, :], in_=xtil[:])))(c8), "sp", 16)
        sp_xtil[c8] = P.v("sp")
    SP_XT = P.v("sp")
    if "xt" in debug:
        dout("xt", [D, S], BF16)
        P.w("sync", "sp", SP_XT)
        P.o("sync", lambda sy: sy.dma_start(out=dbgout["xt"][:], in_=xt_bf.ap()[:]), "sp", 16)
    if upto == "p0":
        return _finish()

    P.w("sync", "sp", SP_XT)
    P.o("sync", lambda sy: sy.dma_start(out=costab[:], in_=costab_d[:]), "ld", 16)
    P.o("sync", lambda sy: sy.dma_start(out=sintab[:], in_=sintab_d[:]), "ld", 16)
    LD_TAB = P.v("ld")

    xtr = xt_bf.ap().rearrange("(c p) t -> p c t", p=128)
    pe_xstr = {}
    gw_oc = {}
    for oc in range(4):
        P.w("gpsimd", "pe", pe_xstr.get((oc - 1, 7), 0))
        P.o("gpsimd", (lambda o_: (lambda g: g.dma_start(out=wqoc[:, o_ % 2, :, :],
             in_=wq_d.rearrange("(c p) m -> p c m", p=128)[:, :, o_ * 128:(o_ + 1) * 128])))(oc), "gw", 16)
        P.o("gpsimd", (lambda o_: (lambda g: g.dma_start(out=wkoc[:, o_ % 2, :, :],
             in_=wk_d.rearrange("(c p) m -> p c m", p=128)[:, :, o_ * 128:(o_ + 1) * 128])))(oc), "gw", 16)
        gw_oc[oc] = P.v("gw")
        for tb in range(8):
            it = oc * 8 + tb
            P.w("sync", "sp", SP_XT)
            if it >= 2:
                P.w("sync", "pe", pe_xstr[(oc, tb - 2)] if tb >= 2 else pe_xstr[(oc - 1, 6 + tb)])
            P.o("sync", (lambda t0, pr: (lambda sy: sy.dma_start(
                out=xstr[:, pr, :, :], in_=xtr[:, :, t0 * 512:(t0 + 1) * 512])))(tb, it % 2), "ld", 16)
            ldv = P.v("ld")
            P.w("tensor", "ld", ldv)
            P.w("tensor", "gw", gw_oc[oc])
            P.w("tensor", "dv", P.v("dv"))
            for c8 in range(8):
                P.o("tensor", (lambda cc, o_, pr: (lambda t: t.matmul(pbC[:],
                     wqoc[:, o_ % 2, cc, :], xstr[:, pr, cc, :],
                     start=(cc == 0), stop=(cc == 7))))(c8, oc, it % 2))
            for c8 in range(8):
                P.o("tensor", (lambda cc, o_, pr: (lambda t: t.matmul(pbD[:],
                     wkoc[:, o_ % 2, cc, :], xstr[:, pr, cc, :],
                     start=(cc == 0), stop=(cc == 7))))(c8, oc, it % 2))
            pe_xstr[(oc, tb)] = P.m("tensor", "pe")
            s0, s1 = tb * 512, (tb + 1) * 512
            P.w("vector", "pe", P.v("pe"))
            P.o("vector", (lambda a, b, o_: (lambda v: v.tensor_scalar_add(aq[:, a:b], pbC[:], bqc[:, o_:o_ + 1])))(s0, s1, oc))
            P.o("vector", (lambda a, b, o_: (lambda v: v.tensor_scalar_add(ak[:, a:b], pbD[:], bkc[:, o_:o_ + 1])))(s0, s1, oc), "dv")
            dvqk = P.v("dv")
            if upto in ("p1q",) and oc == 0 and tb == 7:
                return _finish()
            if upto == "p1q":
                continue
            for (src, dst, half) in ((aq, qrot, slice(0, 512)), (ak, krot, slice(512, 1024))):
                P.w("tensor", "dv", dvqk)
                P.w("tensor", "dv", P.v("dv"))
                P.o("tensor", (lambda s_, a, b, h: (lambda t: t.matmul(pbA[:, h],
                     p32[:], s_[:, a:b], start=True, stop=True)))(src, s0, s1, half), "pe")
                P.w("vector", "pe", P.v("pe"))
                P.w("vector", "ld", LD_TAB)
                P.o("vector", (lambda a, b, h: (lambda v: v.tensor_mul(rtmp1[:], pbA[:, h], sintab[:, a:b])))(s0, s1, half))
                P.o("vector", (lambda s_, a, b: (lambda v: v.tensor_mul(rtmp2[:], s_[:, a:b], costab[:, a:b])))(src, s0, s1))
                P.o("vector", (lambda d_, o_, a, b: (lambda v: v.tensor_add(d_[:, o_, a:b], rtmp1[:], rtmp2[:])))(dst, oc, s0, s1), "dv")
        if upto == "p1r" and oc == 0:
            return _finish()
        for (dst, col) in ((qrot, 0), (krot, 1)):
            P.w("scalar", "dv", P.v("dv"))
            P.o("scalar", (lambda d_, o_: (lambda s: s.activation(aq[:], d_[:, o_, :], AF.Square)))(dst, oc), "ac")
            acs = P.v("ac")
            for tb in range(8):
                P.w("tensor", "ac", acs)
                P.w("tensor", "dv", P.v("dv"))
                P.o("tensor", (lambda t0: (lambda t: t.matmul(pbE[0:1, :], ones128bf[:],
                     aq[:, t0 * 512:(t0 + 1) * 512], start=True, stop=True)))(tb), "pe")
                P.w("vector", "pe", P.v("pe"))
                P.o("vector", lambda v: v.reduce_max(qn[0:1, 7:8], pbE[0:1, :], axis=AX.X))
                P.o("vector", (lambda cl: (lambda v: v.tensor_max(qn[0:1, cl:cl + 1],
                     qn[0:1, cl:cl + 1], qn[0:1, 7:8])))(col), "dv")
        if upto == "p1a" and oc == 0:
            return _finish()
    P.w("scalar", "dv", P.v("dv"))
    P.o("scalar", lambda s: s.activation(qn[0:1, 2:3], qn[0:1, 0:1], AF.Ln))
    P.o("scalar", lambda s: s.activation(qn[0:1, 3:4], qn[0:1, 1:2], AF.Ln), "ac")
    P.w("vector", "ac", P.v("ac"))
    P.o("vector", lambda v: v.tensor_add(qn[0:1, 4:5], qn[0:1, 2:3], qn[0:1, 3:4]), "dv")
    P.w("scalar", "dv", P.v("dv"))
    P.o("scalar", lambda s: s.activation(qn[0:1, 5:6], qn[0:1, 4:5], AF.Exp, scale=0.5), "ac")
    P.w("vector", "ac", P.v("ac"))
    P.o("vector", lambda v: v.tensor_scalar_mul(qn[0:1, 6:7], qn[0:1, 5:6], -0.125), "dv")
    P.w("tensor", "dv", P.v("dv"))
    P.o("tensor", lambda t: t.matmul(pbE[:, 0:1], onesf128[0:1, :], qn[0:1, 6:7], start=True, stop=True), "pe")
    P.w("vector", "pe", P.v("pe"))
    P.o("vector", lambda v: v.tensor_copy(negm[:], pbE[:, 0:1]), "dv")
    DV_NEGM = P.v("dv")
    if upto == "p1b":
        return _finish()
    P.w("gpsimd", "dv", DV_NEGM)
    P.o("gpsimd", lambda g: g.dma_start(out=wvs[:], in_=wv_d.rearrange("(c p) m -> p c m", p=128)), "gw", 16)
    GW_WV = P.v("gw")
    P.w("sync", "dv", DV_NEGM)
    P.o("sync", lambda sy: sy.dma_start(out=bvbc[:], in_=bvbc_d[:]), "ld", 16)
    for tb in range(8):
        it = 32 + tb
        prev = pe_xstr.get((3, tb - 2), 0) if tb < 2 else 0
        P.w("sync", "pe", pe_xstr[("v", tb - 2)] if tb >= 2 else pe_xstr[(3, 6 + tb)])
        P.o("sync", (lambda t0, pr: (lambda sy: sy.dma_start(
            out=xstr[:, pr, :, :], in_=xtr[:, :, t0 * 512:(t0 + 1) * 512])))(tb, it % 2), "ld", 16)
        ldv = P.v("ld")
        for ttl in range(4):
            tts = tb * 4 + ttl
            pv = pbC if tts % 2 == 0 else pbD
            P.w("tensor", "ld", ldv)
            P.w("tensor", "gw", GW_WV)
            P.w("tensor", "dv", P.v("dv"))
            for c8 in range(8):
                P.o("tensor", (lambda cc, tl, pr, pv_: (lambda t: t.matmul(pv_[:],
                     xstr[:, pr, cc, tl * 128:(tl + 1) * 128], wvs[:, cc, :],
                     start=(cc == 0), stop=(cc == 7))))(c8, ttl, it % 2, pv))
            mark = P.m("tensor", "pe")
            if ttl == 3:
                pe_xstr[("v", tb)] = mark
            P.w("vector", "pe", P.v("pe"))
            P.o("vector", (lambda tt_: (lambda v: v.memset(
                vext[:, tt_, :].rearrange("p (h x) -> p h x", x=65)[:, :, 64:65], 1.0)))(tts))
            P.o("vector", (lambda tt_, pv_: (lambda v: v.tensor_add(
                vext[:, tt_, :].rearrange("p (h x) -> p h x", x=65)[:, :, 0:64],
                pv_[:].rearrange("p (h x) -> p h x", x=64),
                bvbc[:].rearrange("p (h x) -> p h x", x=64))))(tts, pv), "dv")
    DV_P1 = P.v("dv")
    if "qkv" in debug:
        dout("qrot", [128, 4 * S], BF16); dout("krot", [128, 4 * S], BF16)
        dout("vext", [128, 32 * 520], BF16); dout("negm", [128, 1])
        P.w("sync", "dv", DV_P1)
        P.o("sync", lambda sy: sy.dma_start(out=dbgout["qrot"][:], in_=qrot[:].rearrange("p a b -> p (a b)")), "sp", 16)
        P.o("sync", lambda sy: sy.dma_start(out=dbgout["krot"][:], in_=krot[:].rearrange("p a b -> p (a b)")), "sp", 16)
        P.o("sync", lambda sy: sy.dma_start(out=dbgout["vext"][:], in_=vext[:].rearrange("p a b -> p (a b)")), "sp", 16)
        P.o("sync", lambda sy: sy.dma_start(out=dbgout["negm"][:], in_=negm[:]), "sp", 16)
    if upto == "p1":
        return _finish()

    SB2 = [pbA, pbB]
    exp_after, msk_after, omm_after = {}, {}, {}
    rbe_after, sc_stage = {}, {}
    gi, nq = 0, 0
    for ph in range(4):
        hA, hB = 2 * ph, 2 * ph + 1
        for qb in range(8):
            K = 4 * (qb + 1)
            qs = slice(qb * 512, (qb + 1) * 512)
            g0 = gi

            def emit_S(kt):
                g = g0 + kt
                par = g % 2
                ks = slice(kt * 128, (kt + 1) * 128)
                if g == 0:
                    P.w("tensor", "dv", DV_P1)
                if kt == 0 and nq > 0:
                    P.w("tensor", "nrm", nq)
                if kt < 2 and nq > 0:
                    P.w("tensor", "dv", rbe_after[nq - 1])
                if g >= 2:
                    P.w("tensor", "exp", exp_after[g - 2])
                P.o("tensor", (lambda p_, k_, q_, pr: (lambda t: t.matmul(SB2[pr][:, 0:512],
                     krot[0:64, p_, k_], qrot[0:64, p_, q_], start=True, stop=True)))(ph, ks, qs, par))
                P.o("tensor", (lambda p_, k_, q_, pr: (lambda t: t.matmul(SB2[pr][:, 512:1024],
                     krot[64:128, p_, k_], qrot[64:128, p_, q_], start=True, stop=True)))(ph, ks, qs, par), "smm")

            def emit_exp(kt):
                g = g0 + kt
                par = g % 2
                P.w("scalar", "smm", P.v("smm"))
                P.w("scalar", "dv", DV_NEGM)
                if g >= 2:
                    P.w("scalar", "omm", omm_after[g - 2])
                P.o("scalar", (lambda pr: (lambda s: s.activation(psb[pr][:], SB2[pr][:, 0:1024],
                     AF.Exp, bias=negm[:, 0:1], scale=0.125)))(par), "exp")
                exp_after[g] = P.v("exp")
                if kt >= K - 4:
                    ob = kt - 4 * qb
                    P.w("vector", "exp", exp_after[g])
                    P.o("vector", (lambda pr, o_: (lambda v: v.tensor_mul(psb[pr][:, 0:512],
                         psb[pr][:, 0:512], cmask[:, o_, :])))(par, ob))
                    P.o("vector", (lambda pr, o_: (lambda v: v.tensor_mul(psb[pr][:, 512:1024],
                         psb[pr][:, 512:1024], cmask[:, o_, :])))(par, ob), "msk")
                    msk_after[g] = P.v("msk")

            def emit_O(kt):
                g = g0 + kt
                par = g % 2
                if kt >= K - 4:
                    P.w("tensor", "msk", msk_after[g])
                else:
                    P.w("tensor", "exp", exp_after[g])
                P.o("tensor", (lambda kt_, pr, st_, sp2, h_: (lambda t: t.matmul(pbC[0:65, :],
                     vext[:, kt_, h_ * 65:h_ * 65 + 65], psb[pr][:, 0:512],
                     start=st_, stop=sp2)))(kt, par, kt == 0, kt == K - 1, hA))
                P.o("tensor", (lambda kt_, pr, st_, sp2, h_: (lambda t: t.matmul(pbD[0:65, :],
                     vext[:, kt_, h_ * 65:h_ * 65 + 65], psb[pr][:, 512:1024],
                     start=st_, stop=sp2)))(kt, par, kt == 0, kt == K - 1, hB), "omm")
                omm_after[g] = P.v("omm")

            emit_S(0)
            emit_exp(0)
            for kt in range(K):
                if kt + 1 < K:
                    emit_S(kt + 1)
                    emit_exp(kt + 1)
                emit_O(kt)
            gi += K
            P.w("vector", "omm", omm_after[gi - 1])
            P.o("vector", lambda v: v.tensor_copy(lrow[64:65, 0:512], pbC[64:65, :]))
            P.o("vector", lambda v: v.tensor_copy(lrow[64:65, 512:1024], pbD[64:65, :]), "lcp")
            P.w("scalar", "lcp", P.v("lcp"))
            P.o("scalar", lambda s: s.activation(lrow2[64:65, :], lrow[64:65, :], AF.Ln))
            P.o("scalar", lambda s: s.activation(lrow[64:65, :], lrow2[64:65, :], AF.Exp, scale=-1.0), "rld")
            P.w("tensor", "rld", P.v("rld"))
            P.w("tensor", "exp", exp_after[gi - 1])
            P.o("tensor", lambda t: t.matmul(pbA[:, 0:512], onesf128[64:65, :],
                 lrow[64:65, 0:512], start=True, stop=True))
            P.o("tensor", lambda t: t.matmul(pbA[:, 512:1024], onesf128[64:65, :],
                 lrow[64:65, 512:1024], start=True, stop=True), "rbc")
            P.w("vector", "rbc", P.v("rbc"))
            P.o("vector", lambda v: v.tensor_copy(rlA[:], pbA[:, 0:512]))
            P.o("vector", lambda v: v.tensor_copy(rlB[:], pbA[:, 512:1024]), "dv")
            rbe_after[nq] = P.v("dv")
            spar = nq % 2
            if nq >= 2:
                P.w("vector", "sc", sc_stage[nq - 2])
            P.o("vector", (lambda sp2: (lambda v: v.tensor_mul(stg[0:64, sp2, 0, :],
                 pbC[0:64, :], rlA[0:64, :])))(spar))
            P.o("vector", (lambda sp2: (lambda v: v.tensor_mul(stg[0:64, sp2, 1, :],
                 pbD[0:64, :], rlB[0:64, :])))(spar), "nrm")
            P.w("scalar", "nrm", P.v("nrm"))
            P.o("scalar", (lambda sp2, h_, q_: (lambda s: s.dma_start(
                out=ag_in.ap()[h_ * 64:(h_ + 1) * 64, q_], in_=stg[0:64, sp2, 0, :])))(spar, hA, qs), "sc", 16)
            P.o("scalar", (lambda sp2, h_, q_: (lambda s: s.dma_start(
                out=ag_in.ap()[h_ * 64:(h_ + 1) * 64, q_], in_=stg[0:64, sp2, 1, :])))(spar, hB, qs), "sc", 16)
            sc_stage[nq] = P.v("sc")
            nq += 1
    SC_ATT = P.v("sc")
    if "attn" in debug:
        dout("attn", [512, S], BF16)
        P.w("sync", "sc", SC_ATT)
        P.o("sync", lambda sy: sy.dma_start(out=dbgout["attn"][:], in_=ag_in.ap()[:]), "sp", 16)
    if upto == "p2":
        return _finish()

    P.w("gpsimd", "omm", omm_after[gi - 1])
    P.w("gpsimd", "sc", SC_ATT)
    P.o("gpsimd", lambda g: g.dma_start(out=wos[:], in_=wo_d.rearrange("(c p) m -> p c m", p=128)), "gw", 16)
    GW_WO = P.v("gw")
    P.w("sync", "omm", omm_after[gi - 1])
    P.o("sync", lambda sy: sy.dma_start(out=bobc[:], in_=bobc_d[:]), "ld", 16)

    P.w("gpsimd", "sc", SC_ATT)
    P.o("gpsimd", lambda g: g.collective_compute("AllGather", ALU.bypass,
         replica_groups=[[0, 1], [2, 3], [4, 5], [6, 7]],
         ins=[ag_in.ap().opt()], outs=[ag_out.ap().opt()]), "cc")
    CC_ATT = P.v("cc")
    g2_agT = {}
    P.w("gpsimd", "ld", LD_CONST)
    for c8 in range(8):
        P.o("gpsimd", (lambda cc: (lambda g: g.iota(agix8f[:, cc:cc + 1],
             pattern=[[0, 1]], base=cc * 128 * S, channel_multiplier=S,
             allow_small_or_imprecise_dtypes=True)))(c8))
    P.o("gpsimd", lambda g: g.tensor_scalar_add(agix8f[:], agix8f[:], hcol[:, 0:1]))
    P.o("gpsimd", lambda g: g.tensor_copy(agix8[:], agix8f[:]), "gp")
    P.w("gpsimd", "cc", CC_ATT)
    for c8 in range(8):
        P.o("gpsimd", (lambda cc: (lambda g: g.indirect_dma_start(
             out=agT[:, cc, :], out_offset=None,
             in_=ag_out.ap(),
             in_offset=IndirectOffsetOnAxis(agix8[:, cc:cc + 1], 1),
             bounds_check=REGS["ag"], oob_is_err=False)))(c8), "g2", 16)
        g2_agT[c8] = P.v("g2")
    G2_AGT = P.v("g2")
    dv_ln2T = {}
    sc_ln2h = {}
    sp_x1h = {}
    for tt in range(16):
        tpar = tt % 2
        P.w("tensor", "g2", G2_AGT)
        P.w("tensor", "gw", GW_WO)
        P.w("tensor", "dv", P.v("dv"))
        for nb2 in range(2):
            for c8 in range(8):
                P.o("tensor", (lambda cc, n_, t_: (lambda t: t.matmul(
                     pbA[:, n_ * 512:(n_ + 1) * 512], agT[:, cc, t_ * 128:(t_ + 1) * 128],
                     wos[:, cc, n_ * 512:(n_ + 1) * 512], start=(cc == 0), stop=(cc == 7))))(c8, nb2, tt))
        P.m("tensor", "pe")
        pey = P.v("pe")
        if tt >= 2:
            P.w("sync", "dv", P.v("dv"))
        P.o("sync", (lambda t_, pr: (lambda sy: sy.dma_start(out=xrw[:, pr, :],
             in_=xrows[t_ * 128:(t_ + 1) * 128, :])))(tt, tpar), "ld", 16)
        P.w("vector", "pe", pey)
        P.w("vector", "ld", P.v("ld"))
        if tt >= 2:
            P.w("vector", "sp", sp_x1h[tt - 2])
        P.o("vector", (lambda pr: (lambda v: v.tensor_add(x1t[:, pr, :], pbA[:, 0:1024], bobc[:])))(tpar))
        P.o("vector", (lambda pr: (lambda v: v.tensor_add(x1t[:, pr, :], x1t[:, pr, :], xrw[:, pr, :])))(tpar), "dv")
        dvx1 = P.v("dv")
        P.w("sync", "dv", dvx1)
        P.o("sync", (lambda t_, pr: (lambda sy: sy.dma_start(
             out=x1_tm.ap()[t_ * 128:(t_ + 1) * 128, :], in_=x1t[:, pr, :])))(tt, tpar), "sp", 16)
        sp_x1h[tt] = P.v("sp")
        P.w("vector", "dv", dvx1)
        P.o("vector", (lambda pr: (lambda v: v.reduce_sum(stt[:, 0:1], x1t[:, pr, :], axis=AX.X)))(tpar), "dv")
        P.w("scalar", "dv", P.v("dv"))
        P.o("scalar", (lambda pr: (lambda s: s.activation(sqf[:], x1t[:, pr, :], AF.Square)))(tpar), "ac")
        P.w("vector", "ac", P.v("ac"))
        P.o("vector", lambda v: v.reduce_sum(stt[:, 1:2], sqf[:], axis=AX.X))
        P.o("vector", lambda v: v.tensor_scalar_mul(stt[:, 2:3], stt[:, 0:1], 1.0 / D))
        P.o("vector", lambda v: v.tensor_scalar_mul(stt[:, 3:4], stt[:, 1:2], 1.0 / D), "dv")
        P.w("scalar", "dv", P.v("dv"))
        P.o("scalar", lambda s: s.activation(stt[:, 4:5], stt[:, 2:3], AF.Square), "ac")
        P.w("vector", "ac", P.v("ac"))
        P.o("vector", lambda v: v.tensor_sub(stt[:, 5:6], stt[:, 3:4], stt[:, 4:5]), "dv")
        P.w("scalar", "dv", P.v("dv"))
        P.o("scalar", lambda s: s.activation(stt[:, 5:6], stt[:, 5:6], AF.Ln, bias=EPS))
        P.o("scalar", lambda s: s.activation(stt[:, 6:7], stt[:, 5:6], AF.Exp, scale=-0.5), "ac")
        P.w("vector", "ac", P.v("ac"))
        if tt >= 2:
            P.w("vector", "sc", sc_ln2h[tt - 2])
        P.o("vector", (lambda pr: (lambda v: v.tensor_scalar_sub(sqf[:], x1t[:, pr, :],
             stt[:, 2:3])))(tpar))
        P.o("vector", (lambda pr: (lambda v: v.tensor_scalar_mul(ln2t[:, pr, :], sqf[:],
             stt[:, 6:7])))(tpar), "dv")
        acl = P.v("dv")
        P.w("scalar", "dv", acl)
        P.o("scalar", (lambda t_, pr: (lambda s: s.dma_start(
             out=ln2_loc.ap()[t_ * 128:(t_ + 1) * 128, :], in_=ln2t[:, pr, :])))(tt, tpar), "sc", 16)
        sc_ln2h[tt] = P.v("sc")
        P.w("gpsimd", "sc", sc_ln2h[tt])
        P.o("gpsimd", (lambda t_: (lambda g: g.collective_compute("AllGather",
             ALU.bypass, replica_groups=[list(range(8))],
             ins=[ln2_loc.ap()[t_ * 128:(t_ + 1) * 128, :].opt()],
             outs=[ln2_all.ap()[t_ * 1024:(t_ + 1) * 1024, :].opt()])))(tt), "cc")
        for c8 in range(8):
            s8 = slice((c8 % 2) * 128, (c8 % 2) * 128 + 128)
            P.w("tensor", "dv", acl)
            P.w("tensor", "dv", P.v("dv"))
            P.o("tensor", (lambda cc, s8_, pr: (lambda t: t.transpose(pbT[:, s8_],
                 ln2t[:, pr, cc * 128:(cc + 1) * 128], ident[:])))(c8, s8, tpar), "pe")
            P.w("vector", "pe", P.v("pe"))
            P.o("vector", (lambda cc, t_, s8_: (lambda v: v.tensor_copy(
                 ln2T[:, cc, t_ * 128:(t_ + 1) * 128], pbT[:, s8_])))(c8, tt, s8), "dv")
        dv_ln2T[tt] = P.v("dv")
    SC_LN2 = P.v("sc")
    SP_X1 = P.v("sp")
    if "x1" in debug:
        dout("x1", [HTOK, D])
        P.w("sync", "sp", SP_X1)
        P.o("sync", lambda sy: sy.dma_start(out=dbgout["x1"][:], in_=x1_tm.ap()[:]), "sp", 16)
    if "ln2" in debug:
        dout("ln2", [HTOK, D], BF16)
        P.w("sync", "sc", SC_LN2)
        P.o("sync", lambda sy: sy.dma_start(out=dbgout["ln2"][:], in_=ln2_loc.ap()[:]), "sp", 16)
    if upto == "p3":
        return _finish()

    CC_LN2 = P.v("cc")

    for tt in range(16):
        P.w("tensor", "dv", dv_ln2T[tt])
        P.w("tensor", "gw", GW_WG)
        P.w("tensor", "dv", P.v("dv"))
        for c8 in range(8):
            P.o("tensor", (lambda cc, t_: (lambda t: t.matmul(pbC[:, 0:8],
                 ln2T[:, cc, t_ * 128:(t_ + 1) * 128], wgs[:, cc, 0:8],
                 start=(cc == 0), stop=(cc == 7))))(c8, tt), "pe" if c8 == 7 else None)
        P.w("vector", "pe", P.v("pe"))
        P.w("vector", "dv", DV_INIT)
        P.o("vector", (lambda t_: (lambda v: v.tensor_add(rt0[:, t_, :], pbC[:, 0:8], bgeps[:])))(tt), "dv")
    for tt in range(16):
        P.o("vector", (lambda t_: (lambda v: v.reduce_max(l1a[:, t_:t_ + 1], rt0[:, t_, :], axis=AX.X)))(tt))
    P.o("vector", lambda v: v.engine_nop(), "dv")
    P.w("scalar", "dv", P.v("dv"))
    P.o("scalar", lambda s: s.copy(l1b[:], l1a[:]), "ac")
    P.w("vector", "ac", P.v("ac"))
    for tt in range(16):
        P.o("vector", (lambda t_: (lambda v: v.tensor_single_scalar(eq1a[:, t_, :], rt0[:, t_, :],
             l1b[:, t_:t_ + 1], ALU.is_equal)))(tt))
    for tt in range(16):
        P.o("vector", (lambda t_: (lambda v: v.scalar_tensor_tensor(msk2[:, t_, :], eq1a[:, t_, :],
             -1e30, rt0[:, t_, :], ALU.mult, ALU.add)))(tt))
    for tt in range(16):
        P.o("vector", (lambda t_: (lambda v: v.reduce_max(l2a[:, t_:t_ + 1], msk2[:, t_, :], axis=AX.X)))(tt))
    P.o("vector", lambda v: v.engine_nop(), "dv")
    P.w("scalar", "dv", P.v("dv"))
    P.o("scalar", lambda s: s.copy(l2b[:], l2a[:]), "ac")
    P.w("vector", "ac", P.v("ac"))
    for tt in range(16):
        P.o("vector", (lambda t_: (lambda v: v.tensor_single_scalar(eq2a[:, t_, :], msk2[:, t_, :],
             l2b[:, t_:t_ + 1], ALU.is_equal)))(tt))
    P.o("vector", lambda v: v.tensor_sub(dal[:], l2a[:], l1a[:]), "dv")
    P.w("scalar", "dv", P.v("dv"))
    P.o("scalar", lambda s: s.activation(g1s[:], dal[:], AF.Sigmoid, scale=-1.0))
    P.o("scalar", lambda s: s.activation(g2s[:], dal[:], AF.Sigmoid), "ac")
    for tt in range(16):
        P.o("vector", (lambda t_: (lambda v: v.tensor_mul(rt0[:, t_, :], eq1a[:, t_, :], iot8[:])))(tt))
    for tt in range(16):
        P.o("vector", (lambda t_: (lambda v: v.reduce_sum(e1s[:, t_:t_ + 1], rt0[:, t_, :], axis=AX.X)))(tt))
    for tt in range(16):
        P.o("vector", (lambda t_: (lambda v: v.tensor_mul(rt0[:, t_, :], eq2a[:, t_, :], iot8[:])))(tt))
    for tt in range(16):
        P.o("vector", (lambda t_: (lambda v: v.reduce_sum(e2s[:, t_:t_ + 1], rt0[:, t_, :], axis=AX.X)))(tt))
    P.o("vector", lambda v: v.tensor_copy(ohb[:, 0, :, :], eq1a[:]))
    P.o("vector", lambda v: v.tensor_copy(ohb[:, 1, :, :], eq2a[:]), "dv")
    DV_OH = P.v("dv")
    pv_evac = {}
    slot_i = 0
    for k2 in range(2):
        for tt in range(16):
            pbank = pbD if slot_i % 2 == 0 else pbC
            sl8 = slice(((slot_i // 2) % 8) * 8, ((slot_i // 2) % 8) * 8 + 8)
            P.w("tensor", "dv", DV_OH)
            if slot_i >= 2:
                P.w("tensor", "dv", pv_evac[slot_i - 2])
            first = True
            for (kk, tt2) in [(0, t2) for t2 in range(16)] + [(1, t2) for t2 in range(16)]:
                if (kk, tt2) >= (k2, tt):
                    break
                P.o("tensor", (lambda kk_, t2_, s8_, st_, pb_: (lambda t: t.matmul(pb_[:, s8_],
                     onesb128[:], ohb[:, kk_, t2_, :], start=st_, stop=False)))(kk, tt2, sl8, first, pbank))
                first = False
            P.o("tensor", (lambda k_, t_, s8_, st_, pb_: (lambda t: t.matmul(pb_[:, s8_],
                 ltri[:], ohb[:, k_, t_, :], start=st_, stop=True)))(k2, tt, sl8, first, pbank), "pe")
            P.w("vector", "pe", P.v("pe"))
            P.o("vector", (lambda k_, t_, s8_, pb_: (lambda v: v.tensor_copy(posall[:, k_, t_, :], pb_[:, s8_])))(k2, tt, sl8, pbank), "dv")
            pv_evac[slot_i] = P.v("dv")
            slot_i += 1
    for k2 in range(2):
        for tt in range(16):
            P.w("tensor", "dv", DV_OH)
            P.o("tensor", (lambda k_, t_, st_: (lambda t: t.matmul(pbE[:, k_ * 8:k_ * 8 + 8],
                 onesb128[:], ohb[:, k_, t_, :], start=st_, stop=(t_ == 15))))(k2, tt, tt == 0),
                "pe" if tt == 15 else None)
    P.w("vector", "pe", P.v("pe"))
    P.o("vector", lambda v: v.tensor_copy(cntrow[0:1, :], pbE[0:1, 0:16]), "dv")
    DV_CNT = P.v("dv")
    P.o("vector", lambda v: v.tensor_mul(msk2[:], posall[:, 0, :, :], eq1a[:]))
    P.o("vector", lambda v: v.tensor_mul(rt0[:], posall[:, 1, :, :], eq2a[:]), "dv")
    for tt in range(16):
        P.o("vector", (lambda t_: (lambda v: v.reduce_sum(poss1[:, t_:t_ + 1], msk2[:, t_, :], axis=AX.X)))(tt))
    for tt in range(16):
        P.o("vector", (lambda t_: (lambda v: v.reduce_sum(poss2[:, t_:t_ + 1], rt0[:, t_, :], axis=AX.X)))(tt))
    P.o("vector", lambda v: v.engine_nop(), "dv")
    DV_ROUTE = P.v("dv")
    P.w("scalar", "dv", DV_CNT)
    P.o("scalar", lambda s: s.dma_start(out=cnt_loc.ap()[0:1, :], in_=cntrow[0:1, 0:8]), "sc", 16)
    P.o("scalar", lambda s: s.dma_start(out=cnt_loc.ap()[1:2, :], in_=cntrow[0:1, 8:16]), "sc", 16)
    P.w("gpsimd", "sc", P.v("sc"))
    P.o("gpsimd", lambda g: g.collective_compute("AllGather", ALU.bypass,
         replica_groups=[list(range(8))],
         ins=[cnt_loc.ap().opt()], outs=[cnt_all.ap().opt()]), "cc")
    P.w("sync", "cc", P.v("cc"))
    P.o("sync", lambda sy: sy.dma_start(out=cntag[:], in_=cnt_all.ap()[:]), "ld", 16)
    P.w("tensor", "ld", P.v("ld"))
    P.o("tensor", lambda t: t.matmul(pbE[0:1, 0:8], selmat[:, 0:1], cntag[:], start=True, stop=True))
    P.o("tensor", lambda t: t.matmul(pbE[0:1, 8:16], selmat[:, 1:2], cntag[:], start=True, stop=True), "pe")
    P.w("vector", "pe", P.v("pe"))
    P.o("vector", lambda v: v.tensor_copy(offrow[0:1, :], pbE[0:1, 0:16]), "dv")
    P.w("tensor", "dv", P.v("dv"))
    P.o("tensor", lambda t: t.matmul(pbE[:, 0:8], onesf128[0:1, :], offrow[0:1, 0:8], start=True, stop=True))
    P.o("tensor", lambda t: t.matmul(pbE[:, 8:16], onesf128[0:1, :], offrow[0:1, 8:16], start=True, stop=True), "pe")
    P.w("vector", "pe", P.v("pe"))
    P.o("vector", lambda v: v.tensor_copy(offb[:, 0, :], pbE[:, 0:8]))
    P.o("vector", lambda v: v.tensor_copy(offb[:, 1, :], pbE[:, 8:16]), "dv")
    DV_OFFB = P.v("dv")
    P.w("vector", "dv", DV_OFFB)
    for tt in range(16):
        P.o("vector", (lambda t_: (lambda v: v.tensor_mul(msk2[:, t_, :], eq1a[:, t_, :], offb[:, 0, :])))(tt))
    for tt in range(16):
        P.o("vector", (lambda t_: (lambda v: v.tensor_mul(rt0[:, t_, :], eq2a[:, t_, :], offb[:, 1, :])))(tt))
    for tt in range(16):
        P.o("vector", (lambda t_: (lambda v: v.reduce_sum(offsl1[:, t_:t_ + 1], msk2[:, t_, :], axis=AX.X)))(tt))
    for tt in range(16):
        P.o("vector", (lambda t_: (lambda v: v.reduce_sum(offsl2[:, t_:t_ + 1], rt0[:, t_, :], axis=AX.X)))(tt))
    def bounce():
        P.o("vector", lambda v: v.engine_nop(), "dv")
        P.w("scalar", "dv", P.v("dv"))
        P.o("scalar", lambda s: s.copy(cntrow[0:1, 0:1], cntrow[0:1, 0:1]), "ac")
        P.w("vector", "ac", P.v("ac"))
    P.o("vector", lambda v: v.tensor_add(poss1[:], poss1[:], offsl1[:]))
    P.o("vector", lambda v: v.tensor_add(poss2[:], poss2[:], offsl2[:]))
    bounce()
    P.o("vector", lambda v: v.tensor_single_scalar(keep1[:], poss1[:], float(CAP), ALU.is_lt))
    P.o("vector", lambda v: v.tensor_single_scalar(keep2[:], poss2[:], float(CAP), ALU.is_lt))
    P.o("vector", lambda v: v.tensor_scalar_mul(offsl1[:], e1s[:], float(CAPP)))
    P.o("vector", lambda v: v.tensor_scalar_mul(offsl2[:], e2s[:], float(CAPP)))
    bounce()
    P.o("vector", lambda v: v.tensor_scalar(l1a[:], keep1[:], -1e6, 1e6, ALU.mult, ALU.add))
    P.o("vector", lambda v: v.tensor_scalar(l2a[:], keep2[:], -1e6, 1e6, ALU.mult, ALU.add))
    P.o("vector", lambda v: v.tensor_add(offsl1[:], offsl1[:], poss1[:]))
    P.o("vector", lambda v: v.tensor_add(offsl2[:], offsl2[:], poss2[:]))
    bounce()
    P.o("vector", lambda v: v.tensor_add(offsl1[:], offsl1[:], l1a[:]))
    P.o("vector", lambda v: v.tensor_add(offsl2[:], offsl2[:], l2a[:]))
    P.o("vector", lambda v: v.tensor_mul(g1s[:], g1s[:], keep1[:]))
    P.o("vector", lambda v: v.tensor_mul(g2s[:], g2s[:], keep2[:]))
    bounce()
    P.o("vector", lambda v: v.tensor_copy(idx1s[:], offsl1[:]))
    P.o("vector", lambda v: v.tensor_copy(idx2s[:], offsl2[:]))
    P.o("vector", lambda v: v.tensor_copy(y1i[:], poss1[:]))
    P.o("vector", lambda v: v.tensor_copy(y2i[:], poss2[:]))
    P.o("vector", lambda v: v.tensor_copy(t1i[:], e1s[:]))
    P.o("vector", lambda v: v.tensor_copy(t2i[:], e2s[:]))
    bounce()
    P.o("vector", lambda v: v.tensor_single_scalar(m1i[:], y1i[:], 511, ALU.bitwise_and))
    P.o("vector", lambda v: v.tensor_single_scalar(m2i[:], y2i[:], 511, ALU.bitwise_and))
    P.o("vector", lambda v: v.tensor_single_scalar(idx1y[:], y1i[:], 8, ALU.mult))
    P.o("vector", lambda v: v.tensor_single_scalar(idx2y[:], y2i[:], 8, ALU.mult))
    P.o("vector", lambda v: v.tensor_single_scalar(t1i[:], t1i[:], 512, ALU.mult))
    P.o("vector", lambda v: v.tensor_single_scalar(t2i[:], t2i[:], 512, ALU.mult))
    bounce()
    P.o("vector", lambda v: v.tensor_single_scalar(m1i[:], m1i[:], 7, ALU.mult))
    P.o("vector", lambda v: v.tensor_single_scalar(m2i[:], m2i[:], 7, ALU.mult))
    P.o("vector", lambda v: v.tensor_add(idx1y[:], idx1y[:], t1i[:]))
    P.o("vector", lambda v: v.tensor_add(idx2y[:], idx2y[:], t2i[:]))
    P.o("vector", lambda v: v.tensor_copy(t1i[:], l1a[:]))
    P.o("vector", lambda v: v.tensor_copy(t2i[:], l2a[:]))
    bounce()
    P.o("vector", lambda v: v.tensor_sub(idx1y[:], idx1y[:], m1i[:]))
    P.o("vector", lambda v: v.tensor_sub(idx2y[:], idx2y[:], m2i[:]))
    bounce()
    P.o("vector", lambda v: v.tensor_add(idx1y[:], idx1y[:], t1i[:]))
    P.o("vector", lambda v: v.tensor_add(idx2y[:], idx2y[:], t2i[:]), "dv")
    xbounce()
    DV_IDX = P.v("dv")
    if upto == "route":
        return _finish()
    w1r = w1_d.rearrange("(c p) f -> p c f", p=128)
    w2r = w2_d.rearrange("(c p) m -> p c m", p=128)
    gw_w1 = []
    P.w("gpsimd", "dv", dv_ln2T[15])
    P.w("gpsimd", "pe", P.v("pe"))
    P.w("gpsimd", "sp", P.v("sp"))
    P.w("gpsimd", "sc", P.v("sc"))
    for q4 in range(4):
        P.o("gpsimd", (lambda q_: (lambda g: g.dma_start(out=w1s[:, :, q_ * 1024:(q_ + 1) * 1024],
             in_=w1r[:, :, q_ * 1024:(q_ + 1) * 1024])))(q4), "gw", 16)
        gw_w1.append(P.v("gw"))
    gw_w2 = []
    for q4 in range(4):
        P.o("gpsimd", (lambda q_: (lambda g: g.dma_start(out=w2s[:, q_ * 8:(q_ + 1) * 8, :],
             in_=w2r[:, q_ * 8:(q_ + 1) * 8, :])))(q4), "gw", 16)
        gw_w2.append(P.v("gw"))
    slot_loc_v = slot_loc.ap().rearrange("(p c) u -> p (c u)", p=128)
    P.w("gpsimd", "dv", DV_INIT)
    P.o("gpsimd", lambda g: g.dma_start(out=slot_loc_v, in_=zero1[:]), "g2", 16)
    G2_Z = P.v("g2")
    P.w("gpsimd", "dv", DV_IDX)
    P.o("gpsimd", lambda g: g.tensor_copy(idx1g[:], idx1s[:]))
    P.o("gpsimd", lambda g: g.tensor_copy(idx2g[:], idx2s[:]), "gp")
    for tt in range(16):
        for k2 in range(2):
            ix = idx1g if k2 == 0 else idx2g
            P.w("gpsimd", "g2", G2_Z)
            P.o("gpsimd", (lambda t_, i_: (lambda g: g.indirect_dma_start(
                 out=slot_loc.ap(),
                 out_offset=IndirectOffsetOnAxis(i_[:, t_:t_ + 1], 0),
                 in_=toks[:, t_:t_ + 1], in_offset=None,
                 bounds_check=REGS["slot"], oob_is_err=False)))(tt, ix), "g2", 16)
    if "sloc" in debug:
        dout("sloc", [E * CAPP, 1])
        P.w("sync", "g2", P.v("g2"))
        P.o("sync", lambda sy: sy.dma_start(out=dbgout["sloc"][:], in_=slot_loc.ap()[:]), "sp", 16)
    P.w("gpsimd", "g2", P.v("g2"))
    P.o("gpsimd", lambda g: g.collective_compute("AllGather", ALU.bypass,
         replica_groups=[list(range(8))],
         ins=[slot_loc.ap().opt()], outs=[slot_ag.ap().opt()]), "cc")
    CC_SLOT = P.v("cc")
    P.o("gpsimd", lambda g: g.tensor_copy(idx1g[:], idx1y[:]))
    P.o("gpsimd", lambda g: g.tensor_copy(idx2g[:], idx2y[:]), "gp")
    P.w("gpsimd", "cc", CC_SLOT)
    P.o("gpsimd", lambda g: g.collective_compute("AllGather", ALU.bypass,
         replica_groups=[list(range(8))],
         ins=[cnt_loc.ap().opt()], outs=[bar_o.ap().opt()]), "cc")
    CC_B2 = P.v("cc")
    if "sag" in debug:
        dout("sag1", [NC * E * CAPP, 1])
        P.w("sync", "cc", CC_B2)
        P.w("sync", "gw", gw_w2[3])
        P.o("sync", lambda sy: sy.dma_start(out=dbgout["sag1"][:], in_=slot_ag.ap()[:]), "sp", 16)
    P.w("gpsimd", "ld", LD_CONST)
    for j in range(8):
        P.o("gpsimd", (lambda j_: (lambda g: g.iota(agix8f[:, j_:j_ + 1],
             pattern=[[0, 1]], base=j_ * E * CAPP, channel_multiplier=128,
             allow_small_or_imprecise_dtypes=True)))(j))
    P.o("gpsimd", lambda g: g.tensor_scalar_add(agix8f[:], agix8f[:], msl0[:, 0:1]))
    P.o("gpsimd", lambda g: g.tensor_copy(agix8[:], agix8f[:]), "gp")
    for j in range(8):
        P.w("gpsimd", "dv", P.v("dv"))
        P.w("gpsimd", "cc", CC_B2)
        P.w("gpsimd", "gw", gw_w2[3])
        P.o("gpsimd", (lambda j_, pr: (lambda g: g.indirect_dma_start(
             out=st39[:, pr, :], out_offset=None, in_=slot_ag.ap(),
             in_offset=IndirectOffsetOnAxis(agix8[0:39, j_:j_ + 1], 0),
             bounds_check=REGS["sag"], oob_is_err=False)))(j, j % 2), "g2", 16)
        P.w("vector", "g2", P.v("g2"))
        P.w("vector", "dv", DV_INIT)
        P.o("vector", (lambda pr: (lambda v: v.tensor_add(sum39[:], sum39[:], st39[:, pr, :])))(j % 2), "dv")
    P.w("tensor", "dv", P.v("dv"))
    P.o("tensor", lambda t: t.transpose(pbE[:, 0:39], sum39[:], identf[0:39, 0:39]), "pe")
    P.w("vector", "pe", P.v("pe"))
    P.o("vector", lambda v: v.tensor_copy(slotm[:], pbE[:, 0:39]), "dv")
    xbounce()
    P.o("vector", lambda v: v.tensor_copy(i39a[:], slotm[:]), "dv")
    xbounce()
    P.o("vector", lambda v: v.tensor_single_scalar(i39b[:], i39a[:], 11, ALU.arith_shift_right))
    P.o("vector", lambda v: v.tensor_single_scalar(i39c[:], i39a[:], 127, ALU.bitwise_and))
    P.o("vector", lambda v: v.tensor_single_scalar(slotY[:], i39a[:], 8, ALU.mult))
    bounce()
    P.o("vector", lambda v: v.tensor_single_scalar(i39b[:], i39b[:], 16256, ALU.mult))
    P.o("vector", lambda v: v.tensor_single_scalar(i39c[:], i39c[:], 7, ALU.mult))
    bounce()
    P.o("vector", lambda v: v.tensor_sub(slotY[:], slotY[:], i39b[:]))
    bounce()
    P.o("vector", lambda v: v.tensor_sub(slotY[:], slotY[:], i39c[:]), "dv")
    xbounce()
    DV_SLOT = P.v("dv")
    P.w("gpsimd", "dv", DV_SLOT)
    P.o("gpsimd", lambda g: g.tensor_copy(slotmi[:], slotY[:]), "gp")
    if "route2" in debug:
        dout("oh1d", [128, 128]); dout("e1d", [128, 16])
        dout("posd", [128, 256]); dout("cntd", [1, 16])
        dout("offbd", [128, 16]); dout("cntagd", [16, 8])
        P.w("sync", "dv", DV_SLOT)
        P.o("sync", lambda sy: sy.dma_start(out=dbgout["oh1d"][:], in_=eq1a[:].rearrange("p a b -> p (a b)")), "sp", 16)
        P.o("sync", lambda sy: sy.dma_start(out=dbgout["e1d"][:], in_=e1s[:]), "sp", 16)
        P.o("sync", lambda sy: sy.dma_start(out=dbgout["posd"][:], in_=posall[:].rearrange("p a b c -> p (a b c)")), "sp", 16)
        P.o("sync", lambda sy: sy.dma_start(out=dbgout["cntd"][:], in_=cntrow[:]), "sp", 16)
        P.o("sync", lambda sy: sy.dma_start(out=dbgout["offbd"][:], in_=offb[:].rearrange("p a b -> p (a b)")), "sp", 16)
        P.o("sync", lambda sy: sy.dma_start(out=dbgout["cntagd"][:], in_=cntag[:]), "sp", 16)
    if "route" in debug:
        dout("idx1", [128, 16], I32); dout("idx2", [128, 16], I32); dout("g1", [128, 16])
        dout("g2v", [128, 16]); dout("slotm", [128, 39]); dout("poss1", [128, 16])
        P.w("sync", "dv", DV_SLOT)
        P.o("sync", lambda sy: sy.dma_start(out=dbgout["idx1"][:], in_=idx1s[:]), "sp", 16)
        P.o("sync", lambda sy: sy.dma_start(out=dbgout["idx2"][:], in_=idx2s[:]), "sp", 16)
        P.o("sync", lambda sy: sy.dma_start(out=dbgout["g1"][:], in_=g1s[:]), "sp", 16)
        P.o("sync", lambda sy: sy.dma_start(out=dbgout["g2v"][:], in_=g2s[:]), "sp", 16)
        P.o("sync", lambda sy: sy.dma_start(out=dbgout["slotm"][:], in_=slotm[:]), "sp", 16)
        P.o("sync", lambda sy: sy.dma_start(out=dbgout["poss1"][:], in_=poss1[:]), "sp", 16)
        dout("slotmi", [128, 39], I32)
        P.o("sync", lambda sy: sy.dma_start(out=dbgout["slotmi"][:], in_=slotmi[:]), "sp", 16)
    if upto == "slot":
        return _finish()

    g2_gath, pe_stage, ach, dv_ye, sc_ye = {}, {}, {}, {}, {}
    HSL = [pbA[:, 0:512], pbA[:, 512:1024], pbB[:, 0:512], pbB[:, 512:1024]]
    if "ln2a" in debug:
        dout("ln2a", [NC * HTOK, D], BF16)
        P.w("sync", "cc", CC_B2)
        P.o("sync", lambda sy: sy.dma_start(out=dbgout["ln2a"][:], in_=ln2_all.ap()[:]), "sp", 16)
    def ye_chunk_trigger(k):
        P.w("gpsimd", "sc", sc_ye[min(4 * k + 3, 38)])
        P.o("gpsimd", (lambda k_: (lambda g: g.collective_compute("AllGather",
             ALU.bypass, replica_groups=[list(range(8))],
             ins=[ye_loc.ap()[k_ * 512:(k_ + 1) * 512, :].opt()],
             outs=[ye_all.ap()[k_ * 4096:(k_ + 1) * 4096, :].opt()])))(k), "cc")
    for rb in range(10):
        sts = RB_ST[rb]
        if rb >= 2:
            ye_chunk_trigger(rb - 2)
        for st in sts:
            j = st - rb * 4
            P.w("gpsimd", "dv", DV_SLOT)
            P.w("gpsimd", "cc", CC_B2)
            if st >= 2:
                P.w("gpsimd", "pe", pe_stage[st - 2])
            P.o("gpsimd", (lambda s_, pr: (lambda g: g.indirect_dma_start(
                 out=stage[:, pr, :], out_offset=None,
                 in_=ln2_all.ap(),
                 in_offset=IndirectOffsetOnAxis(slotmi[:, s_:s_ + 1], 0),
                 bounds_check=REGS["ln2"], oob_is_err=False)))(st, st % 2), "g2", 16)
            g2_gath[st] = P.v("g2")
            for c8 in range(8):
                s8 = slice(((st * 8 + c8) % 2) * 128, ((st * 8 + c8) % 2) * 128 + 128)
                P.w("tensor", "g2", g2_gath[st])
                P.w("tensor", "dv", P.v("dv"))
                P.o("tensor", (lambda pr, cc, s8_: (lambda t: t.transpose(pbT[:, s8_],
                     stage[:, pr, cc * 128:(cc + 1) * 128], ident[:])))(st % 2, c8, s8), "pe")
                if c8 == 7:
                    pe_stage[st] = P.v("pe")
                P.w("vector", "pe", P.v("pe"))
                P.o("vector", (lambda cc, j_, s8_: (lambda v: v.tensor_copy(
                     bufT[:, cc, j_ * 128:(j_ + 1) * 128], pbT[:, s8_])))(c8, j, s8), "dv")
        dv_bufT = P.v("dv")
        xbounce()
        AC_BUF = P.v("ac")
        for fc in range(32):
            P.w("tensor", "dv", dv_bufT)
            P.w("tensor", "ac", AC_BUF)
            P.w("tensor", "gw", gw_w1[fc // 8])
            if fc >= 4:
                P.w("tensor", "ac", ach[(rb, fc - 4)])
            for c8 in range(8):
                P.o("tensor", (lambda cc, f_, sl_: (lambda t: t.matmul(sl_,
                     w1s[:, cc, f_ * 128:(f_ + 1) * 128], bufT[:, cc, :],
                     start=(cc == 0), stop=(cc == 7))))(c8, fc, HSL[fc % 4]))
            P.m("tensor", "pe")
            P.w("scalar", "pe", P.v("pe"))
            P.o("scalar", (lambda f_, sl_: (lambda s: s.activation(hsb[:, f_, :], sl_,
                 AF.Gelu_apprx_tanh, bias=b1c[:, f_:f_ + 1])))(fc, HSL[fc % 4]), "ac")
            ach[(rb, fc)] = P.v("ac")
        P.w("scalar", "ac", ach[(rb, 31)])
        P.o("scalar", lambda s: s.copy(cntrow[0:1, 0:1], cntrow[0:1, 0:1]), "ac")
        AC_H = P.v("ac")
        for j in range(len(sts)):
            rt = rb * 4 + j
            ypar = rt % 2
            P.w("tensor", "ac", AC_H)
            P.w("tensor", "gw", gw_w2[3])
            if rt >= 1:
                P.w("tensor", "dv", dv_ye[rt - 1])
            for nb2 in range(2):
                pv = pbC if nb2 == 0 else pbD
                for fc in range(32):
                    P.o("tensor", (lambda f_, j_, n_, pv_: (lambda t: t.matmul(pv_[:],
                         hsb[:, f_, j_ * 128:(j_ + 1) * 128], w2s[:, f_, n_ * 512:(n_ + 1) * 512],
                         start=(f_ == 0), stop=(f_ == 31))))(fc, j, nb2, pv))
            P.m("tensor", "pe")
            P.w("vector", "pe", P.v("pe"))
            if rt >= 2:
                P.w("vector", "sc", sc_ye[rt - 2])
            P.o("vector", (lambda pr: (lambda v: v.tensor_add(yet[:, pr, 0:512], pbC[:], b2bc[:, 0:512])))(ypar))
            P.o("vector", (lambda pr: (lambda v: v.tensor_add(yet[:, pr, 512:1024], pbD[:], b2bc[:, 512:1024])))(ypar), "dv")
            dv_ye[rt] = P.v("dv")
            P.w("scalar", "dv", dv_ye[rt])
            P.o("scalar", (lambda r_, pr: (lambda s: s.dma_start(
                 out=ye_loc.ap()[r_ * 128:(r_ + 1) * 128, :], in_=yet[:, pr, :])))(rt, ypar), "sc", 16)
            sc_ye[rt] = P.v("sc")
    SC_YE = P.v("sc")
    if "ye" in debug:
        dout("ye", [CAPP, D], BF16)
        P.w("sync", "sc", SC_YE)
        P.o("sync", lambda sy: sy.dma_start(out=dbgout["ye"][:], in_=ye_loc.ap()[0:CAPP, :]), "sp", 16)
    if upto == "p4":
        return _finish()
    ye_chunk_trigger(8)
    ye_chunk_trigger(9)
    CC_YE = P.v("cc")
    P.w("gpsimd", "cc", CC_YE)
    P.o("gpsimd", lambda g: g.collective_compute("AllGather", ALU.bypass,
         replica_groups=[list(range(8))],
         ins=[cnt_loc.ap().opt()], outs=[bar_o.ap().opt()]), "cc")
    CC_B3 = P.v("cc")

    sp_outh = {}
    P.w("vector", "cc", CC_B3)
    P.o("vector", lambda v: v.memset(gb1[:], 0.0))
    P.o("vector", lambda v: v.memset(gb2[:], 0.0), "dv")
    xbounce()
    GB_Z = P.v("dv")
    dv_comb = {}
    for tt in range(16):
        tpar = tt % 2
        if tt >= 2:
            P.w("gpsimd", "dv", dv_comb[tt - 2])
        P.w("gpsimd", "dv", GB_Z)
        P.w("gpsimd", "cc", CC_B3)
        P.o("gpsimd", (lambda t_, pr: (lambda g: g.indirect_dma_start(
             out=gb1[:, pr, :], out_offset=None, in_=ye_all.ap(),
             in_offset=IndirectOffsetOnAxis(idx1g[:, t_:t_ + 1], 0),
             bounds_check=REGS["ye"], oob_is_err=False)))(tt, tpar), "g2", 16)
        P.o("gpsimd", (lambda t_, pr: (lambda g: g.indirect_dma_start(
             out=gb2[:, pr, :], out_offset=None, in_=ye_all.ap(),
             in_offset=IndirectOffsetOnAxis(idx2g[:, t_:t_ + 1], 0),
             bounds_check=REGS["ye"], oob_is_err=False)))(tt, tpar), "g2", 16)
        g2v = P.v("g2")
        if tt < 2:
            P.w("sync", "cc", CC_B3)
        P.o("sync", (lambda t_, pr: (lambda sy: sy.dma_start(out=x1r[:, pr, :],
             in_=x1_tm.ap()[t_ * 128:(t_ + 1) * 128, :])))(tt, tpar), "ld", 16)
        P.w("vector", "g2", g2v)
        P.w("vector", "ld", P.v("ld"))
        if tt >= 2:
            P.w("vector", "sp", sp_outh[tt - 2])
        P.o("vector", (lambda t_, pr: (lambda v: v.tensor_scalar_mul(zsum[:], gb1[:, pr, :],
             g1s[:, t_:t_ + 1])))(tt, tpar))
        P.o("vector", (lambda t_, pr: (lambda v: v.scalar_tensor_tensor(outt[:, pr, :],
             gb2[:, pr, :], g2s[:, t_:t_ + 1], zsum[:], ALU.mult, ALU.add)))(tt, tpar))
        P.o("vector", (lambda pr: (lambda v: v.tensor_add(outt[:, pr, :], outt[:, pr, :],
             x1r[:, pr, :])))(tpar), "dv")
        dv_comb[tt] = P.v("dv")
        P.w("sync", "dv", P.v("dv"))
        P.o("sync", (lambda t_, pr: (lambda sy: sy.dma_start(
             out=y_out[t_ * 128:(t_ + 1) * 128, :], in_=outt[:, pr, :])))(tt, tpar), "sp", 16)
        sp_outh[tt] = P.v("sp")
    if "sag" in debug:
        dout("sag2", [NC * E * CAPP, 1])
        P.w("sync", "sp", P.v("sp"))
        P.w("sync", "dv", P.v("dv"))
        P.o("sync", lambda sy: sy.dma_start(out=dbgout["sag2"][:], in_=slot_ag.ap()[:]), "sp", 16)
    P.w("sync", "sp", P.v("sp"))
    P.w("sync", "sc", P.v("sc"))
    P.w("sync", "g2", P.v("g2"))
    P.w("sync", "cc", P.v("cc"))
    P.w("sync", "gw", P.v("gw"))

    return _finish()


_NC_CACHE = {}


def kernel(**inputs):
    if "main" not in _NC_CACHE:
        _NC_CACHE["main"] = build()[0]
    nc = _NC_CACHE["main"]
    consts = _host_constants()
    perm = _perm_cols()
    in_maps = [_host_inputs_for_core(c, inputs, consts, perm) for c in range(NC)]
    res = run_bass_kernel_spmd(nc, in_maps, core_ids=list(range(NC)))
    out = np.empty((4, S, D), np.float32)
    for c in range(NC):
        b, hg = c // 2, c % 2
        out[b, hg * HTOK:(hg + 1) * HTOK, :] = res.results[c]["y_out"]
    return out


# version 05
# speedup vs baseline: 1.2093x; 1.0014x over previous
import numpy as np
import ml_dtypes
from contextlib import ExitStack

import concourse.bass as bass
import concourse.mybir as mybir
from concourse.bass_utils import run_bass_kernel_spmd
from concourse.bass_core_rust import IndirectOffsetOnAxis

F32 = mybir.dt.float32
BF16 = mybir.dt.bfloat16
I32 = mybir.dt.int32
AF = mybir.ActivationFunctionType
ALU = mybir.AluOpType
AX = mybir.AxisListType
BF = ml_dtypes.bfloat16

D, S, HTOK, FF = 1024, 4096, 2048, 4096
E, CAP, CAPP = 8, 4916, 4992
NC = 8
EPS = 1e-5
NST = CAPP // 128
RB_ST = [list(range(rb * 4, min(rb * 4 + 4, NST))) for rb in range(10)]


class Plan:
    ENG = ("sync", "scalar", "vector", "tensor", "gpsimd")

    def __init__(self):
        self.ops = {e: [] for e in self.ENG}
        self.cnt = {}

    def w(self, eng, sem, val):
        if val > 0:
            self.ops[eng].append(("w", sem, val))

    def o(self, eng, fn, inc=None, k=1):
        incs = [(inc, k)] if inc is not None else []
        self.ops[eng].append(["o", fn, incs])
        if inc is not None:
            self.cnt[inc] = self.cnt.get(inc, 0) + k
        return self.cnt.get(inc, 0)

    def m(self, eng, inc, k=1):
        for entry in reversed(self.ops[eng]):
            if entry[0] == "o":
                entry[2].append((inc, k))
                break
        else:
            raise RuntimeError("no op to mark")
        self.cnt[inc] = self.cnt.get(inc, 0) + k
        return self.cnt[inc]

    def v(self, sem):
        return self.cnt.get(sem, 0)


def _host_constants():
    inv_freq = 1.0 / 10000.0 ** (2.0 * np.arange(0, 32, dtype=np.float64) / 64.0)
    ang = np.arange(S, dtype=np.float64)[None, :] * inv_freq[:, None]
    c32, s32 = np.cos(ang), np.sin(ang)
    costab = np.empty((128, S), np.float32)
    sintab = np.empty((128, S), np.float32)
    for d in range(128):
        j = d % 64
        costab[d] = c32[j % 32]
        sintab[d] = (-1.0 if j < 32 else 1.0) * s32[j % 32]
    cm = np.zeros((128, 4, 512), np.float32)
    ki = np.arange(128)[:, None]
    qi = np.arange(512)[None, :]
    for ob in range(4):
        cm[:, ob, :] = (ki + 128 * ob <= qi)
    p32 = np.zeros((128, 128), np.float32)
    for m in range(128):
        p32[(m + 32) if m % 64 < 32 else (m - 32), m] = 1.0
    return {
        "costab": costab.astype(BF), "sintab": sintab.astype(BF),
        "cmask": cm.reshape(128, 2048).astype(BF),
        "p32": p32.astype(BF),
        "ltri": (np.arange(128)[:, None] < np.arange(128)[None, :]).astype(BF),
        "ident": np.eye(128, dtype=BF),
        "onesbf1": np.ones((1, 128), BF),
        "onesf128": np.ones((128, 128), np.float32),
        "identf": np.eye(128, dtype=np.float32),
        "ones128bf": np.ones((128, 1), BF),
        "onesb128": np.ones((128, 128), BF),
        "fcon": np.broadcast_to(np.array([1e-5, 0.5, -0.5, -1.0, 0.125, 1.0, 0.0, 0.0], np.float32), (128, 8)).copy(),
    }


def _perm_cols():
    idx = np.empty(512, np.int64)
    for h in range(8):
        b = h * 64
        idx[b:b + 32] = b + 2 * np.arange(32)
        idx[b + 32:b + 64] = b + 2 * np.arange(32) + 1
    return idx


def _host_inputs_for_core(c, inp, consts, perm):
    b, hg = c // 2, c % 2
    x = np.asarray(inp["L0_attn_x"])
    xb = np.ascontiguousarray(x[b])
    sl = slice(hg * 512, (hg + 1) * 512)
    d = {
        "xT": np.ascontiguousarray(xb.T),
        "xrows": np.ascontiguousarray(xb[hg * HTOK:(hg + 1) * HTOK]),
        "wq": np.ascontiguousarray(np.asarray(inp["L0_attn_Wq"])[:, sl][:, perm]),
        "wk": np.ascontiguousarray(np.asarray(inp["L0_attn_Wk"])[:, sl][:, perm]),
        "wv": np.ascontiguousarray(np.asarray(inp["L0_attn_Wv"])[:, sl]),
        "wo": np.ascontiguousarray(np.asarray(inp["L0_attn_Wo"])),
        "bq": np.ascontiguousarray(np.asarray(inp["L0_attn_bq"])[sl][perm].reshape(4, 128).T),
        "bk": np.ascontiguousarray(np.asarray(inp["L0_attn_bk"])[sl][perm].reshape(4, 128).T),
        "bvbc": np.ascontiguousarray(np.broadcast_to(np.asarray(inp["L0_attn_bv"])[sl], (128, 512))),
        "bobc": np.ascontiguousarray(np.broadcast_to(np.asarray(inp["L0_attn_bo"]), (128, 1024))),
        "wg": np.ascontiguousarray(np.asarray(inp["L0_ffn_Wg"])),
        "bgbc": np.ascontiguousarray(np.broadcast_to(np.asarray(inp["L0_ffn_bg"]), (128, 8))),
        "w1": np.ascontiguousarray(np.asarray(inp["L0_ffn_W1"])[c]),
        "w2": np.ascontiguousarray(np.asarray(inp["L0_ffn_W2"])[c]),
        "b1": np.ascontiguousarray(np.asarray(inp["L0_ffn_b1"])[c].reshape(32, 128).T),
        "b2bc": np.ascontiguousarray(np.broadcast_to(np.asarray(inp["L0_ffn_b2"])[c], (128, 1024))),
        "tok0": np.full((128, 1), float(c * HTOK), np.float32),
        "hcol": np.full((128, 1), float(hg * HTOK), np.float32),
        "myslot0": np.full((128, 1), float(c * CAPP), np.float32),
    }
    sel = np.zeros((16, 2), np.float32)
    for r in range(8):
        for kp in range(2):
            if kp == 0 and r < c:
                sel[2 * r + kp, 0] = 1.0
            if (kp == 0 and r != c) or (kp == 1 and r < c):
                sel[2 * r + kp, 1] = 1.0
    d["selmat"] = sel
    d.update(consts)
    return d


def build(debug=(), upto="all", race=True):
    nc = bass.Bass("TRN2", target_bir_lowering=False, detect_race_conditions=race)
    ctx = ExitStack()

    def din(name, shape, dt=F32):
        return nc.dram_tensor(name, shape, dt, kind="ExternalInput").ap()

    xT = din("xT", [D, S]); xrows = din("xrows", [HTOK, D])
    wq_d = din("wq", [D, 512]); wk_d = din("wk", [D, 512])
    wv_d = din("wv", [D, 512]); wo_d = din("wo", [D, D])
    bq_d = din("bq", [128, 4]); bk_d = din("bk", [128, 4])
    bvbc_d = din("bvbc", [128, 512]); bobc_d = din("bobc", [128, 1024])
    wg_d = din("wg", [D, 8]); bgbc_d = din("bgbc", [128, 8])
    w1_d = din("w1", [D, FF]); w2_d = din("w2", [FF, D])
    b1_d = din("b1", [128, 32]); b2bc_d = din("b2bc", [128, 1024])
    selmat_d = din("selmat", [16, 2]); tok0_d = din("tok0", [128, 1])
    hcol_d = din("hcol", [128, 1])
    myslot0_d = din("myslot0", [128, 1])
    costab_d = din("costab", [128, S], BF16); sintab_d = din("sintab", [128, S], BF16)
    cmask_d = din("cmask", [128, 2048], BF16)
    p32_d = din("p32", [128, 128], BF16); ltri_d = din("ltri", [128, 128], BF16)
    ident_d = din("ident", [128, 128], BF16)
    onesbf1_d = din("onesbf1", [1, 128], BF16)
    onesf128_d = din("onesf128", [128, 128])
    identf_d = din("identf", [128, 128])
    ones128bf_d = din("ones128bf", [128, 1], BF16)
    onesb128_d = din("onesb128", [128, 128], BF16)
    fcon_d = din("fcon", [128, 8])

    y_out = nc.dram_tensor("y_out", [HTOK, D], F32, kind="ExternalOutput").ap()
    dbgout = {}

    def dout(name, shape, dt=F32):
        t = nc.dram_tensor("dbg_" + name, shape, dt, kind="ExternalOutput").ap()
        dbgout[name] = t
        return t

    xt_bf = nc.dram_tensor("xt_bf", [D, S], BF16)
    ag_in = nc.dram_tensor("ag_in", [512, S], BF16)
    ag_out = nc.dram_tensor("ag_out", [1024, S], BF16)
    x1_tm = nc.dram_tensor("x1_tm", [HTOK, D], F32)
    ln2_loc = nc.dram_tensor("ln2_loc", [HTOK, D], BF16)
    ln2_all = nc.dram_tensor("ln2_all", [NC * HTOK, D], BF16)
    cnt_loc = nc.dram_tensor("cnt_loc", [2, 8], F32)
    cnt_all = nc.dram_tensor("cnt_all", [16, 8], F32)
    bar_o = nc.dram_tensor("bar_o", [16, 8], F32)
    slot_loc = nc.dram_tensor("slot_loc", [E * CAPP, 1], F32)
    slot_ag = nc.dram_tensor("slot_ag", [NC * E * CAPP, 1], F32)
    ye_loc = nc.dram_tensor("ye_loc", [5120, D], BF16)
    ye_all = nc.dram_tensor("ye_all", [NC * 5120, D], BF16)

    sb = lambda n, s, d: ctx.enter_context(nc.sbuf_tensor(n, s, d))
    cmask = sb("cmaskt", [128, 4, 512], BF16)
    p32 = sb("p32t", [128, 128], BF16)
    ltri = sb("ltrit", [128, 128], BF16)
    ident = sb("identt", [128, 128], BF16)
    onesbf1 = sb("onesbf1t", [1, 128], BF16)
    onesf128 = sb("onesf128t", [128, 128], F32)
    onesb128 = sb("onesb128t", [128, 128], BF16)
    identf = sb("identft", [128, 128], F32)
    ones128bf = sb("ones128bft", [128, 1], BF16)
    fcon = sb("fcont", [128, 8], F32)
    for _i, _v in enumerate([1e-5, 0.5, -0.5, -1.0, 0.125, 1.0]):
        try:
            nc.const_aps.register(F32, _v, fcon[:, _i:_i + 1])
        except Exception:
            pass
    wgs = sb("wgs", [128, 8, 8], BF16)
    bqc = sb("bqc", [128, 4], F32)
    bkc = sb("bkc", [128, 4], F32)
    bgbc = sb("bgbct", [128, 8], F32)
    bgeps = sb("bgepst", [128, 8], F32)
    l1a = sb("l1at", [128, 16], F32)
    l1b = sb("l1bt", [128, 16], F32)
    l2a = sb("l2at", [128, 16], F32)
    l2b = sb("l2bt", [128, 16], F32)
    dal = sb("dalt", [128, 16], F32)
    offsl1 = sb("offsl1t", [128, 16], F32)
    offsl2 = sb("offsl2t", [128, 16], F32)
    cntrow = sb("cntrowt", [1, 16], F32)
    b1c = sb("b1ct", [128, 32], F32)
    b2bc = sb("b2bct", [128, 1024], F32)
    selmat = sb("selmatt", [16, 2], F32)
    tok0 = sb("tok0t", [128, 1], F32)
    hcol = sb("hcolt", [128, 1], F32)
    msl0 = sb("msl0t", [128, 1], F32)
    qn = sb("qnt", [1, 8], F32)
    negm = sb("negmt", [128, 1], F32)
    rtmp1 = sb("rtmp1", [128, 512], BF16)
    rtmp2 = sb("rtmp2", [128, 512], BF16)
    wqoc = sb("wqoc", [128, 2, 8, 128], BF16)
    wkoc = sb("wkoc", [128, 2, 8, 128], BF16)
    iot8 = sb("iot8", [128, 8], F32)
    e1s = sb("e1st", [128, 16], F32); e2s = sb("e2st", [128, 16], F32)
    poss1 = sb("poss1", [128, 16], F32); poss2 = sb("poss2", [128, 16], F32)
    g1s = sb("g1st", [128, 16], F32); g2s = sb("g2st", [128, 16], F32)
    keep1 = sb("keep1t", [128, 16], F32); keep2 = sb("keep2t", [128, 16], F32)
    idx1s = sb("idx1st", [128, 16], I32); idx2s = sb("idx2st", [128, 16], I32)
    idx1g = sb("idx1gt", [128, 16], I32); idx2g = sb("idx2gt", [128, 16], I32)
    y1i = sb("y1it", [128, 16], I32); y2i = sb("y2it", [128, 16], I32)
    m1i = sb("m1it", [128, 16], I32); m2i = sb("m2it", [128, 16], I32)
    t1i = sb("t1it", [128, 16], I32); t2i = sb("t2it", [128, 16], I32)
    idx1y = sb("idx1yt", [128, 16], I32); idx2y = sb("idx2yt", [128, 16], I32)
    slotY = sb("slotYt", [128, 39], I32)
    i39a = sb("i39at", [128, 39], I32); i39b = sb("i39bt", [128, 39], I32)
    i39c = sb("i39ct", [128, 39], I32)
    toks = sb("tokst", [128, 16], F32)
    offrow = sb("offrowt", [1, 16], F32)
    cntag = sb("cntagt", [16, 8], F32)
    offb = sb("offbt", [128, 2, 8], F32)
    zero1 = sb("zero1t", [128, 312], F32)
    st39 = sb("st39t", [39, 2, 128], F32)
    sum39 = sb("sum39t", [39, 128], F32)
    slotm = sb("slotmt", [128, 39], F32)
    slotmi = sb("slotmit", [128, 39], I32)
    agix = sb("agixt", [128, 1], I32)
    agixf = sb("agixft", [128, 1], F32)
    agix8 = sb("agix8t", [128, 8], I32)
    agix8f = sb("agix8ft", [128, 8], F32)
    stt = sb("sttt", [128, 8], F32)

    auto_used = nc.SBUF_PARTITION_SIZE_BYTES - nc.sbuf_bytes_remaining
    B0 = 48640
    assert auto_used <= B0, f"auto SBUF region overflow: {auto_used} > {B0}"
    TOP = nc.SBUF_PARTITION_SIZE_BYTES
    at = lambda n, s, d, off: nc.alloc_sbuf_tensor_at(n, s, d, offset=B0 + off)

    qrot = at("qrot", [128, 4, S], BF16, 0)
    krot = at("krot", [128, 4, S], BF16, 32768)
    vext = at("vext", [128, 32, 520], BF16, 65536)
    Z0 = 98816
    xf0 = at("xf0", [128, S], F32, 0)
    xf1 = at("xf1", [128, S], F32, 16384)
    xbb = at("xbb", [128, S], BF16, 32768)
    sqb = at("sqb", [128, S], BF16, 40960)
    R1 = at("R1", [1, S], F32, 49152)
    R2 = at("R2", [1, S], F32, 65536)
    xtil = at("xtil", [128, S], BF16, 81920)
    R1b = at("R1b", [1, S], BF16, Z0 + 0)
    R2b = at("R2b", [1, S], BF16, Z0 + 8192)
    rbcb = at("rbcb", [128, S], BF16, Z0 + 16384)
    murbcb = at("murbcb", [128, S], BF16, Z0 + 24576)
    tmpr = at("tmprt", [1, 3, 512], F32, Z0 + 32768)
    xstr = at("xstr", [128, 2, 8, 512], BF16, Z0 + 0)
    aq = at("aqt", [128, S], BF16, Z0 + 16384)
    ak = at("akt", [128, S], BF16, Z0 + 24576)
    costab = at("costabt", [128, S], BF16, Z0 + 32768)
    sintab = at("sintabt", [128, S], BF16, Z0 + 40960)
    wvs = at("wvst", [128, 8, 512], BF16, Z0 + 16384)
    bvbc = at("bvbct", [128, 512], F32, Z0 + 24576)
    psb = [at("psb0", [128, 1024], BF16, Z0 + 32768),
           at("psb1", [128, 1024], BF16, Z0 + 34816),
           at("psb2", [128, 1024], BF16, Z0 + 16384),
           at("psb3", [128, 1024], BF16, Z0 + 18432)]
    rlA = at("rlA", [128, 512], F32, Z0 + 36864)
    rlB = at("rlB", [128, 512], F32, Z0 + 38912)
    lrow = at("lrowt", [65, 1024], F32, Z0 + 40960)
    lrow2 = at("lrow2t", [65, 1024], F32, Z0 + 45056)
    stg = at("stgt", [64, 2, 2, 512], BF16, Z0 + 0)
    agT = at("agTt", [128, 8, HTOK], BF16, 0)
    ln2T = at("ln2Tt", [128, 8, HTOK], BF16, 32768)
    xrw = at("xrwt", [128, 2, 1024], F32, 65536)
    x1t = at("x1tt", [128, 2, 1024], F32, 73728)
    ln2t = at("ln2tt", [128, 2, 1024], BF16, 81920)
    sqf = at("sqft", [128, 1024], F32, 86016)
    rt0 = at("rt0t", [128, 16, 8], F32, 65536)
    eq1a = at("eq1at", [128, 16, 8], F32, 66048)
    msk2 = at("msk2t", [128, 16, 8], F32, 66560)
    eq2a = at("eq2at", [128, 16, 8], F32, 67072)
    ohb = at("ohbt", [128, 2, 16, 8], BF16, 67584)
    posall = at("posallt", [128, 2, 16, 8], F32, 68096)
    wos = at("wost", [128, 8, 1024], BF16, 90112)
    bobc = at("bobct", [128, 1024], F32, 106496)
    w1s = at("w1st", [128, 8, FF], BF16, 0)
    w2s = at("w2st", [128, 32, 1024], BF16, 65536)
    hsb = at("hsbt", [128, 32, 512], BF16, 131072)
    bufT = at("bufTt", [128, 8, 512], BF16, 163840)
    stage = at("staget", [128, 2, 1024], BF16, 172032)
    yet = at("yett", [128, 2, 1024], BF16, 176128)
    gb1 = at("gb1t", [128, 2, 1024], BF16, 0)
    gb2 = at("gb2t", [128, 2, 1024], BF16, 4096)
    x1r = at("x1rt", [128, 2, 1024], F32, 8192)
    outt = at("outtt", [128, 2, 1024], F32, 16384)
    zsum = at("zsumt", [128, 1024], F32, 24576)
    assert B0 + 180224 <= TOP - 8, (B0 + 180224, TOP)

    ps_ = lambda n, s, d: ctx.enter_context(nc.psum_tensor(n, s, d))
    pbA = ps_("pbA", [128, 1024], F32)
    pbB = ps_("pbB", [128, 1024], F32)
    pbC = ps_("pbC", [128, 512], F32)
    pbD = ps_("pbD", [128, 512], F32)
    pbE = ps_("pbE", [128, 512], F32)
    pbT = ps_("pbT", [128, 256], BF16)

    semn = ["ld", "gw", "pe", "dv", "ac", "sp", "g2", "sc", "gp",
            "smm", "exp", "msk", "omm", "nrm", "lcp", "rld", "rbc", "cc"]
    sems = {n: ctx.enter_context(nc.semaphore("s_" + n)) for n in semn}

    P = Plan()

    def xbounce():
        P.o("vector", lambda v: v.engine_nop(), "dv")
        P.w("scalar", "dv", P.v("dv"))
        P.o("scalar", lambda s: s.copy(cntrow[0:1, 0:1], cntrow[0:1, 0:1]), "ac")
        P.w("gpsimd", "ac", P.v("ac"))
        P.w("vector", "ac", P.v("ac"))

    def _finish():
        for _sn in semn:
            P.w("sync", _sn, P.v(_sn))
        block = ctx.enter_context(nc.Block())

        def emit(name):
            def body(eng):
                for entry in P.ops[name]:
                    if entry[0] == "w":
                        eng.wait_ge(sems[entry[1]], entry[2])
                    else:
                        bi = entry[1](eng)
                        for (inc, k) in entry[2]:
                            bi = bi.then_inc(sems[inc], k)
            return body

        block.sync(emit("sync"))
        block.scalar(emit("scalar"))
        block.vector(emit("vector"))
        block.tensor(emit("tensor"))
        block.gpsimd(emit("gpsimd"))
        ctx.close()
        return nc, dbgout


    const_loads = [(cmask, cmask_d.rearrange("p (a b) -> p a b", b=512)),
                   (p32, p32_d), (ltri, ltri_d), (ident, ident_d),
                   (onesbf1, onesbf1_d), (onesf128, onesf128_d), (identf, identf_d),
                   (onesb128, onesb128_d),
                   (ones128bf, ones128bf_d), (fcon, fcon_d), (bqc, bq_d), (bkc, bk_d),
                   (bgbc, bgbc_d),
                   (b1c, b1_d), (b2bc, b2bc_d), (selmat, selmat_d),
                   (tok0, tok0_d), (hcol, hcol_d), (msl0, myslot0_d)]
    for dst, src in const_loads:
        P.o("sync", (lambda d_, s_: (lambda sy: sy.dma_start(out=d_[:], in_=s_[:])))(dst, src), "ld", 16)
    LD_CONST = P.v("ld")
    P.o("gpsimd", lambda g: g.dma_start(out=wgs[:], in_=wg_d.rearrange("(c p) m -> p c m", p=128)), "gw", 16)
    GW_WG = P.v("gw")
    REGS = {}
    def _mkregs(g):
        REGS["slot"] = g.to_reg(E * CAPP - 1)
        REGS["ln2"] = g.to_reg(NC * HTOK - 1)
        REGS["ag"] = g.to_reg(1024 * S - 1)
        REGS["sag"] = g.to_reg(NC * E * CAPP - 1)
        REGS["ye"] = g.to_reg(NC * 5120 - 1)
        return g.nop()
    P.o("gpsimd", _mkregs)
    P.o("gpsimd", lambda g: g.iota(iot8[:], pattern=[[1, 8]], base=0, channel_multiplier=0,
                                   allow_small_or_imprecise_dtypes=True), "gp")
    P.o("gpsimd", lambda g: g.iota(toks[:], pattern=[[128, 16]], base=0, channel_multiplier=1,
                                   allow_small_or_imprecise_dtypes=True), "gp")
    G2_IOTA = P.v("gp")
    P.o("vector", lambda v: v.memset(zero1[:], 0.0), "dv")
    P.o("vector", lambda v: v.memset(qn[:], 0.0), "dv")
    P.o("vector", lambda v: v.memset(R1[:], 0.0), "dv")
    P.o("vector", lambda v: v.memset(R2[:], 0.0), "dv")
    P.o("vector", lambda v: v.memset(sum39[:], 0.0), "dv")
    P.o("vector", lambda v: v.memset(cntrow[:], 0.0), "dv")
    P.w("vector", "gp", G2_IOTA)
    P.w("vector", "ld", LD_CONST)
    P.o("vector", lambda v: v.tensor_scalar_add(toks[:], toks[:], tok0[:, 0:1]), "dv")
    P.o("vector", lambda v: v.tensor_scalar_mul(l1a[:, 0:8], iot8[:], 1e-5), "dv")
    P.w("vector", "dv", P.v("dv"))
    P.o("vector", lambda v: v.tensor_sub(bgeps[:], bgbc[:], l1a[:, 0:8]), "dv")
    DV_INIT = P.v("dv")

    xTr = xT.rearrange("(c p) t -> p c t", p=128)
    xf = [xf0, xf1]

    dv_cast = {}
    for c8 in range(8):
        if c8 >= 2:
            P.w("sync", "dv", dv_cast[c8 - 2])
        P.o("sync", (lambda cc: (lambda sy: sy.dma_start(out=xf[cc % 2][:], in_=xTr[:, cc, :])))(c8), "ld", 16)
        ldv = P.v("ld")
        P.w("vector", "ld", ldv)
        if c8 >= 1:
            P.w("vector", "pe", P.v("pe"))
        P.o("vector", (lambda cc: (lambda v: v.tensor_copy(xbb[:], xf[cc % 2][:])))(c8), "dv")
        dv_cast[c8] = P.v("dv")
        P.w("scalar", "dv", dv_cast[c8])
        P.w("scalar", "pe", P.v("pe"))
        P.o("scalar", lambda s: s.activation(sqb[:], xbb[:], AF.Square), "ac")
        acs = P.v("ac")
        for tb in range(8):
            P.w("tensor", "dv", dv_cast[c8])
            P.w("tensor", "ac", acs)
            P.w("tensor", "dv", P.v("dv"))
            P.o("tensor", (lambda t0: (lambda t: t.matmul(pbC[0:1, :], ones128bf[:],
                 xbb[:, t0 * 512:(t0 + 1) * 512], start=True, stop=True)))(tb))
            P.o("tensor", (lambda t0: (lambda t: t.matmul(pbD[0:1, :], ones128bf[:],
                 sqb[:, t0 * 512:(t0 + 1) * 512], start=True, stop=True)))(tb), "pe")
            P.w("vector", "pe", P.v("pe"))
            P.o("vector", (lambda t0: (lambda v: v.tensor_add(
                 R1[0:1, t0 * 512:(t0 + 1) * 512], R1[0:1, t0 * 512:(t0 + 1) * 512], pbC[0:1, :])))(tb))
            P.o("vector", (lambda t0: (lambda v: v.tensor_add(
                 R2[0:1, t0 * 512:(t0 + 1) * 512], R2[0:1, t0 * 512:(t0 + 1) * 512], pbD[0:1, :])))(tb), "dv")
    for tb in range(8):
        s0, s1 = tb * 512, (tb + 1) * 512
        P.w("vector", "dv", P.v("dv"))
        P.o("vector", (lambda a, b: (lambda v: v.tensor_scalar_mul(tmpr[0:1, 0, :], R1[0:1, a:b], 1.0 / D)))(s0, s1))
        P.o("vector", (lambda a, b: (lambda v: v.tensor_scalar_mul(tmpr[0:1, 1, :], R2[0:1, a:b], 1.0 / D)))(s0, s1), "dv")
        P.w("scalar", "dv", P.v("dv"))
        P.o("scalar", lambda s: s.activation(tmpr[0:1, 2, :], tmpr[0:1, 0, :], AF.Square), "ac")
        P.w("vector", "ac", P.v("ac"))
        P.o("vector", lambda v: v.tensor_sub(tmpr[0:1, 1, :], tmpr[0:1, 1, :], tmpr[0:1, 2, :]), "dv")
        P.w("scalar", "dv", P.v("dv"))
        P.o("scalar", lambda s: s.activation(tmpr[0:1, 1, :], tmpr[0:1, 1, :], AF.Ln, bias=EPS))
        P.o("scalar", (lambda a, b: (lambda s: s.activation(R1[0:1, a:b], tmpr[0:1, 1, :], AF.Exp, scale=-0.5)))(s0, s1), "ac")
        P.w("vector", "ac", P.v("ac"))
        P.o("vector", (lambda a, b: (lambda v: v.tensor_mul(R2[0:1, a:b], tmpr[0:1, 0, :], R1[0:1, a:b])))(s0, s1))
        P.o("vector", (lambda a, b: (lambda v: v.tensor_copy(R1b[0:1, a:b], R1[0:1, a:b])))(s0, s1))
        P.o("vector", (lambda a, b: (lambda v: v.tensor_copy(R2b[0:1, a:b], R2[0:1, a:b])))(s0, s1), "dv")
    DV_STATS = P.v("dv")
    for tb in range(8):
        s0, s1 = tb * 512, (tb + 1) * 512
        P.w("tensor", "dv", P.v("dv"))
        P.o("tensor", (lambda a, b: (lambda t: t.matmul(pbE[:], onesbf1[:], R1b[0:1, a:b],
             start=True, stop=True)))(s0, s1), "pe")
        P.w("vector", "pe", P.v("pe"))
        P.o("vector", (lambda a, b: (lambda v: v.tensor_copy(rbcb[:, a:b], pbE[:])))(s0, s1), "dv")
        P.w("tensor", "dv", P.v("dv"))
        P.o("tensor", (lambda a, b: (lambda t: t.matmul(pbE[:], onesbf1[:], R2b[0:1, a:b],
             start=True, stop=True)))(s0, s1), "pe")
        P.w("vector", "pe", P.v("pe"))
        P.o("vector", (lambda a, b: (lambda v: v.tensor_copy(murbcb[:, a:b], pbE[:])))(s0, s1), "dv")
    DV_BC = P.v("dv")

    sp_xtil = {}
    for c8 in range(8):
        P.w("sync", "dv", DV_BC)
        if c8 >= 2:
            P.w("sync", "dv", P.v("dv"))
        P.o("sync", (lambda cc: (lambda sy: sy.dma_start(out=xf[cc % 2][:], in_=xTr[:, cc, :])))(c8), "ld", 16)
        ldv = P.v("ld")
        P.w("vector", "ld", ldv)
        if c8 >= 1:
            P.w("vector", "sp", sp_xtil[c8 - 1])
        P.o("vector", (lambda cc: (lambda v: v.tensor_copy(xbb[:], xf[cc % 2][:])))(c8))
        P.o("vector", lambda v: v.tensor_mul(xtil[:], xbb[:], rbcb[:]))
        P.o("vector", lambda v: v.tensor_sub(xtil[:], xtil[:], murbcb[:]), "dv")
        P.w("sync", "dv", P.v("dv"))
        P.o("sync", (lambda cc: (lambda sy: sy.dma_start(
            out=xt_bf.ap()[cc * 128:(cc + 1) * 128, :], in_=xtil[:])))(c8), "sp", 16)
        sp_xtil[c8] = P.v("sp")
    SP_XT = P.v("sp")
    if "xt" in debug:
        dout("xt", [D, S], BF16)
        P.w("sync", "sp", SP_XT)
        P.o("sync", lambda sy: sy.dma_start(out=dbgout["xt"][:], in_=xt_bf.ap()[:]), "sp", 16)
    if upto == "p0":
        return _finish()

    P.w("sync", "sp", SP_XT)
    P.o("sync", lambda sy: sy.dma_start(out=costab[:], in_=costab_d[:]), "ld", 16)
    P.o("sync", lambda sy: sy.dma_start(out=sintab[:], in_=sintab_d[:]), "ld", 16)
    LD_TAB = P.v("ld")

    xtr = xt_bf.ap().rearrange("(c p) t -> p c t", p=128)
    pe_xstr = {}
    gw_oc = {}
    for oc in range(4):
        P.w("gpsimd", "pe", pe_xstr.get((oc - 1, 7), 0))
        P.o("gpsimd", (lambda o_: (lambda g: g.dma_start(out=wqoc[:, o_ % 2, :, :],
             in_=wq_d.rearrange("(c p) m -> p c m", p=128)[:, :, o_ * 128:(o_ + 1) * 128])))(oc), "gw", 16)
        P.o("gpsimd", (lambda o_: (lambda g: g.dma_start(out=wkoc[:, o_ % 2, :, :],
             in_=wk_d.rearrange("(c p) m -> p c m", p=128)[:, :, o_ * 128:(o_ + 1) * 128])))(oc), "gw", 16)
        gw_oc[oc] = P.v("gw")
        for tb in range(8):
            it = oc * 8 + tb
            P.w("sync", "sp", SP_XT)
            if it >= 2:
                P.w("sync", "pe", pe_xstr[(oc, tb - 2)] if tb >= 2 else pe_xstr[(oc - 1, 6 + tb)])
            P.o("sync", (lambda t0, pr: (lambda sy: sy.dma_start(
                out=xstr[:, pr, :, :], in_=xtr[:, :, t0 * 512:(t0 + 1) * 512])))(tb, it % 2), "ld", 16)
            ldv = P.v("ld")
            P.w("tensor", "ld", ldv)
            P.w("tensor", "gw", gw_oc[oc])
            P.w("tensor", "dv", P.v("dv"))
            for c8 in range(8):
                P.o("tensor", (lambda cc, o_, pr: (lambda t: t.matmul(pbC[:],
                     wqoc[:, o_ % 2, cc, :], xstr[:, pr, cc, :],
                     start=(cc == 0), stop=(cc == 7))))(c8, oc, it % 2))
            for c8 in range(8):
                P.o("tensor", (lambda cc, o_, pr: (lambda t: t.matmul(pbD[:],
                     wkoc[:, o_ % 2, cc, :], xstr[:, pr, cc, :],
                     start=(cc == 0), stop=(cc == 7))))(c8, oc, it % 2))
            pe_xstr[(oc, tb)] = P.m("tensor", "pe")
            s0, s1 = tb * 512, (tb + 1) * 512
            P.w("vector", "pe", P.v("pe"))
            P.o("vector", (lambda a, b, o_: (lambda v: v.tensor_scalar_add(aq[:, a:b], pbC[:], bqc[:, o_:o_ + 1])))(s0, s1, oc))
            P.o("vector", (lambda a, b, o_: (lambda v: v.tensor_scalar_add(ak[:, a:b], pbD[:], bkc[:, o_:o_ + 1])))(s0, s1, oc), "dv")
            dvqk = P.v("dv")
            if upto in ("p1q",) and oc == 0 and tb == 7:
                return _finish()
            if upto == "p1q":
                continue
            for (src, dst, half) in ((aq, qrot, slice(0, 512)), (ak, krot, slice(512, 1024))):
                P.w("tensor", "dv", dvqk)
                P.w("tensor", "dv", P.v("dv"))
                P.o("tensor", (lambda s_, a, b, h: (lambda t: t.matmul(pbA[:, h],
                     p32[:], s_[:, a:b], start=True, stop=True)))(src, s0, s1, half), "pe")
                P.w("vector", "pe", P.v("pe"))
                P.w("vector", "ld", LD_TAB)
                P.o("vector", (lambda a, b, h: (lambda v: v.tensor_mul(rtmp1[:], pbA[:, h], sintab[:, a:b])))(s0, s1, half))
                P.o("vector", (lambda s_, a, b: (lambda v: v.tensor_mul(rtmp2[:], s_[:, a:b], costab[:, a:b])))(src, s0, s1))
                P.o("vector", (lambda d_, o_, a, b: (lambda v: v.tensor_add(d_[:, o_, a:b], rtmp1[:], rtmp2[:])))(dst, oc, s0, s1), "dv")
        if upto == "p1r" and oc == 0:
            return _finish()
        for (dst, col) in ((qrot, 0), (krot, 1)):
            P.w("scalar", "dv", P.v("dv"))
            P.o("scalar", (lambda d_, o_: (lambda s: s.activation(aq[:], d_[:, o_, :], AF.Square)))(dst, oc), "ac")
            acs = P.v("ac")
            for tb in range(8):
                P.w("tensor", "ac", acs)
                P.w("tensor", "dv", P.v("dv"))
                P.o("tensor", (lambda t0: (lambda t: t.matmul(pbE[0:1, :], ones128bf[:],
                     aq[:, t0 * 512:(t0 + 1) * 512], start=True, stop=True)))(tb), "pe")
                P.w("vector", "pe", P.v("pe"))
                P.o("vector", lambda v: v.reduce_max(qn[0:1, 7:8], pbE[0:1, :], axis=AX.X))
                P.o("vector", (lambda cl: (lambda v: v.tensor_max(qn[0:1, cl:cl + 1],
                     qn[0:1, cl:cl + 1], qn[0:1, 7:8])))(col), "dv")
        if upto == "p1a" and oc == 0:
            return _finish()
    P.w("scalar", "dv", P.v("dv"))
    P.o("scalar", lambda s: s.activation(qn[0:1, 2:3], qn[0:1, 0:1], AF.Ln))
    P.o("scalar", lambda s: s.activation(qn[0:1, 3:4], qn[0:1, 1:2], AF.Ln), "ac")
    P.w("vector", "ac", P.v("ac"))
    P.o("vector", lambda v: v.tensor_add(qn[0:1, 4:5], qn[0:1, 2:3], qn[0:1, 3:4]), "dv")
    P.w("scalar", "dv", P.v("dv"))
    P.o("scalar", lambda s: s.activation(qn[0:1, 5:6], qn[0:1, 4:5], AF.Exp, scale=0.5), "ac")
    P.w("vector", "ac", P.v("ac"))
    P.o("vector", lambda v: v.tensor_scalar_mul(qn[0:1, 6:7], qn[0:1, 5:6], -0.125), "dv")
    P.w("tensor", "dv", P.v("dv"))
    P.o("tensor", lambda t: t.matmul(pbE[:, 0:1], onesf128[0:1, :], qn[0:1, 6:7], start=True, stop=True), "pe")
    P.w("vector", "pe", P.v("pe"))
    P.o("vector", lambda v: v.tensor_copy(negm[:], pbE[:, 0:1]), "dv")
    DV_NEGM = P.v("dv")
    if upto == "p1b":
        return _finish()
    P.w("gpsimd", "dv", DV_NEGM)
    P.o("gpsimd", lambda g: g.dma_start(out=wvs[:], in_=wv_d.rearrange("(c p) m -> p c m", p=128)), "gw", 16)
    GW_WV = P.v("gw")
    P.w("sync", "dv", DV_NEGM)
    P.o("sync", lambda sy: sy.dma_start(out=bvbc[:], in_=bvbc_d[:]), "ld", 16)
    for tb in range(8):
        it = 32 + tb
        prev = pe_xstr.get((3, tb - 2), 0) if tb < 2 else 0
        P.w("sync", "pe", pe_xstr[("v", tb - 2)] if tb >= 2 else pe_xstr[(3, 6 + tb)])
        P.o("sync", (lambda t0, pr: (lambda sy: sy.dma_start(
            out=xstr[:, pr, :, :], in_=xtr[:, :, t0 * 512:(t0 + 1) * 512])))(tb, it % 2), "ld", 16)
        ldv = P.v("ld")
        for ttl in range(4):
            tts = tb * 4 + ttl
            pv = pbC if tts % 2 == 0 else pbD
            P.w("tensor", "ld", ldv)
            P.w("tensor", "gw", GW_WV)
            P.w("tensor", "dv", P.v("dv"))
            for c8 in range(8):
                P.o("tensor", (lambda cc, tl, pr, pv_: (lambda t: t.matmul(pv_[:],
                     xstr[:, pr, cc, tl * 128:(tl + 1) * 128], wvs[:, cc, :],
                     start=(cc == 0), stop=(cc == 7))))(c8, ttl, it % 2, pv))
            mark = P.m("tensor", "pe")
            if ttl == 3:
                pe_xstr[("v", tb)] = mark
            P.w("vector", "pe", P.v("pe"))
            P.o("vector", (lambda tt_: (lambda v: v.memset(
                vext[:, tt_, :].rearrange("p (h x) -> p h x", x=65)[:, :, 64:65], 1.0)))(tts))
            P.o("vector", (lambda tt_, pv_: (lambda v: v.tensor_add(
                vext[:, tt_, :].rearrange("p (h x) -> p h x", x=65)[:, :, 0:64],
                pv_[:].rearrange("p (h x) -> p h x", x=64),
                bvbc[:].rearrange("p (h x) -> p h x", x=64))))(tts, pv), "dv")
    DV_P1 = P.v("dv")
    if "qkv" in debug:
        dout("qrot", [128, 4 * S], BF16); dout("krot", [128, 4 * S], BF16)
        dout("vext", [128, 32 * 520], BF16); dout("negm", [128, 1])
        P.w("sync", "dv", DV_P1)
        P.o("sync", lambda sy: sy.dma_start(out=dbgout["qrot"][:], in_=qrot[:].rearrange("p a b -> p (a b)")), "sp", 16)
        P.o("sync", lambda sy: sy.dma_start(out=dbgout["krot"][:], in_=krot[:].rearrange("p a b -> p (a b)")), "sp", 16)
        P.o("sync", lambda sy: sy.dma_start(out=dbgout["vext"][:], in_=vext[:].rearrange("p a b -> p (a b)")), "sp", 16)
        P.o("sync", lambda sy: sy.dma_start(out=dbgout["negm"][:], in_=negm[:]), "sp", 16)
    if upto == "p1":
        return _finish()

    SB2 = [pbA, pbB]
    exp_after, msk_after, omm_after = {}, {}, {}
    rbe_after, sc_stage = {}, {}
    gi, nq = 0, 0
    for ph in range(4):
        hA, hB = 2 * ph, 2 * ph + 1
        for qb in range(8):
            K = 4 * (qb + 1)
            qs = slice(qb * 512, (qb + 1) * 512)
            g0 = gi

            def emit_S(kt):
                g = g0 + kt
                par = g % 2
                ks = slice(kt * 128, (kt + 1) * 128)
                if g == 0:
                    P.w("tensor", "dv", DV_P1)
                if kt == 0 and nq > 0:
                    P.w("tensor", "nrm", nq)
                if kt < 2 and nq > 0:
                    P.w("tensor", "dv", rbe_after[nq - 1])
                if g >= 2:
                    P.w("tensor", "exp", exp_after[g - 2])
                P.o("tensor", (lambda p_, k_, q_, pr: (lambda t: t.matmul(SB2[pr][:, 0:512],
                     krot[0:64, p_, k_], qrot[0:64, p_, q_], start=True, stop=True)))(ph, ks, qs, par))
                P.o("tensor", (lambda p_, k_, q_, pr: (lambda t: t.matmul(SB2[pr][:, 512:1024],
                     krot[64:128, p_, k_], qrot[64:128, p_, q_], start=True, stop=True)))(ph, ks, qs, par), "smm")

            def emit_exp(kt):
                g = g0 + kt
                par = g % 2
                p4_ = g % 4
                P.w("scalar", "smm", P.v("smm"))
                P.w("scalar", "dv", DV_NEGM)
                if g >= 4:
                    P.w("scalar", "omm", omm_after[g - 4])
                P.o("scalar", (lambda pr, pp: (lambda s: s.activation(psb[pp][:], SB2[pr][:, 0:1024],
                     AF.Exp, bias=negm[:, 0:1], scale=0.125)))(par, p4_), "exp")
                exp_after[g] = P.v("exp")
                if kt >= K - 4:
                    ob = kt - 4 * qb
                    P.w("vector", "exp", exp_after[g])
                    P.o("vector", (lambda pp, o_: (lambda v: v.tensor_mul(psb[pp][:, 0:512],
                         psb[pp][:, 0:512], cmask[:, o_, :])))(p4_, ob))
                    P.o("vector", (lambda pp, o_: (lambda v: v.tensor_mul(psb[pp][:, 512:1024],
                         psb[pp][:, 512:1024], cmask[:, o_, :])))(p4_, ob), "msk")
                    msk_after[g] = P.v("msk")

            def emit_O(kt):
                g = g0 + kt
                p4_ = g % 4
                if kt >= K - 4:
                    P.w("tensor", "msk", msk_after[g])
                else:
                    P.w("tensor", "exp", exp_after[g])
                P.o("tensor", (lambda kt_, pp, st_, sp2, h_: (lambda t: t.matmul(pbC[0:65, :],
                     vext[:, kt_, h_ * 65:h_ * 65 + 65], psb[pp][:, 0:512],
                     start=st_, stop=sp2)))(kt, p4_, kt == 0, kt == K - 1, hA))
                P.o("tensor", (lambda kt_, pp, st_, sp2, h_: (lambda t: t.matmul(pbD[0:65, :],
                     vext[:, kt_, h_ * 65:h_ * 65 + 65], psb[pp][:, 512:1024],
                     start=st_, stop=sp2)))(kt, p4_, kt == 0, kt == K - 1, hB), "omm")
                omm_after[g] = P.v("omm")

            emit_S(0)
            emit_exp(0)
            for kt in range(K):
                if kt + 1 < K:
                    emit_S(kt + 1)
                    emit_exp(kt + 1)
                emit_O(kt)
            gi += K
            P.w("vector", "omm", omm_after[gi - 1])
            P.o("vector", lambda v: v.tensor_copy(lrow[64:65, 0:512], pbC[64:65, :]))
            P.o("vector", lambda v: v.tensor_copy(lrow[64:65, 512:1024], pbD[64:65, :]), "lcp")
            P.w("scalar", "lcp", P.v("lcp"))
            P.o("scalar", lambda s: s.activation(lrow2[64:65, :], lrow[64:65, :], AF.Ln))
            P.o("scalar", lambda s: s.activation(lrow[64:65, :], lrow2[64:65, :], AF.Exp, scale=-1.0), "rld")
            P.w("tensor", "rld", P.v("rld"))
            P.w("tensor", "exp", exp_after[gi - 1])
            P.o("tensor", lambda t: t.matmul(pbA[:, 0:512], onesf128[64:65, :],
                 lrow[64:65, 0:512], start=True, stop=True))
            P.o("tensor", lambda t: t.matmul(pbA[:, 512:1024], onesf128[64:65, :],
                 lrow[64:65, 512:1024], start=True, stop=True), "rbc")
            P.w("vector", "rbc", P.v("rbc"))
            P.o("vector", lambda v: v.tensor_copy(rlA[:], pbA[:, 0:512]))
            P.o("vector", lambda v: v.tensor_copy(rlB[:], pbA[:, 512:1024]), "dv")
            rbe_after[nq] = P.v("dv")
            spar = nq % 2
            if nq >= 2:
                P.w("vector", "sc", sc_stage[nq - 2])
            P.o("vector", (lambda sp2: (lambda v: v.tensor_mul(stg[0:64, sp2, 0, :],
                 pbC[0:64, :], rlA[0:64, :])))(spar))
            P.o("vector", (lambda sp2: (lambda v: v.tensor_mul(stg[0:64, sp2, 1, :],
                 pbD[0:64, :], rlB[0:64, :])))(spar), "nrm")
            P.w("scalar", "nrm", P.v("nrm"))
            P.o("scalar", (lambda sp2, h_, q_: (lambda s: s.dma_start(
                out=ag_in.ap()[h_ * 64:(h_ + 1) * 64, q_], in_=stg[0:64, sp2, 0, :])))(spar, hA, qs), "sc", 16)
            P.o("scalar", (lambda sp2, h_, q_: (lambda s: s.dma_start(
                out=ag_in.ap()[h_ * 64:(h_ + 1) * 64, q_], in_=stg[0:64, sp2, 1, :])))(spar, hB, qs), "sc", 16)
            sc_stage[nq] = P.v("sc")
            nq += 1
    SC_ATT = P.v("sc")
    if "attn" in debug:
        dout("attn", [512, S], BF16)
        P.w("sync", "sc", SC_ATT)
        P.o("sync", lambda sy: sy.dma_start(out=dbgout["attn"][:], in_=ag_in.ap()[:]), "sp", 16)
    if upto == "p2":
        return _finish()

    P.w("gpsimd", "omm", omm_after[gi - 1])
    P.w("gpsimd", "sc", SC_ATT)
    P.o("gpsimd", lambda g: g.dma_start(out=wos[:], in_=wo_d.rearrange("(c p) m -> p c m", p=128)), "gw", 16)
    GW_WO = P.v("gw")
    P.w("sync", "omm", omm_after[gi - 1])
    P.o("sync", lambda sy: sy.dma_start(out=bobc[:], in_=bobc_d[:]), "ld", 16)

    P.w("gpsimd", "sc", SC_ATT)
    P.o("gpsimd", lambda g: g.collective_compute("AllGather", ALU.bypass,
         replica_groups=[[0, 1], [2, 3], [4, 5], [6, 7]],
         ins=[ag_in.ap().opt()], outs=[ag_out.ap().opt()]), "cc")
    CC_ATT = P.v("cc")
    g2_agT = {}
    P.w("gpsimd", "ld", LD_CONST)
    for c8 in range(8):
        P.o("gpsimd", (lambda cc: (lambda g: g.iota(agix8f[:, cc:cc + 1],
             pattern=[[0, 1]], base=cc * 128 * S, channel_multiplier=S,
             allow_small_or_imprecise_dtypes=True)))(c8))
    P.o("gpsimd", lambda g: g.tensor_scalar_add(agix8f[:], agix8f[:], hcol[:, 0:1]))
    P.o("gpsimd", lambda g: g.tensor_copy(agix8[:], agix8f[:]), "gp")
    P.w("gpsimd", "cc", CC_ATT)
    for c8 in range(8):
        P.o("gpsimd", (lambda cc: (lambda g: g.indirect_dma_start(
             out=agT[:, cc, :], out_offset=None,
             in_=ag_out.ap(),
             in_offset=IndirectOffsetOnAxis(agix8[:, cc:cc + 1], 1),
             bounds_check=REGS["ag"], oob_is_err=False)))(c8), "g2", 16)
        g2_agT[c8] = P.v("g2")
    G2_AGT = P.v("g2")
    dv_ln2T = {}
    sc_ln2h = {}
    sp_x1h = {}
    for tt in range(16):
        tpar = tt % 2
        P.w("tensor", "g2", G2_AGT)
        P.w("tensor", "gw", GW_WO)
        P.w("tensor", "dv", P.v("dv"))
        for nb2 in range(2):
            for c8 in range(8):
                P.o("tensor", (lambda cc, n_, t_: (lambda t: t.matmul(
                     pbA[:, n_ * 512:(n_ + 1) * 512], agT[:, cc, t_ * 128:(t_ + 1) * 128],
                     wos[:, cc, n_ * 512:(n_ + 1) * 512], start=(cc == 0), stop=(cc == 7))))(c8, nb2, tt))
        P.m("tensor", "pe")
        pey = P.v("pe")
        if tt >= 2:
            P.w("sync", "dv", P.v("dv"))
        P.o("sync", (lambda t_, pr: (lambda sy: sy.dma_start(out=xrw[:, pr, :],
             in_=xrows[t_ * 128:(t_ + 1) * 128, :])))(tt, tpar), "ld", 16)
        P.w("vector", "pe", pey)
        P.w("vector", "ld", P.v("ld"))
        if tt >= 2:
            P.w("vector", "sp", sp_x1h[tt - 2])
        P.o("vector", (lambda pr: (lambda v: v.tensor_add(x1t[:, pr, :], pbA[:, 0:1024], bobc[:])))(tpar))
        P.o("vector", (lambda pr: (lambda v: v.tensor_add(x1t[:, pr, :], x1t[:, pr, :], xrw[:, pr, :])))(tpar), "dv")
        dvx1 = P.v("dv")
        P.w("sync", "dv", dvx1)
        P.o("sync", (lambda t_, pr: (lambda sy: sy.dma_start(
             out=x1_tm.ap()[t_ * 128:(t_ + 1) * 128, :], in_=x1t[:, pr, :])))(tt, tpar), "sp", 16)
        sp_x1h[tt] = P.v("sp")
        P.w("vector", "dv", dvx1)
        P.o("vector", (lambda pr: (lambda v: v.reduce_sum(stt[:, 0:1], x1t[:, pr, :], axis=AX.X)))(tpar), "dv")
        P.w("scalar", "dv", P.v("dv"))
        P.o("scalar", (lambda pr: (lambda s: s.activation(sqf[:], x1t[:, pr, :], AF.Square)))(tpar), "ac")
        P.w("vector", "ac", P.v("ac"))
        P.o("vector", lambda v: v.reduce_sum(stt[:, 1:2], sqf[:], axis=AX.X))
        P.o("vector", lambda v: v.tensor_scalar_mul(stt[:, 2:3], stt[:, 0:1], 1.0 / D))
        P.o("vector", lambda v: v.tensor_scalar_mul(stt[:, 3:4], stt[:, 1:2], 1.0 / D), "dv")
        P.w("scalar", "dv", P.v("dv"))
        P.o("scalar", lambda s: s.activation(stt[:, 4:5], stt[:, 2:3], AF.Square), "ac")
        P.w("vector", "ac", P.v("ac"))
        P.o("vector", lambda v: v.tensor_sub(stt[:, 5:6], stt[:, 3:4], stt[:, 4:5]), "dv")
        P.w("scalar", "dv", P.v("dv"))
        P.o("scalar", lambda s: s.activation(stt[:, 5:6], stt[:, 5:6], AF.Ln, bias=EPS))
        P.o("scalar", lambda s: s.activation(stt[:, 6:7], stt[:, 5:6], AF.Exp, scale=-0.5), "ac")
        P.w("vector", "ac", P.v("ac"))
        if tt >= 2:
            P.w("vector", "sc", sc_ln2h[tt - 2])
        P.o("vector", (lambda pr: (lambda v: v.tensor_scalar_sub(sqf[:], x1t[:, pr, :],
             stt[:, 2:3])))(tpar))
        P.o("vector", (lambda pr: (lambda v: v.tensor_scalar_mul(ln2t[:, pr, :], sqf[:],
             stt[:, 6:7])))(tpar), "dv")
        acl = P.v("dv")
        P.w("scalar", "dv", acl)
        P.o("scalar", (lambda t_, pr: (lambda s: s.dma_start(
             out=ln2_loc.ap()[t_ * 128:(t_ + 1) * 128, :], in_=ln2t[:, pr, :])))(tt, tpar), "sc", 16)
        sc_ln2h[tt] = P.v("sc")
        P.w("gpsimd", "sc", sc_ln2h[tt])
        P.o("gpsimd", (lambda t_: (lambda g: g.collective_compute("AllGather",
             ALU.bypass, replica_groups=[list(range(8))],
             ins=[ln2_loc.ap()[t_ * 128:(t_ + 1) * 128, :].opt()],
             outs=[ln2_all.ap()[t_ * 1024:(t_ + 1) * 1024, :].opt()])))(tt), "cc")
        for c8 in range(8):
            s8 = slice((c8 % 2) * 128, (c8 % 2) * 128 + 128)
            P.w("tensor", "dv", acl)
            P.w("tensor", "dv", P.v("dv"))
            P.o("tensor", (lambda cc, s8_, pr: (lambda t: t.transpose(pbT[:, s8_],
                 ln2t[:, pr, cc * 128:(cc + 1) * 128], ident[:])))(c8, s8, tpar), "pe")
            P.w("vector", "pe", P.v("pe"))
            P.o("vector", (lambda cc, t_, s8_: (lambda v: v.tensor_copy(
                 ln2T[:, cc, t_ * 128:(t_ + 1) * 128], pbT[:, s8_])))(c8, tt, s8), "dv")
        dv_ln2T[tt] = P.v("dv")
    SC_LN2 = P.v("sc")
    SP_X1 = P.v("sp")
    if "x1" in debug:
        dout("x1", [HTOK, D])
        P.w("sync", "sp", SP_X1)
        P.o("sync", lambda sy: sy.dma_start(out=dbgout["x1"][:], in_=x1_tm.ap()[:]), "sp", 16)
    if "ln2" in debug:
        dout("ln2", [HTOK, D], BF16)
        P.w("sync", "sc", SC_LN2)
        P.o("sync", lambda sy: sy.dma_start(out=dbgout["ln2"][:], in_=ln2_loc.ap()[:]), "sp", 16)
    if upto == "p3":
        return _finish()

    CC_LN2 = P.v("cc")

    for tt in range(16):
        P.w("tensor", "dv", dv_ln2T[tt])
        P.w("tensor", "gw", GW_WG)
        P.w("tensor", "dv", P.v("dv"))
        for c8 in range(8):
            P.o("tensor", (lambda cc, t_: (lambda t: t.matmul(pbC[:, 0:8],
                 ln2T[:, cc, t_ * 128:(t_ + 1) * 128], wgs[:, cc, 0:8],
                 start=(cc == 0), stop=(cc == 7))))(c8, tt), "pe" if c8 == 7 else None)
        P.w("vector", "pe", P.v("pe"))
        P.w("vector", "dv", DV_INIT)
        P.o("vector", (lambda t_: (lambda v: v.tensor_add(rt0[:, t_, :], pbC[:, 0:8], bgeps[:])))(tt), "dv")
    for tt in range(16):
        P.o("vector", (lambda t_: (lambda v: v.reduce_max(l1a[:, t_:t_ + 1], rt0[:, t_, :], axis=AX.X)))(tt))
    P.o("vector", lambda v: v.engine_nop(), "dv")
    P.w("scalar", "dv", P.v("dv"))
    P.o("scalar", lambda s: s.copy(l1b[:], l1a[:]), "ac")
    P.w("vector", "ac", P.v("ac"))
    for tt in range(16):
        P.o("vector", (lambda t_: (lambda v: v.tensor_single_scalar(eq1a[:, t_, :], rt0[:, t_, :],
             l1b[:, t_:t_ + 1], ALU.is_equal)))(tt))
    for tt in range(16):
        P.o("vector", (lambda t_: (lambda v: v.scalar_tensor_tensor(msk2[:, t_, :], eq1a[:, t_, :],
             -1e30, rt0[:, t_, :], ALU.mult, ALU.add)))(tt))
    for tt in range(16):
        P.o("vector", (lambda t_: (lambda v: v.reduce_max(l2a[:, t_:t_ + 1], msk2[:, t_, :], axis=AX.X)))(tt))
    P.o("vector", lambda v: v.engine_nop(), "dv")
    P.w("scalar", "dv", P.v("dv"))
    P.o("scalar", lambda s: s.copy(l2b[:], l2a[:]), "ac")
    P.w("vector", "ac", P.v("ac"))
    for tt in range(16):
        P.o("vector", (lambda t_: (lambda v: v.tensor_single_scalar(eq2a[:, t_, :], msk2[:, t_, :],
             l2b[:, t_:t_ + 1], ALU.is_equal)))(tt))
    P.o("vector", lambda v: v.tensor_sub(dal[:], l2a[:], l1a[:]), "dv")
    P.w("scalar", "dv", P.v("dv"))
    P.o("scalar", lambda s: s.activation(g1s[:], dal[:], AF.Sigmoid, scale=-1.0))
    P.o("scalar", lambda s: s.activation(g2s[:], dal[:], AF.Sigmoid), "ac")
    for tt in range(16):
        P.o("vector", (lambda t_: (lambda v: v.tensor_mul(rt0[:, t_, :], eq1a[:, t_, :], iot8[:])))(tt))
    for tt in range(16):
        P.o("vector", (lambda t_: (lambda v: v.reduce_sum(e1s[:, t_:t_ + 1], rt0[:, t_, :], axis=AX.X)))(tt))
    for tt in range(16):
        P.o("vector", (lambda t_: (lambda v: v.tensor_mul(rt0[:, t_, :], eq2a[:, t_, :], iot8[:])))(tt))
    for tt in range(16):
        P.o("vector", (lambda t_: (lambda v: v.reduce_sum(e2s[:, t_:t_ + 1], rt0[:, t_, :], axis=AX.X)))(tt))
    P.o("vector", lambda v: v.tensor_copy(ohb[:, 0, :, :], eq1a[:]))
    P.o("vector", lambda v: v.tensor_copy(ohb[:, 1, :, :], eq2a[:]), "dv")
    DV_OH = P.v("dv")
    pv_evac = {}
    slot_i = 0
    for k2 in range(2):
        for tt in range(16):
            pbank = pbD if slot_i % 2 == 0 else pbC
            sl8 = slice(((slot_i // 2) % 8) * 8, ((slot_i // 2) % 8) * 8 + 8)
            P.w("tensor", "dv", DV_OH)
            if slot_i >= 2:
                P.w("tensor", "dv", pv_evac[slot_i - 2])
            first = True
            for (kk, tt2) in [(0, t2) for t2 in range(16)] + [(1, t2) for t2 in range(16)]:
                if (kk, tt2) >= (k2, tt):
                    break
                P.o("tensor", (lambda kk_, t2_, s8_, st_, pb_: (lambda t: t.matmul(pb_[:, s8_],
                     onesb128[:], ohb[:, kk_, t2_, :], start=st_, stop=False)))(kk, tt2, sl8, first, pbank))
                first = False
            P.o("tensor", (lambda k_, t_, s8_, st_, pb_: (lambda t: t.matmul(pb_[:, s8_],
                 ltri[:], ohb[:, k_, t_, :], start=st_, stop=True)))(k2, tt, sl8, first, pbank), "pe")
            P.w("vector", "pe", P.v("pe"))
            P.o("vector", (lambda k_, t_, s8_, pb_: (lambda v: v.tensor_copy(posall[:, k_, t_, :], pb_[:, s8_])))(k2, tt, sl8, pbank), "dv")
            pv_evac[slot_i] = P.v("dv")
            slot_i += 1
    for k2 in range(2):
        for tt in range(16):
            P.w("tensor", "dv", DV_OH)
            P.o("tensor", (lambda k_, t_, st_: (lambda t: t.matmul(pbE[:, k_ * 8:k_ * 8 + 8],
                 onesb128[:], ohb[:, k_, t_, :], start=st_, stop=(t_ == 15))))(k2, tt, tt == 0),
                "pe" if tt == 15 else None)
    P.w("vector", "pe", P.v("pe"))
    P.o("vector", lambda v: v.tensor_copy(cntrow[0:1, :], pbE[0:1, 0:16]), "dv")
    DV_CNT = P.v("dv")
    P.o("vector", lambda v: v.tensor_mul(msk2[:], posall[:, 0, :, :], eq1a[:]))
    P.o("vector", lambda v: v.tensor_mul(rt0[:], posall[:, 1, :, :], eq2a[:]), "dv")
    for tt in range(16):
        P.o("vector", (lambda t_: (lambda v: v.reduce_sum(poss1[:, t_:t_ + 1], msk2[:, t_, :], axis=AX.X)))(tt))
    for tt in range(16):
        P.o("vector", (lambda t_: (lambda v: v.reduce_sum(poss2[:, t_:t_ + 1], rt0[:, t_, :], axis=AX.X)))(tt))
    P.o("vector", lambda v: v.engine_nop(), "dv")
    DV_ROUTE = P.v("dv")
    P.w("scalar", "dv", DV_CNT)
    P.o("scalar", lambda s: s.dma_start(out=cnt_loc.ap()[0:1, :], in_=cntrow[0:1, 0:8]), "sc", 16)
    P.o("scalar", lambda s: s.dma_start(out=cnt_loc.ap()[1:2, :], in_=cntrow[0:1, 8:16]), "sc", 16)
    P.w("gpsimd", "sc", P.v("sc"))
    P.o("gpsimd", lambda g: g.collective_compute("AllGather", ALU.bypass,
         replica_groups=[list(range(8))],
         ins=[cnt_loc.ap().opt()], outs=[cnt_all.ap().opt()]), "cc")
    P.w("sync", "cc", P.v("cc"))
    P.o("sync", lambda sy: sy.dma_start(out=cntag[:], in_=cnt_all.ap()[:]), "ld", 16)
    P.w("tensor", "ld", P.v("ld"))
    P.o("tensor", lambda t: t.matmul(pbE[0:1, 0:8], selmat[:, 0:1], cntag[:], start=True, stop=True))
    P.o("tensor", lambda t: t.matmul(pbE[0:1, 8:16], selmat[:, 1:2], cntag[:], start=True, stop=True), "pe")
    P.w("vector", "pe", P.v("pe"))
    P.o("vector", lambda v: v.tensor_copy(offrow[0:1, :], pbE[0:1, 0:16]), "dv")
    P.w("tensor", "dv", P.v("dv"))
    P.o("tensor", lambda t: t.matmul(pbE[:, 0:8], onesf128[0:1, :], offrow[0:1, 0:8], start=True, stop=True))
    P.o("tensor", lambda t: t.matmul(pbE[:, 8:16], onesf128[0:1, :], offrow[0:1, 8:16], start=True, stop=True), "pe")
    P.w("vector", "pe", P.v("pe"))
    P.o("vector", lambda v: v.tensor_copy(offb[:, 0, :], pbE[:, 0:8]))
    P.o("vector", lambda v: v.tensor_copy(offb[:, 1, :], pbE[:, 8:16]), "dv")
    DV_OFFB = P.v("dv")
    P.w("vector", "dv", DV_OFFB)
    for tt in range(16):
        P.o("vector", (lambda t_: (lambda v: v.tensor_mul(msk2[:, t_, :], eq1a[:, t_, :], offb[:, 0, :])))(tt))
    for tt in range(16):
        P.o("vector", (lambda t_: (lambda v: v.tensor_mul(rt0[:, t_, :], eq2a[:, t_, :], offb[:, 1, :])))(tt))
    for tt in range(16):
        P.o("vector", (lambda t_: (lambda v: v.reduce_sum(offsl1[:, t_:t_ + 1], msk2[:, t_, :], axis=AX.X)))(tt))
    for tt in range(16):
        P.o("vector", (lambda t_: (lambda v: v.reduce_sum(offsl2[:, t_:t_ + 1], rt0[:, t_, :], axis=AX.X)))(tt))
    def bounce():
        P.o("vector", lambda v: v.engine_nop(), "dv")
        P.w("scalar", "dv", P.v("dv"))
        P.o("scalar", lambda s: s.copy(cntrow[0:1, 0:1], cntrow[0:1, 0:1]), "ac")
        P.w("vector", "ac", P.v("ac"))
    P.o("vector", lambda v: v.tensor_add(poss1[:], poss1[:], offsl1[:]))
    P.o("vector", lambda v: v.tensor_add(poss2[:], poss2[:], offsl2[:]))
    bounce()
    P.o("vector", lambda v: v.tensor_single_scalar(keep1[:], poss1[:], float(CAP), ALU.is_lt))
    P.o("vector", lambda v: v.tensor_single_scalar(keep2[:], poss2[:], float(CAP), ALU.is_lt))
    P.o("vector", lambda v: v.tensor_scalar_mul(offsl1[:], e1s[:], float(CAPP)))
    P.o("vector", lambda v: v.tensor_scalar_mul(offsl2[:], e2s[:], float(CAPP)))
    bounce()
    P.o("vector", lambda v: v.tensor_scalar(l1a[:], keep1[:], -1e6, 1e6, ALU.mult, ALU.add))
    P.o("vector", lambda v: v.tensor_scalar(l2a[:], keep2[:], -1e6, 1e6, ALU.mult, ALU.add))
    P.o("vector", lambda v: v.tensor_add(offsl1[:], offsl1[:], poss1[:]))
    P.o("vector", lambda v: v.tensor_add(offsl2[:], offsl2[:], poss2[:]))
    bounce()
    P.o("vector", lambda v: v.tensor_add(offsl1[:], offsl1[:], l1a[:]))
    P.o("vector", lambda v: v.tensor_add(offsl2[:], offsl2[:], l2a[:]))
    P.o("vector", lambda v: v.tensor_mul(g1s[:], g1s[:], keep1[:]))
    P.o("vector", lambda v: v.tensor_mul(g2s[:], g2s[:], keep2[:]))
    bounce()
    P.o("vector", lambda v: v.tensor_copy(idx1s[:], offsl1[:]))
    P.o("vector", lambda v: v.tensor_copy(idx2s[:], offsl2[:]))
    P.o("vector", lambda v: v.tensor_copy(y1i[:], poss1[:]))
    P.o("vector", lambda v: v.tensor_copy(y2i[:], poss2[:]))
    P.o("vector", lambda v: v.tensor_copy(t1i[:], e1s[:]))
    P.o("vector", lambda v: v.tensor_copy(t2i[:], e2s[:]))
    bounce()
    P.o("vector", lambda v: v.tensor_single_scalar(m1i[:], y1i[:], 511, ALU.bitwise_and))
    P.o("vector", lambda v: v.tensor_single_scalar(m2i[:], y2i[:], 511, ALU.bitwise_and))
    P.o("vector", lambda v: v.tensor_single_scalar(idx1y[:], y1i[:], 8, ALU.mult))
    P.o("vector", lambda v: v.tensor_single_scalar(idx2y[:], y2i[:], 8, ALU.mult))
    P.o("vector", lambda v: v.tensor_single_scalar(t1i[:], t1i[:], 512, ALU.mult))
    P.o("vector", lambda v: v.tensor_single_scalar(t2i[:], t2i[:], 512, ALU.mult))
    bounce()
    P.o("vector", lambda v: v.tensor_single_scalar(m1i[:], m1i[:], 7, ALU.mult))
    P.o("vector", lambda v: v.tensor_single_scalar(m2i[:], m2i[:], 7, ALU.mult))
    P.o("vector", lambda v: v.tensor_add(idx1y[:], idx1y[:], t1i[:]))
    P.o("vector", lambda v: v.tensor_add(idx2y[:], idx2y[:], t2i[:]))
    P.o("vector", lambda v: v.tensor_copy(t1i[:], l1a[:]))
    P.o("vector", lambda v: v.tensor_copy(t2i[:], l2a[:]))
    bounce()
    P.o("vector", lambda v: v.tensor_sub(idx1y[:], idx1y[:], m1i[:]))
    P.o("vector", lambda v: v.tensor_sub(idx2y[:], idx2y[:], m2i[:]))
    bounce()
    P.o("vector", lambda v: v.tensor_add(idx1y[:], idx1y[:], t1i[:]))
    P.o("vector", lambda v: v.tensor_add(idx2y[:], idx2y[:], t2i[:]), "dv")
    xbounce()
    DV_IDX = P.v("dv")
    if upto == "route":
        return _finish()
    w1r = w1_d.rearrange("(c p) f -> p c f", p=128)
    w2r = w2_d.rearrange("(c p) m -> p c m", p=128)
    gw_w1 = []
    P.w("gpsimd", "dv", dv_ln2T[15])
    P.w("gpsimd", "pe", P.v("pe"))
    P.w("gpsimd", "sp", P.v("sp"))
    P.w("gpsimd", "sc", P.v("sc"))
    for q4 in range(4):
        P.o("gpsimd", (lambda q_: (lambda g: g.dma_start(out=w1s[:, :, q_ * 1024:(q_ + 1) * 1024],
             in_=w1r[:, :, q_ * 1024:(q_ + 1) * 1024])))(q4), "gw", 16)
        gw_w1.append(P.v("gw"))
    gw_w2 = []
    for q4 in range(4):
        P.o("gpsimd", (lambda q_: (lambda g: g.dma_start(out=w2s[:, q_ * 8:(q_ + 1) * 8, :],
             in_=w2r[:, q_ * 8:(q_ + 1) * 8, :])))(q4), "gw", 16)
        gw_w2.append(P.v("gw"))
    slot_loc_v = slot_loc.ap().rearrange("(p c) u -> p (c u)", p=128)
    P.w("gpsimd", "dv", DV_INIT)
    P.o("gpsimd", lambda g: g.dma_start(out=slot_loc_v, in_=zero1[:]), "g2", 16)
    G2_Z = P.v("g2")
    P.w("gpsimd", "dv", DV_IDX)
    P.o("gpsimd", lambda g: g.tensor_copy(idx1g[:], idx1s[:]))
    P.o("gpsimd", lambda g: g.tensor_copy(idx2g[:], idx2s[:]), "gp")
    for tt in range(16):
        for k2 in range(2):
            ix = idx1g if k2 == 0 else idx2g
            P.w("gpsimd", "g2", G2_Z)
            P.o("gpsimd", (lambda t_, i_: (lambda g: g.indirect_dma_start(
                 out=slot_loc.ap(),
                 out_offset=IndirectOffsetOnAxis(i_[:, t_:t_ + 1], 0),
                 in_=toks[:, t_:t_ + 1], in_offset=None,
                 bounds_check=REGS["slot"], oob_is_err=False)))(tt, ix), "g2", 16)
    if "sloc" in debug:
        dout("sloc", [E * CAPP, 1])
        P.w("sync", "g2", P.v("g2"))
        P.o("sync", lambda sy: sy.dma_start(out=dbgout["sloc"][:], in_=slot_loc.ap()[:]), "sp", 16)
    P.w("gpsimd", "g2", P.v("g2"))
    P.o("gpsimd", lambda g: g.collective_compute("AllGather", ALU.bypass,
         replica_groups=[list(range(8))],
         ins=[slot_loc.ap().opt()], outs=[slot_ag.ap().opt()]), "cc")
    CC_SLOT = P.v("cc")
    P.o("gpsimd", lambda g: g.tensor_copy(idx1g[:], idx1y[:]))
    P.o("gpsimd", lambda g: g.tensor_copy(idx2g[:], idx2y[:]), "gp")
    P.w("gpsimd", "cc", CC_SLOT)
    P.o("gpsimd", lambda g: g.collective_compute("AllGather", ALU.bypass,
         replica_groups=[list(range(8))],
         ins=[cnt_loc.ap().opt()], outs=[bar_o.ap().opt()]), "cc")
    CC_B2 = P.v("cc")
    if "sag" in debug:
        dout("sag1", [NC * E * CAPP, 1])
        P.w("sync", "cc", CC_B2)
        P.w("sync", "gw", gw_w2[3])
        P.o("sync", lambda sy: sy.dma_start(out=dbgout["sag1"][:], in_=slot_ag.ap()[:]), "sp", 16)
    P.w("gpsimd", "ld", LD_CONST)
    for j in range(8):
        P.o("gpsimd", (lambda j_: (lambda g: g.iota(agix8f[:, j_:j_ + 1],
             pattern=[[0, 1]], base=j_ * E * CAPP, channel_multiplier=128,
             allow_small_or_imprecise_dtypes=True)))(j))
    P.o("gpsimd", lambda g: g.tensor_scalar_add(agix8f[:], agix8f[:], msl0[:, 0:1]))
    P.o("gpsimd", lambda g: g.tensor_copy(agix8[:], agix8f[:]), "gp")
    for j in range(8):
        P.w("gpsimd", "dv", P.v("dv"))
        P.w("gpsimd", "cc", CC_B2)
        P.o("gpsimd", (lambda j_, pr: (lambda g: g.indirect_dma_start(
             out=st39[:, pr, :], out_offset=None, in_=slot_ag.ap(),
             in_offset=IndirectOffsetOnAxis(agix8[0:39, j_:j_ + 1], 0),
             bounds_check=REGS["sag"], oob_is_err=False)))(j, j % 2), "g2", 16)
        P.w("vector", "g2", P.v("g2"))
        P.w("vector", "dv", DV_INIT)
        P.o("vector", (lambda pr: (lambda v: v.tensor_add(sum39[:], sum39[:], st39[:, pr, :])))(j % 2), "dv")
    P.w("tensor", "dv", P.v("dv"))
    P.o("tensor", lambda t: t.transpose(pbE[:, 0:39], sum39[:], identf[0:39, 0:39]), "pe")
    P.w("vector", "pe", P.v("pe"))
    P.o("vector", lambda v: v.tensor_copy(slotm[:], pbE[:, 0:39]), "dv")
    xbounce()
    P.o("vector", lambda v: v.tensor_copy(i39a[:], slotm[:]), "dv")
    xbounce()
    P.o("vector", lambda v: v.tensor_single_scalar(i39b[:], i39a[:], 11, ALU.arith_shift_right))
    P.o("vector", lambda v: v.tensor_single_scalar(i39c[:], i39a[:], 127, ALU.bitwise_and))
    P.o("vector", lambda v: v.tensor_single_scalar(slotY[:], i39a[:], 8, ALU.mult))
    bounce()
    P.o("vector", lambda v: v.tensor_single_scalar(i39b[:], i39b[:], 16256, ALU.mult))
    P.o("vector", lambda v: v.tensor_single_scalar(i39c[:], i39c[:], 7, ALU.mult))
    bounce()
    P.o("vector", lambda v: v.tensor_sub(slotY[:], slotY[:], i39b[:]))
    bounce()
    P.o("vector", lambda v: v.tensor_sub(slotY[:], slotY[:], i39c[:]), "dv")
    xbounce()
    DV_SLOT = P.v("dv")
    P.w("gpsimd", "dv", DV_SLOT)
    P.o("gpsimd", lambda g: g.tensor_copy(slotmi[:], slotY[:]), "gp")
    if "route2" in debug:
        dout("oh1d", [128, 128]); dout("e1d", [128, 16])
        dout("posd", [128, 256]); dout("cntd", [1, 16])
        dout("offbd", [128, 16]); dout("cntagd", [16, 8])
        P.w("sync", "dv", DV_SLOT)
        P.o("sync", lambda sy: sy.dma_start(out=dbgout["oh1d"][:], in_=eq1a[:].rearrange("p a b -> p (a b)")), "sp", 16)
        P.o("sync", lambda sy: sy.dma_start(out=dbgout["e1d"][:], in_=e1s[:]), "sp", 16)
        P.o("sync", lambda sy: sy.dma_start(out=dbgout["posd"][:], in_=posall[:].rearrange("p a b c -> p (a b c)")), "sp", 16)
        P.o("sync", lambda sy: sy.dma_start(out=dbgout["cntd"][:], in_=cntrow[:]), "sp", 16)
        P.o("sync", lambda sy: sy.dma_start(out=dbgout["offbd"][:], in_=offb[:].rearrange("p a b -> p (a b)")), "sp", 16)
        P.o("sync", lambda sy: sy.dma_start(out=dbgout["cntagd"][:], in_=cntag[:]), "sp", 16)
    if "route" in debug:
        dout("idx1", [128, 16], I32); dout("idx2", [128, 16], I32); dout("g1", [128, 16])
        dout("g2v", [128, 16]); dout("slotm", [128, 39]); dout("poss1", [128, 16])
        P.w("sync", "dv", DV_SLOT)
        P.o("sync", lambda sy: sy.dma_start(out=dbgout["idx1"][:], in_=idx1s[:]), "sp", 16)
        P.o("sync", lambda sy: sy.dma_start(out=dbgout["idx2"][:], in_=idx2s[:]), "sp", 16)
        P.o("sync", lambda sy: sy.dma_start(out=dbgout["g1"][:], in_=g1s[:]), "sp", 16)
        P.o("sync", lambda sy: sy.dma_start(out=dbgout["g2v"][:], in_=g2s[:]), "sp", 16)
        P.o("sync", lambda sy: sy.dma_start(out=dbgout["slotm"][:], in_=slotm[:]), "sp", 16)
        P.o("sync", lambda sy: sy.dma_start(out=dbgout["poss1"][:], in_=poss1[:]), "sp", 16)
        dout("slotmi", [128, 39], I32)
        P.o("sync", lambda sy: sy.dma_start(out=dbgout["slotmi"][:], in_=slotmi[:]), "sp", 16)
    if upto == "slot":
        return _finish()

    g2_gath, pe_stage, ach, dv_ye, sc_ye = {}, {}, {}, {}, {}
    HSL = [pbA[:, 0:512], pbA[:, 512:1024], pbB[:, 0:512], pbB[:, 512:1024]]
    if "ln2a" in debug:
        dout("ln2a", [NC * HTOK, D], BF16)
        P.w("sync", "cc", CC_B2)
        P.o("sync", lambda sy: sy.dma_start(out=dbgout["ln2a"][:], in_=ln2_all.ap()[:]), "sp", 16)
    def ye_chunk_trigger(k):
        P.w("gpsimd", "sc", sc_ye[min(4 * k + 3, 38)])
        P.o("gpsimd", (lambda k_: (lambda g: g.collective_compute("AllGather",
             ALU.bypass, replica_groups=[list(range(8))],
             ins=[ye_loc.ap()[k_ * 512:(k_ + 1) * 512, :].opt()],
             outs=[ye_all.ap()[k_ * 4096:(k_ + 1) * 4096, :].opt()])))(k), "cc")
    for rb in range(10):
        sts = RB_ST[rb]
        if rb >= 2:
            ye_chunk_trigger(rb - 2)
        for st in sts:
            j = st - rb * 4
            P.w("gpsimd", "dv", DV_SLOT)
            P.w("gpsimd", "cc", CC_B2)
            if st >= 2:
                P.w("gpsimd", "pe", pe_stage[st - 2])
            P.o("gpsimd", (lambda s_, pr: (lambda g: g.indirect_dma_start(
                 out=stage[:, pr, :], out_offset=None,
                 in_=ln2_all.ap(),
                 in_offset=IndirectOffsetOnAxis(slotmi[:, s_:s_ + 1], 0),
                 bounds_check=REGS["ln2"], oob_is_err=False)))(st, st % 2), "g2", 16)
            g2_gath[st] = P.v("g2")
            for c8 in range(8):
                s8 = slice(((st * 8 + c8) % 2) * 128, ((st * 8 + c8) % 2) * 128 + 128)
                P.w("tensor", "g2", g2_gath[st])
                P.w("tensor", "dv", P.v("dv"))
                P.o("tensor", (lambda pr, cc, s8_: (lambda t: t.transpose(pbT[:, s8_],
                     stage[:, pr, cc * 128:(cc + 1) * 128], ident[:])))(st % 2, c8, s8), "pe")
                if c8 == 7:
                    pe_stage[st] = P.v("pe")
                P.w("vector", "pe", P.v("pe"))
                P.o("vector", (lambda cc, j_, s8_: (lambda v: v.tensor_copy(
                     bufT[:, cc, j_ * 128:(j_ + 1) * 128], pbT[:, s8_])))(c8, j, s8), "dv")
        dv_bufT = P.v("dv")
        xbounce()
        AC_BUF = P.v("ac")
        for fc in range(32):
            P.w("tensor", "dv", dv_bufT)
            P.w("tensor", "ac", AC_BUF)
            P.w("tensor", "gw", gw_w1[fc // 8])
            if fc >= 4:
                P.w("tensor", "ac", ach[(rb, fc - 4)])
            for c8 in range(8):
                P.o("tensor", (lambda cc, f_, sl_: (lambda t: t.matmul(sl_,
                     w1s[:, cc, f_ * 128:(f_ + 1) * 128], bufT[:, cc, :],
                     start=(cc == 0), stop=(cc == 7))))(c8, fc, HSL[fc % 4]))
            P.m("tensor", "pe")
            P.w("scalar", "pe", P.v("pe"))
            P.o("scalar", (lambda f_, sl_: (lambda s: s.activation(hsb[:, f_, :], sl_,
                 AF.Gelu_apprx_tanh, bias=b1c[:, f_:f_ + 1])))(fc, HSL[fc % 4]), "ac")
            ach[(rb, fc)] = P.v("ac")
        P.w("scalar", "ac", ach[(rb, 31)])
        P.o("scalar", lambda s: s.copy(cntrow[0:1, 0:1], cntrow[0:1, 0:1]), "ac")
        AC_H = P.v("ac")
        for j in range(len(sts)):
            rt = rb * 4 + j
            ypar = rt % 2
            P.w("tensor", "ac", AC_H)
            P.w("tensor", "gw", gw_w2[3])
            if rt >= 1:
                P.w("tensor", "dv", dv_ye[rt - 1])
            for nb2 in range(2):
                pv = pbC if nb2 == 0 else pbD
                for fc in range(32):
                    P.o("tensor", (lambda f_, j_, n_, pv_: (lambda t: t.matmul(pv_[:],
                         hsb[:, f_, j_ * 128:(j_ + 1) * 128], w2s[:, f_, n_ * 512:(n_ + 1) * 512],
                         start=(f_ == 0), stop=(f_ == 31))))(fc, j, nb2, pv))
            P.m("tensor", "pe")
            P.w("vector", "pe", P.v("pe"))
            if rt >= 2:
                P.w("vector", "sc", sc_ye[rt - 2])
            P.o("vector", (lambda pr: (lambda v: v.tensor_add(yet[:, pr, 0:512], pbC[:], b2bc[:, 0:512])))(ypar))
            P.o("vector", (lambda pr: (lambda v: v.tensor_add(yet[:, pr, 512:1024], pbD[:], b2bc[:, 512:1024])))(ypar), "dv")
            dv_ye[rt] = P.v("dv")
            P.w("scalar", "dv", dv_ye[rt])
            P.o("scalar", (lambda r_, pr: (lambda s: s.dma_start(
                 out=ye_loc.ap()[r_ * 128:(r_ + 1) * 128, :], in_=yet[:, pr, :])))(rt, ypar), "sc", 16)
            sc_ye[rt] = P.v("sc")
    SC_YE = P.v("sc")
    if "ye" in debug:
        dout("ye", [CAPP, D], BF16)
        P.w("sync", "sc", SC_YE)
        P.o("sync", lambda sy: sy.dma_start(out=dbgout["ye"][:], in_=ye_loc.ap()[0:CAPP, :]), "sp", 16)
    if upto == "p4":
        return _finish()
    ye_chunk_trigger(8)
    ye_chunk_trigger(9)
    CC_YE = P.v("cc")
    P.w("gpsimd", "cc", CC_YE)
    P.o("gpsimd", lambda g: g.collective_compute("AllGather", ALU.bypass,
         replica_groups=[list(range(8))],
         ins=[cnt_loc.ap().opt()], outs=[bar_o.ap().opt()]), "cc")
    CC_B3 = P.v("cc")

    sp_outh = {}
    P.w("vector", "cc", CC_B3)
    P.o("vector", lambda v: v.memset(gb1[:], 0.0))
    P.o("vector", lambda v: v.memset(gb2[:], 0.0), "dv")
    xbounce()
    GB_Z = P.v("dv")
    dv_comb = {}
    for tt in range(16):
        tpar = tt % 2
        if tt >= 2:
            P.w("gpsimd", "dv", dv_comb[tt - 2])
        P.w("gpsimd", "dv", GB_Z)
        P.w("gpsimd", "cc", CC_B3)
        P.o("gpsimd", (lambda t_, pr: (lambda g: g.indirect_dma_start(
             out=gb1[:, pr, :], out_offset=None, in_=ye_all.ap(),
             in_offset=IndirectOffsetOnAxis(idx1g[:, t_:t_ + 1], 0),
             bounds_check=REGS["ye"], oob_is_err=False)))(tt, tpar), "g2", 16)
        P.o("gpsimd", (lambda t_, pr: (lambda g: g.indirect_dma_start(
             out=gb2[:, pr, :], out_offset=None, in_=ye_all.ap(),
             in_offset=IndirectOffsetOnAxis(idx2g[:, t_:t_ + 1], 0),
             bounds_check=REGS["ye"], oob_is_err=False)))(tt, tpar), "g2", 16)
        g2v = P.v("g2")
        if tt < 2:
            P.w("sync", "cc", CC_B3)
        P.o("sync", (lambda t_, pr: (lambda sy: sy.dma_start(out=x1r[:, pr, :],
             in_=x1_tm.ap()[t_ * 128:(t_ + 1) * 128, :])))(tt, tpar), "ld", 16)
        P.w("vector", "g2", g2v)
        P.w("vector", "ld", P.v("ld"))
        if tt >= 2:
            P.w("vector", "sp", sp_outh[tt - 2])
        P.o("vector", (lambda t_, pr: (lambda v: v.tensor_scalar_mul(zsum[:], gb1[:, pr, :],
             g1s[:, t_:t_ + 1])))(tt, tpar))
        P.o("vector", (lambda t_, pr: (lambda v: v.scalar_tensor_tensor(outt[:, pr, :],
             gb2[:, pr, :], g2s[:, t_:t_ + 1], zsum[:], ALU.mult, ALU.add)))(tt, tpar))
        P.o("vector", (lambda pr: (lambda v: v.tensor_add(outt[:, pr, :], outt[:, pr, :],
             x1r[:, pr, :])))(tpar), "dv")
        dv_comb[tt] = P.v("dv")
        P.w("sync", "dv", P.v("dv"))
        P.o("sync", (lambda t_, pr: (lambda sy: sy.dma_start(
             out=y_out[t_ * 128:(t_ + 1) * 128, :], in_=outt[:, pr, :])))(tt, tpar), "sp", 16)
        sp_outh[tt] = P.v("sp")
    if "sag" in debug:
        dout("sag2", [NC * E * CAPP, 1])
        P.w("sync", "sp", P.v("sp"))
        P.w("sync", "dv", P.v("dv"))
        P.o("sync", lambda sy: sy.dma_start(out=dbgout["sag2"][:], in_=slot_ag.ap()[:]), "sp", 16)
    P.w("sync", "sp", P.v("sp"))
    P.w("sync", "sc", P.v("sc"))
    P.w("sync", "g2", P.v("g2"))
    P.w("sync", "cc", P.v("cc"))
    P.w("sync", "gw", P.v("gw"))

    return _finish()


_NC_CACHE = {}


def kernel(**inputs):
    if "main" not in _NC_CACHE:
        _NC_CACHE["main"] = build()[0]
    nc = _NC_CACHE["main"]
    consts = _host_constants()
    perm = _perm_cols()
    in_maps = [_host_inputs_for_core(c, inputs, consts, perm) for c in range(NC)]
    res = run_bass_kernel_spmd(nc, in_maps, core_ids=list(range(NC)))
    out = np.empty((4, S, D), np.float32)
    for c in range(NC):
        b, hg = c // 2, c % 2
        out[b, hg * HTOK:(hg + 1) * HTOK, :] = res.results[c]["y_out"]
    return out
